# version 04
# baseline (speedup 1.0000x reference)
import contextlib
import numpy as np
import ml_dtypes
import concourse.bass as bass
import concourse.mybir as mybir
from concourse.bass_utils import run_bass_kernel_spmd

F32 = mybir.dt.float32
BF16 = mybir.dt.bfloat16
AF = mybir.ActivationFunctionType
ALU = mybir.AluOpType
AX = mybir.AxisListType

B, S, D = 2, 8192, 1024
H, HD = 16, 64
NI = 4096
NE = 8
QUAD = 512
NQ = 4
TOK = NQ * QUAD
SL = 256
NSL = S // SL
EPS = 1e-5
TEXT_KT = [16, 32, 48, 64]
OWN_PSLOT = [3, 7, 11, 15]


class Sem:
    __slots__ = ("h", "n")

    def __init__(self, h):
        self.h = h
        self.n = 0


class Plan:
    ENGS = ("tensor", "vector", "scalar", "gpsimd", "sync")

    def __init__(self):
        self.ops = {e: [] for e in self.ENGS}

    def emit(self, eng, fn=None, waits=(), inc=None, inc_k=1):
        w = [(s.h, v) for (s, v) in waits if v > 0]
        tgt = None
        if inc is not None:
            inc.n += inc_k
            tgt = (inc.h, inc_k)

        def run(e, w=w, fn=fn, tgt=tgt):
            for h, v in w:
                e.wait_ge(h, v)
            if fn is not None:
                ins = fn(e)
                if tgt is not None:
                    ins.then_inc(tgt[0], tgt[1])
            elif tgt is not None:
                e.sem_inc(tgt[0], tgt[1])

        self.ops[eng].append(run)

    @staticmethod
    def wv(*sems):
        return [(s, s.n) for s in sems]

    @staticmethod
    def wb(sem, back):
        return [(sem, max(0, sem.n - back))]


def build_program(debug=False):
    nc = bass.Bass("TRN2", target_bir_lowering=False)
    P = Plan()
    wv, wb = P.wv, P.wb

    def din(name, shape, dt=F32):
        return nc.dram_tensor(name, list(shape), dt, kind="ExternalInput").ap()

    x_fm = din("x_fm", (D, S))
    x_tm = din("x_tm", (TOK, D))
    c2t_d = din("c2t", (128, S), BF16)
    s2t_d = din("s2t", (128, S), BF16)
    pswap_d = din("pswap", (128, 128), BF16)
    identb_d = din("identb", (128, 128), BF16)
    identf_d = din("identf", (128, 128), F32)
    ones64_d = din("ones64", (128, 64), BF16)
    maskA_d = din("maskA", (128, 64), BF16)
    maskB_d = din("maskB", (128, 64), BF16)
    amask_d = din("amask", (64, 128, 1024), BF16)
    wqt_d = din("wqt", (D, D))
    wkv_d = din("wkv", (D, 128))
    wot_d = din("wot", (D, D))
    gw_d = din("gate_w", (D, NE))
    gb_d = din("gate_b", (1, NE))
    w1s_d = din("sh_w1", (D, NI))
    w2s_d = din("sh_w2", (NI, D))
    ew1_d = din("ew1", (NE, D, NI))
    ew2_d = din("ew2", (NE, NI, D))
    b1l_d = din("b1l", (128, 32))
    eb1l_d = din("eb1l", (128, NE * 32))
    eb2p_d = din("eb2p", (64, D))
    onesrow_d = din("onesrow", (1, 2048), BF16)

    y = nc.dram_tensor("y", [TOK, D], F32, kind="ExternalOutput").ap()
    dbg = {}
    if debug:
        for nm, shp in (("xn", [128, 8 * TOK]), ("krope", [128, S]), ("vdup", [128, S]),
                        ("qall", [128, 8 * TOK]), ("attn", [128, 8 * TOK]),
                        ("wroute", [8, TOK]), ("acc0", [128, 8 * 1024]),
                        ("router", [128, 16 * 48])):
            dbg[nm] = nc.dram_tensor("dbg_" + nm, shp, F32, kind="ExternalOutput").ap()

    stats_d = nc.dram_tensor("stats_d", [3, S], F32, kind="Internal").ap()
    sums_d = nc.dram_tensor("sums_d", [64, QUAD], F32, kind="Internal").ap()
    cb_d = nc.dram_tensor("cb_d", [1, 1], F32, kind="Internal").ap()
    wroute_dd = nc.dram_tensor("wroute_dd", [NE, TOK], F32, kind="Internal").ap()
    attno_d = nc.dram_tensor("attno_d", [8, 128, TOK], F32, kind="Internal").ap()

    ctx = contextlib.ExitStack()
    sems = {}

    def sem(name):
        if name not in sems:
            sems[name] = Sem(ctx.enter_context(nc.semaphore(name)))
        return sems[name]

    ST = lambda *a, **k: ctx.enter_context(nc.sbuf_tensor(*a, **k))

    xn_all = ST("xn_all", [128, 8 * TOK], BF16)
    cbias = ST("cbias", [128, 1], F32)
    wroute_bf = ST("wroute_bf", [64, TOK], BF16)
    identb_s = ST("identb_s", [128, 128], BF16)
    identf_s = ST("identf_s", [128, 128], F32)
    pswap_s = ST("pswap_s", [128, 128], BF16)
    ones64_s = ST("ones64_s", [128, 64], BF16)
    maskA_s = ST("maskA_s", [128, 64], BF16)
    maskB_s = ST("maskB_s", [128, 64], BF16)
    wkv_bf = ST("wkv_bf", [128, 8 * 128], BF16)
    gw_bf = ST("gw_bf", [128, 64], BF16)
    gbc_s = ST("gbc_s", [128, NE], F32)
    b1l_s = ST("b1l_s", [128, 32], F32)
    eb1l_s = ST("eb1l_s", [128, NE * 32], F32)
    eb2p_bf = ST("eb2p_bf", [64, D], BF16)
    call_s = ST("call_s", [1, 64], F32)

    attn_ctx = contextlib.ExitStack()
    AT = lambda *a, **k: attn_ctx.enter_context(nc.sbuf_tensor(*a, **k))
    krope = AT("krope", [128, S], BF16, side="right")
    vdup = AT("vdup", [128, S], BF16, side="right")
    qall = AT("qall", [128, 8 * TOK], BF16, side="right")
    dstg = ST("dstg", [128, 2048], F32) if debug else None

    s_const = sem("s_const")
    s_cast = sem("s_cast")
    for o, i in ((identb_s, identb_d), (identf_s, identf_d), (pswap_s, pswap_d),
                 (ones64_s, ones64_d), (maskA_s, maskA_d), (maskB_s, maskB_d),
                 (b1l_s, b1l_d), (eb1l_s, eb1l_d)):
        P.emit("sync", lambda e, o=o, i=i: e.dma_start(out=o[:], in_=i), inc=s_const, inc_k=16)
    P.emit("sync", lambda e: e.dma_start(out=gbc_s[:], in_=gb_d.broadcast_to([128, NE])),
           inc=s_const, inc_k=16)
    P.emit("gpsimd", lambda e: e.dma_start(
        out=wkv_bf[:].rearrange("p (c i) -> p c i", c=8),
        in_=wkv_d.rearrange("(c p) i -> p c i", p=128)), inc=s_cast, inc_k=16)
    P.emit("gpsimd", lambda e: e.dma_start(
        out=gw_bf[:].rearrange("p (c e) -> p c e", c=8),
        in_=gw_d.rearrange("(c p) e -> p c e", p=128)), inc=s_cast, inc_k=16)
    P.emit("gpsimd", lambda e: e.dma_start(out=eb2p_bf[:], in_=eb2p_d), inc=s_cast, inc_k=16)

    def dump(name, src, width, waitsems):
        if not debug:
            return
        s_dbg = sem("s_dbg")
        v_dbg = sem("v_dbg")
        for off in range(0, width, 2048):
            P.emit("vector", lambda e, src=src, off=off: e.tensor_copy(
                dstg[:], src[:, off:off + 2048]),
                waits=wv(*waitsems) + wv(s_dbg), inc=v_dbg)
            P.emit("sync", lambda e, name=name, off=off: e.dma_start(
                out=dbg[name][:, off:off + 2048], in_=dstg[:]),
                waits=wv(v_dbg), inc=s_dbg, inc_k=16)

    s_xf = [sem("s_xf0"), sem("s_xf1")]
    s_xb = [sem("s_xb0"), sem("s_xb1")]
    s_tab = [sem("s_tab0"), sem("s_tab1")]
    s_statw = sem("s_statw"); s_bc = sem("s_bc")
    p_st = sem("p_st"); p_kv = sem("p_kv"); p_ksw = sem("p_ksw"); p_vt = sem("p_vt")
    a_xsq = sem("a_xsq"); v_var = sem("v_var"); a_sq = sem("a_sq"); v_stats = sem("v_stats")
    v_xn = sem("v_xn"); a_k = sem("a_k"); v_v = sem("v_v"); v_k = sem("v_k"); v_vd = sem("v_vd")
    v_mrow = sem("v_mrow")

    with contextlib.ExitStack() as p1:
        T1 = lambda *a, **k: p1.enter_context(nc.sbuf_tensor(*a, **k))
        PT1 = lambda *a, **k: p1.enter_context(nc.psum_tensor(*a, **k))
        xf = [T1(f"xf{i}", [128, 8 * SL], F32) for i in range(2)]
        xb = [T1(f"xb{i}", [128, 8 * SL], BF16) for i in range(2)]
        xsq = T1("xsq", [128, 8 * SL], BF16)
        c2b = [T1(f"c2b{i}", [128, SL], BF16) for i in range(2)]
        s2b = [T1(f"s2b{i}", [128, SL], BF16) for i in range(2)]
        xn_tmp = T1("xn_tmp", [128, 8 * SL], BF16)
        m_row = T1("m_row", [1, SL], F32)
        msq_row = T1("msq_row", [1, SL], F32)
        var_row = T1("var_row", [1, SL], F32)
        sq_row = T1("sq_row", [1, SL], F32)
        r_row = T1("r_row", [1, SL], F32)
        nmr_row = T1("nmr_row", [1, SL], F32)
        mbc = T1("mbc", [128, SL], F32)
        rbc = T1("rbc", [128, SL], F32)
        kbf = T1("kbf", [64, SL], BF16)
        kswb = T1("kswb", [64, SL], BF16)
        kt1 = T1("kt1", [64, SL], BF16)
        kt2 = T1("kt2", [64, SL], BF16)
        vbf = T1("vbf", [64, SL], BF16)
        s1p = [PT1(f"s1p{i}", [64, SL], F32) for i in range(2)]
        s2p = [PT1(f"s2p{i}", [64, SL], F32) for i in range(2)]
        kvp = [PT1(f"kvp{i}", [128, SL], F32) for i in range(2)]
        kswp = PT1("kswp", [64, SL], F32)
        vtp = PT1("vtp", [128, 128], BF16)

        for s in range(NSL):
            par = s % 2
            ts = s * SL
            pslot = ts // QUAD
            own = pslot in OWN_PSLOT
            dcol = (OWN_PSLOT.index(pslot) * QUAD + ts % QUAD) if own else 0
            xd = xn_all if own else xn_tmp

            def xsl(c, xd=xd, dcol=dcol):
                if xd is xn_all:
                    return xd[:, c * TOK + dcol:c * TOK + dcol + SL]
                return xd[:, c * SL:(c + 1) * SL]

            P.emit("sync", lambda e, p=par, ts=ts: e.dma_start(
                out=xf[p][:].rearrange("p (c t) -> p c t", c=8),
                in_=x_fm[:, ts:ts + SL].rearrange("(c p) t -> p c t", p=128)),
                waits=wb(v_xn, 1) if s >= 2 else [], inc=s_xf[par], inc_k=16)
            P.emit("gpsimd", lambda e, p=par, ts=ts: e.dma_start(
                out=xb[p][:].rearrange("p (c t) -> p c t", c=8),
                in_=x_fm[:, ts:ts + SL].rearrange("(c p) t -> p c t", p=128)),
                waits=wb(p_st, 1) if s >= 2 else [], inc=s_xb[par], inc_k=16)
            P.emit("sync", lambda e, p=par, ts=ts: e.dma_start(out=c2b[p][:], in_=c2t_d[:, ts:ts + SL]),
                   waits=wb(v_k, 1) if s >= 2 else [], inc=s_tab[par], inc_k=16)
            P.emit("sync", lambda e, p=par, ts=ts: e.dma_start(out=s2b[p][:], in_=s2t_d[:, ts:ts + SL]),
                   inc=s_tab[par], inc_k=16)

            P.emit("scalar", lambda e, p=par: e.activation(xsq[:], xb[p][:], AF.Square),
                   waits=wv(s_xb[par]) + wv(p_st), inc=a_xsq)
            for c in range(8):
                P.emit("tensor", lambda e, p=par, c=c: e.matmul(
                    s1p[p][:], ones64_s[:], xb[p][:, c * SL:(c + 1) * SL],
                    start=(c == 0), stop=(c == 7)),
                    waits=(wv(s_xb[par], s_const) + wb(v_mrow, 1) if c == 0 else []))
            for c in range(8):
                P.emit("tensor", lambda e, p=par, c=c: e.matmul(
                    s2p[p][:], ones64_s[:], xsq[:, c * SL:(c + 1) * SL],
                    start=(c == 0), stop=(c == 7)),
                    waits=(wv(a_xsq) + wb(v_var, 1) if c == 0 else []),
                    inc=(p_st if c == 7 else None))
            P.emit("vector", lambda e, p=par: e.tensor_scalar_mul(m_row[:], s1p[p][0:1, :], 1.0 / D),
                   waits=wv(p_st, s_statw), inc=v_mrow)
            P.emit("vector", lambda e: e.tensor_mul(msq_row[:], m_row[:], m_row[:]))
            P.emit("vector", lambda e, p=par: e.scalar_tensor_tensor(
                var_row[:], s2p[p][0:1, :], 1.0 / D, msq_row[:], op0=ALU.mult, op1=ALU.subtract),
                )
            P.emit("vector", lambda e: e.tensor_scalar_add(var_row[:], var_row[:], EPS),
                inc=v_var)
            P.emit("scalar", lambda e: e.activation(sq_row[:], var_row[:], AF.Sqrt),
                   waits=wv(v_var), inc=a_sq)
            P.emit("vector", lambda e: e.reciprocal(r_row[:], sq_row[:]), waits=wv(a_sq))
            P.emit("vector", lambda e: e.scalar_tensor_tensor(
                nmr_row[:], m_row[:], -1.0, r_row[:], op0=ALU.mult, op1=ALU.mult), inc=v_stats)
            P.emit("sync", lambda e, ts=ts: e.dma_start(out=stats_d[0:1, ts:ts + SL], in_=m_row[:]),
                   waits=wv(v_stats), inc=s_statw, inc_k=16)
            P.emit("sync", lambda e, ts=ts: e.dma_start(out=stats_d[1:2, ts:ts + SL], in_=r_row[:]),
                   inc=s_statw, inc_k=16)
            P.emit("sync", lambda e, ts=ts: e.dma_start(out=stats_d[2:3, ts:ts + SL], in_=nmr_row[:]),
                   inc=s_statw, inc_k=16)
            P.emit("sync", lambda e, ts=ts: e.dma_start(
                out=mbc[:], in_=stats_d[0:1, ts:ts + SL].broadcast_to([128, SL])),
                waits=wv(s_statw) + wv(v_xn), inc=s_bc, inc_k=16)
            P.emit("sync", lambda e, ts=ts: e.dma_start(
                out=rbc[:], in_=stats_d[1:2, ts:ts + SL].broadcast_to([128, SL])),
                inc=s_bc, inc_k=16)
            for c in range(8):
                P.emit("vector", lambda e, p=par, c=c: e.tensor_sub(
                    xf[p][:, c * SL:(c + 1) * SL], xf[p][:, c * SL:(c + 1) * SL], mbc[:]),
                    waits=(wv(s_bc, s_xf[par]) + wv(p_kv) if c == 0 else []))
            for c in range(8):
                P.emit("vector", lambda e, p=par, c=c, xsl=xsl: e.tensor_mul(
                    xsl(c), xf[p][:, c * SL:(c + 1) * SL], rbc[:]),
                    inc=(v_xn if c == 7 else None))
            for c in range(8):
                P.emit("tensor", lambda e, p=par, c=c, xsl=xsl: e.matmul(
                    kvp[p][:], wkv_bf[:, c * 128:(c + 1) * 128], xsl(c),
                    start=(c == 0), stop=(c == 7)),
                    waits=(wv(v_xn, s_cast) + wb(a_k, 1) + wb(v_v, 1) if c == 0 else []),
                    inc=(p_kv if c == 7 else None))
            P.emit("scalar", lambda e, p=par: e.activation(kbf[:], kvp[p][0:64, :], AF.Identity),
                   waits=wv(p_kv, p_ksw, v_k), inc=a_k)
            P.emit("vector", lambda e, p=par: e.tensor_copy(vbf[:], kvp[p][64:128, :]),
                   waits=wv(p_kv, p_vt), inc=v_v)
            P.emit("tensor", lambda e: e.matmul(kswp[:], pswap_s[0:64, 0:64], kbf[:],
                                                start=True, stop=True),
                   waits=wv(a_k, v_k), inc=p_ksw)
            P.emit("vector", lambda e, p=par: e.tensor_mul(kt1[:], kbf[:], c2b[p][0:64, :]),
                   waits=wv(s_tab[par], p_ksw))
            P.emit("vector", lambda e: e.tensor_copy(kswb[:], kswp[0:64, :]))
            P.emit("vector", lambda e, p=par: e.tensor_mul(kt2[:], kswb[:], s2b[p][0:64, :]))
            P.emit("vector", lambda e, ts=ts: e.tensor_add(krope[0:64, ts:ts + SL], kt1[:], kt2[:]))
            P.emit("vector", lambda e, ts=ts: e.tensor_copy(
                krope[64:128, ts:ts + SL], krope[0:64, ts:ts + SL]), inc=v_k)
            for k2 in range(2):
                P.emit("tensor", lambda e, k2=k2: e.transpose(
                    vtp[:, k2 * 64:(k2 + 1) * 64], vbf[:, k2 * 128:(k2 + 1) * 128], identb_s[0:64, 0:64]),
                    waits=(wv(v_v, s_const, v_vd) if k2 == 0 else []),
                    inc=(p_vt if k2 == 1 else None))
            for k2 in range(2):
                jt = ts // 128 + k2
                P.emit("vector", lambda e, k2=k2, jt=jt: e.tensor_copy(
                    vdup[:, jt * 128:jt * 128 + 64], vtp[:, k2 * 64:(k2 + 1) * 64]),
                    waits=(wv(p_vt) if k2 == 0 else []))
                P.emit("vector", lambda e, k2=k2, jt=jt: e.tensor_copy(
                    vdup[:, jt * 128 + 64:jt * 128 + 128], vtp[:, k2 * 64:(k2 + 1) * 64]),
                    inc=(v_vd if k2 == 1 else None))

    dump("xn", xn_all, 8 * TOK, (v_xn,))
    dump("krope", krope, S, (v_k,))
    dump("vdup", vdup, S, (v_vd,))

    p_q = sem("p_q"); a_q = sem("a_q"); p_qsw = sem("p_qsw"); v_q = sem("v_q")
    p_dg = sem("p_dg"); v_dp = sem("v_dp"); v_call = sem("v_call"); v_cb = sem("v_cb")
    s_cb = sem("s_cb"); s_qtab = sem("s_qtab")

    with contextlib.ExitStack() as p2:
        wqt_bf = p2.enter_context(nc.sbuf_tensor("wqt_bf", [128, 8 * 1024], BF16, side="right"))
        T2 = lambda *a, **k: p2.enter_context(nc.sbuf_tensor(*a, **k))
        qbf = T2("qbf", [128, 512], BF16)
        qswb = T2("qswb", [128, 512], BF16)
        qt1 = T2("qt1", [128, 512], BF16)
        qt2 = T2("qt2", [128, 512], BF16)
        dp = T2("dp", [128, 512], BF16)
        c2q = T2("c2q", [128, 512], BF16)
        s2q = T2("s2q", [128, 512], BF16)
        cbrow = T2("cbrow", [1, 1], F32)
        qp = [p2.enter_context(nc.psum_tensor(f"qp{i}", [128, 512], F32)) for i in range(2)]
        qswp = p2.enter_context(nc.psum_tensor("qswp", [128, 512], F32))
        dgA = p2.enter_context(nc.psum_tensor("dgA", [64, 512], F32))
        dgB = p2.enter_context(nc.psum_tensor("dgB", [64, 512], F32))

        s_wqt = sem("s_wqt")
        P.emit("gpsimd", lambda e: e.dma_start(
            out=wqt_bf[:].rearrange("p (c i) -> p c i", c=8),
            in_=wqt_d.rearrange("(c p) i -> p c i", p=128)), inc=s_wqt, inc_k=16)

        for g in range(NQ):
            lq = g * QUAD
            ts = OWN_PSLOT[g] * QUAD
            P.emit("sync", lambda e, ts=ts: e.dma_start(out=c2q[:], in_=c2t_d[:, ts:ts + QUAD]),
                   waits=wv(v_q), inc=s_qtab, inc_k=16)
            P.emit("sync", lambda e, ts=ts: e.dma_start(out=s2q[:], in_=s2t_d[:, ts:ts + QUAD]),
                   inc=s_qtab, inc_k=16)
            for c in range(8):
                pp = c % 2
                for dc in range(8):
                    P.emit("tensor", lambda e, pp=pp, c=c, dc=dc, lq=lq: e.matmul(
                        qp[pp][:], wqt_bf[:, dc * 1024 + c * 128:dc * 1024 + (c + 1) * 128],
                        xn_all[:, dc * TOK + lq:dc * TOK + lq + QUAD],
                        start=(dc == 0), stop=(dc == 7)),
                        waits=(wv(s_wqt, v_xn) + wb(a_q, 1)
                               + (wv(v_mrow, v_var, v_k, v_vd, a_k, v_v)
                                  if (g == 0 and c == 0) else [])
                               if dc == 0 else []),
                        inc=(p_q if dc == 7 else None))
                P.emit("scalar", lambda e, pp=pp: e.activation(qbf[:], qp[pp][:], AF.Identity),
                       waits=wv(p_q, p_qsw, v_q), inc=a_q)
                P.emit("tensor", lambda e: e.matmul(qswp[:], pswap_s[:], qbf[:],
                                                    start=True, stop=True),
                       waits=wv(a_q, v_q), inc=p_qsw)
                P.emit("vector", lambda e: e.tensor_copy(qswb[:], qswp[:]), waits=wv(p_qsw))
                P.emit("vector", lambda e: e.tensor_mul(qt1[:], qbf[:], c2q[:]),
                       waits=wv(s_qtab))
                P.emit("vector", lambda e: e.tensor_mul(qt2[:], qswb[:], s2q[:]))
                P.emit("vector", lambda e, c=c, lq=lq: e.tensor_add(
                    qall[:, c * TOK + lq:c * TOK + lq + QUAD], qt1[:], qt2[:]), inc=v_q)
            for c in range(8):
                P.emit("vector", lambda e, c=c, lq=lq, ts=ts: e.tensor_mul(
                    dp[:], qall[:, c * TOK + lq:c * TOK + lq + QUAD], krope[:, ts:ts + QUAD]),
                    waits=wv(v_q, v_k) + wv(p_dg), inc=v_dp)
                P.emit("tensor", lambda e: e.matmul(dgA[:], maskA_s[:], dp[:], start=True, stop=True),
                       waits=wv(v_dp, v_call))
                P.emit("tensor", lambda e: e.matmul(dgB[:], maskB_s[:], dp[:], start=True, stop=True),
                       inc=p_dg)
                idx = g * 16 + 2 * c
                P.emit("vector", lambda e, idx=idx: e.reduce_max(
                    call_s[0:1, idx:idx + 1], dgA[0:1, :], axis=AX.X), waits=wv(p_dg), inc=v_call)
                P.emit("vector", lambda e, idx=idx: e.reduce_max(
                    call_s[0:1, idx + 1:idx + 2], dgB[0:1, :], axis=AX.X), inc=v_call)
        P.emit("vector", lambda e: e.reduce_max(cbrow[:], call_s[:], axis=AX.X),
               waits=wv(v_call), inc=v_cb)
        P.emit("vector", lambda e: e.tensor_scalar_mul(cbrow[:], cbrow[:], -0.125),
               waits=wv(v_cb), inc=v_cb)
        P.emit("sync", lambda e: e.dma_start(out=cb_d, in_=cbrow[:]),
               waits=wv(v_cb), inc=s_cb, inc_k=16)
        P.emit("sync", lambda e: e.dma_start(out=cbias[:], in_=cb_d.broadcast_to([128, 1])),
               waits=wv(s_cb), inc=s_cb, inc_k=16)

    dump("qall", qall, 8 * TOK, (v_q,))

    p_qk = sem("p_qk"); a_exp = sem("a_exp"); v_msk = sem("v_msk"); p_av = sem("p_av")
    v_oev = sem("v_oev"); v_rec = sem("v_rec"); s_sumw = sem("s_sumw"); s_rbc = sem("s_rbc")
    v_div = sem("v_div"); s_amask = [sem("s_amask0"), sem("s_amask1")]

    with contextlib.ExitStack() as p3:
        T3 = lambda *a, **k: p3.enter_context(nc.sbuf_tensor(*a, **k))
        Ebuf = [T3(f"Ebuf{i}", [128, 1024], BF16) for i in range(2)]
        mbuf = [T3(f"mbuf{i}", [128, 1024], BF16) for i in range(2)]
        ostg = [T3(f"ostg{i}", [128, 512], F32) for i in range(2)]
        ssb = T3("ssb", [128, 512], F32)
        recA = T3("recA", [1, 512], F32)
        recB = T3("recB", [1, 512], F32)
        rbc2 = [T3(f"rbc2{i}", [128, 512], F32) for i in range(2)]
        scp = [p3.enter_context(nc.psum_tensor(f"scp{i}", [128, 1024], F32)) for i in range(2)]
        op_ = p3.enter_context(nc.psum_tensor("op_", [128, 512], F32))
        sump = p3.enter_context(nc.psum_tensor("sump", [128, 512], F32))

        for slot in range(NQ):
            lq = slot * QUAD
            TKT = TEXT_KT[slot]
            W0 = TKT - 16
            for hp in range(8):
                for j in range(TKT):
                    st = j % 2
                    if j >= W0:
                        w = j - W0
                        P.emit("sync", lambda e, slot=slot, w=w, st=st: e.dma_start(
                            out=mbuf[st][:], in_=amask_d[slot * 16 + w]),
                            waits=wb(v_msk, 1), inc=s_amask[st], inc_k=16)
                    qk_waits = wb(a_exp, 1)
                    if slot == 0 and hp == 0 and j == 0:
                        qk_waits = qk_waits + wv(s_cb, v_q, v_k, v_vd, v_call, a_q)
                    P.emit("tensor", lambda e, j=j, hp=hp, lq=lq, st=st: e.matmul(
                        scp[st][:, 0:512], krope[0:64, j * 128:(j + 1) * 128],
                        qall[0:64, hp * TOK + lq:hp * TOK + lq + QUAD],
                        start=True, stop=True, tile_position=(0, 0)), waits=qk_waits)
                    P.emit("tensor", lambda e, j=j, hp=hp, lq=lq, st=st: e.matmul(
                        scp[st][:, 512:1024], krope[64:128, j * 128:(j + 1) * 128],
                        qall[64:128, hp * TOK + lq:hp * TOK + lq + QUAD],
                        start=True, stop=True, tile_position=(64, 0)), inc=p_qk)
                    P.emit("scalar", lambda e, st=st: e.activation(
                        Ebuf[st][:], scp[st][:], AF.Exp, bias=cbias[:, 0:1], scale=0.125),
                        waits=wv(p_qk) + wb(p_av, 1), inc=a_exp)
                    if j >= W0:
                        P.emit("vector", lambda e, st=st: e.tensor_mul(
                            Ebuf[st][:], Ebuf[st][:], mbuf[st][:]),
                            waits=wv(a_exp, s_amask[st]), inc=v_msk)
                        dep = wv(v_msk)
                    else:
                        dep = wv(a_exp)
                    first = (j == 0)
                    last = (j == TKT - 1)
                    P.emit("tensor", lambda e, j=j, st=st, first=first, last=last: e.matmul(
                        op_[0:64, :], vdup[:, j * 128:j * 128 + 64], Ebuf[st][:, 0:512],
                        start=first, stop=last, tile_position=(0, 0),
                        skip_group_check=True),
                        waits=dep + (wv(v_oev) if first else []))
                    P.emit("tensor", lambda e, j=j, st=st, first=first, last=last: e.matmul(
                        op_[64:128, :], vdup[:, j * 128 + 64:j * 128 + 128], Ebuf[st][:, 512:1024],
                        start=first, stop=last, tile_position=(0, 64),
                        skip_group_check=True))
                    P.emit("tensor", lambda e, st=st, first=first, last=last: e.matmul(
                        sump[0:64, :], ones64_s[:], Ebuf[st][:, 0:512],
                        start=first, stop=last, tile_position=(0, 0),
                        skip_group_check=True))
                    P.emit("tensor", lambda e, st=st, first=first, last=last: e.matmul(
                        sump[64:128, :], ones64_s[:], Ebuf[st][:, 512:1024],
                        start=first, stop=last, tile_position=(0, 64),
                        skip_group_check=True), inc=p_av)
                ow = hp % 2
                P.emit("vector", lambda e: e.tensor_copy(ssb[:], sump[:]),
                       waits=wv(p_av, s_sumw))
                P.emit("vector", lambda e, ow=ow: e.tensor_copy(ostg[ow][:], op_[:]),
                       waits=wb(v_div, 1), inc=v_oev)
                P.emit("vector", lambda e: e.reciprocal(recA[:], ssb[0:1, :]))
                P.emit("vector", lambda e: e.reciprocal(recB[:], ssb[64:65, :]), inc=v_rec)
                r0 = slot * 16 + hp * 2
                P.emit("sync", lambda e, r0=r0: e.dma_start(out=sums_d[r0:r0 + 1, :], in_=recA[:]),
                       waits=wv(v_rec), inc=s_sumw, inc_k=16)
                P.emit("sync", lambda e, r0=r0: e.dma_start(out=sums_d[r0 + 1:r0 + 2, :], in_=recB[:]),
                       inc=s_sumw, inc_k=16)
                P.emit("sync", lambda e, r0=r0: e.dma_start(
                    out=rbc2[0][:], in_=sums_d[r0:r0 + 1, :].broadcast_to([128, 512])),
                    waits=wv(s_sumw) + wv(v_div), inc=s_rbc, inc_k=16)
                P.emit("sync", lambda e, r0=r0: e.dma_start(
                    out=rbc2[1][:], in_=sums_d[r0 + 1:r0 + 2, :].broadcast_to([128, 512])),
                    inc=s_rbc, inc_k=16)
                P.emit("vector", lambda e, ow=ow, hp=hp, lq=lq: e.tensor_mul(
                    qall[0:64, hp * TOK + lq:hp * TOK + lq + QUAD], ostg[ow][0:64, :],
                    rbc2[0][0:64, :]), waits=wv(s_rbc))
                P.emit("vector", lambda e, ow=ow, hp=hp, lq=lq: e.tensor_mul(
                    qall[64:128, hp * TOK + lq:hp * TOK + lq + QUAD], ostg[ow][64:128, :],
                    rbc2[1][64:128, :]), inc=v_div)

    dump("attn", qall, 8 * TOK, (v_div,))

    p_wo = sem("p_wo"); v_wo = sem("v_wo")
    s_wo = [sem("s_wo0"), sem("s_wo1")]
    s_wot = sem("s_wot")
    with contextlib.ExitStack() as pwo:
        wot_bf = pwo.enter_context(nc.sbuf_tensor("wot_bf", [128, 8 * 1024], BF16, side="right"))
        wstg = [pwo.enter_context(nc.sbuf_tensor(f"wstg{i}", [128, 512], F32)) for i in range(2)]
        wop = [pwo.enter_context(nc.psum_tensor(f"wop{i}", [128, 512], F32)) for i in range(2)]
        P.emit("gpsimd", lambda e: e.dma_start(
            out=wot_bf[:].rearrange("p (c i) -> p c i", c=8),
            in_=wot_d.rearrange("(c p) i -> p c i", p=128)),
            waits=wv(p_q), inc=s_wot, inc_k=16)
        it = 0
        for oc in range(8):
            for tsl in range(4):
                pp = it % 2
                lq = tsl * 512
                for ac in range(8):
                    P.emit("tensor", lambda e, pp=pp, oc=oc, ac=ac, lq=lq: e.matmul(
                        wop[pp][:], wot_bf[:, ac * 1024 + oc * 128:ac * 1024 + (oc + 1) * 128],
                        qall[:, ac * TOK + lq:ac * TOK + lq + 512],
                        start=(ac == 0), stop=(ac == 7)),
                        waits=(wv(v_div, s_wot) + wb(v_wo, 1) if ac == 0 else []),
                        inc=(p_wo if ac == 7 else None))
                P.emit("vector", lambda e, pp=pp: e.tensor_copy(wstg[pp][:], wop[pp][:]),
                       waits=wv(p_wo, s_wo[pp]), inc=v_wo)
                P.emit("sync", lambda e, pp=pp, oc=oc, lq=lq: e.dma_start(
                    out=attno_d[oc, :, lq:lq + 512], in_=wstg[pp][:]),
                    waits=wv(v_wo), inc=s_wo[pp], inc_k=16)
                it += 1

    attn_ctx.close()

    p_lg = sem("p_lg"); v_rt = sem("v_rt"); a_re = sem("a_re"); p_wt = sem("p_wt")
    v_wr = sem("v_wr"); s_wr = sem("s_wr"); g_v = sem("g_v"); v_init = sem("v_init")
    tp_v = sem("tp_v"); tp_s = sem("tp_s")
    with contextlib.ExitStack() as p4:
        T4 = lambda *a, **k: p4.enter_context(nc.sbuf_tensor(*a, **k))
        lgs = T4("lgs", [128, NE], F32)
        nmax = T4("nmax", [128, 1], F32)
        el = T4("el", [128, NE], F32)
        se = T4("se", [128, 1], F32)
        rec = T4("rec", [128, 1], F32)
        probs_all = T4("probs_all", [128, 128], F32)
        t4a = T4("t4a", [128, 64], F32)
        t2a = T4("t2a", [128, 32], F32)
        m1b = T4("m1b", [128, 128], F32)
        m2b = T4("m2b", [128, 128], F32)
        mk1a = T4("mk1a", [128, 128], F32)
        p2va = T4("p2va", [128, 128], F32)
        mk2a = T4("mk2a", [128, 128], F32)
        wsela = T4("wsela", [128, 128], F32)
        s12a = T4("s12a", [128, 128], F32)
        rdena = T4("rdena", [128, 128], F32)
        wta = T4("wta", [128, 128], F32)
        lgp = p4.enter_context(nc.psum_tensor("lgp", [128, NE], F32))
        wtp = p4.enter_context(nc.psum_tensor("wtp", [32, 128], F32))

        P.emit("vector", lambda e: e.memset(wroute_bf[:], 0.0), inc=v_init)
        P.emit("gpsimd", lambda e: e.dma_start(out=wroute_bf[8:9, :], in_=onesrow_d),
               waits=wv(v_init), inc=sem("s_ones"), inc_k=16)

        for tt in range(16):
            lq = tt * 128
            for dc in range(8):
                P.emit("tensor", lambda e, dc=dc, lq=lq: e.matmul(
                    lgp[:], xn_all[:, dc * TOK + lq:dc * TOK + lq + 128],
                    gw_bf[:, dc * 8:(dc + 1) * 8],
                    start=(dc == 0), stop=(dc == 7)),
                    waits=(wv(s_cast, v_rt) + (wv(v_wo, v_xn) if tt == 0 else [])
                           if dc == 0 else []),
                    inc=(p_lg if dc == 7 else None))
            P.emit("vector", lambda e: e.tensor_add(lgs[:], lgp[:], gbc_s[:]),
                   waits=wv(p_lg, s_const) + (wv(sem("s_dbgr")) if (debug and tt > 0) else []))
            P.emit("vector", lambda e: e.reduce_max(nmax[:], lgs[:], axis=AX.X, negate=True),
                   inc=g_v)
            P.emit("scalar", lambda e: e.activation(el[:], lgs[:], AF.Exp, bias=nmax[:, 0:1],
                                                    accum_out=se[:, 0:1]),
                   waits=wv(g_v), inc=a_re)
            P.emit("vector", lambda e: e.reciprocal(rec[:], se[:]), waits=wv(a_re), inc=g_v)
            P.emit("vector", lambda e, tt=tt: e.tensor_single_scalar(
                probs_all[:, tt * NE:(tt + 1) * NE], el[:], rec[:, 0:1],
                op=ALU.mult), waits=wv(g_v), inc=v_rt)
            if debug:
                s_dbgr = sem("s_dbgr")
                for slot_i, srcb in ((0, lgs),):
                    P.emit("sync", lambda e, tt=tt, slot_i=slot_i, srcb=srcb: e.dma_start(
                        out=dbg["router"][:, tt * 16 + slot_i * 8:tt * 16 + slot_i * 8 + 8],
                        in_=srcb[:, 0:8]),
                        waits=wv(v_rt), inc=s_dbgr, inc_k=16)

        rr8 = lambda t: t[:].rearrange("p (a e) -> p a e", e=8)
        rr4 = lambda t: t[:].rearrange("p (a e) -> p a e", e=4)
        rr2 = lambda t: t[:].rearrange("p (a e) -> p a e", e=2)
        step = {"i": 0}

        def chain(fn):
            i = step["i"]; step["i"] += 1
            P.emit("vector", fn,
                   waits=wv(tp_s) + (wv(v_rt) if i == 0 else []), inc=tp_v)
            P.emit("scalar", None, waits=wv(tp_v), inc=tp_s)

        def top_chain(src, dstb):
            chain(lambda e, src=src: e.tensor_tensor(
                rr4(t4a), rr8(src)[:, :, 0:4], rr8(src)[:, :, 4:8], op=ALU.max))
            chain(lambda e: e.tensor_tensor(
                rr2(t2a), rr4(t4a)[:, :, 0:2], rr4(t4a)[:, :, 2:4], op=ALU.max))
            chain(lambda e, dstb=dstb: e.tensor_tensor(
                rr8(dstb)[:, :, 0:1], rr2(t2a)[:, :, 0:1], rr2(t2a)[:, :, 1:2], op=ALU.max))
            chain(lambda e, dstb=dstb: e.tensor_copy(rr8(dstb)[:, :, 1:2], rr8(dstb)[:, :, 0:1]))
            chain(lambda e, dstb=dstb: e.tensor_copy(rr8(dstb)[:, :, 2:4], rr8(dstb)[:, :, 0:2]))
            chain(lambda e, dstb=dstb: e.tensor_copy(rr8(dstb)[:, :, 4:8], rr8(dstb)[:, :, 0:4]))

        top_chain(probs_all, m1b)
        chain(lambda e: e.tensor_tensor(mk1a[:], probs_all[:], m1b[:], op=ALU.is_ge))
        chain(lambda e: e.scalar_tensor_tensor(
            p2va[:], mk1a[:], -1e30, probs_all[:], op0=ALU.mult, op1=ALU.add))
        top_chain(p2va, m2b)
        chain(lambda e: e.tensor_tensor(mk2a[:], probs_all[:], m2b[:], op=ALU.is_ge))
        chain(lambda e: e.tensor_tensor(wsela[:], probs_all[:], mk2a[:], op=ALU.mult))
        chain(lambda e: e.tensor_add(s12a[:], m1b[:], m2b[:]))
        chain(lambda e: e.tensor_scalar_add(s12a[:], s12a[:], 1e-9))
        chain(lambda e: e.reciprocal(rdena[:], s12a[:]))
        chain(lambda e: e.tensor_tensor(wta[:], wsela[:], rdena[:], op=ALU.mult))

        for tt in range(16):
            lq = tt * 128
            P.emit("tensor", lambda e, tt=tt: e.transpose(
                wtp[0:8, :], wta[:, tt * 8:(tt + 1) * 8], identf_s[:]),
                waits=(wv(tp_v, tp_s, s_const) if tt == 0 else []) + wv(v_wr),
                inc=p_wt)
            P.emit("vector", lambda e, lq=lq: e.tensor_copy(wroute_bf[0:8, lq:lq + 128],
                                                            wtp[0:8, :]),
                   waits=wv(p_wt, v_init) + (wv(sems["s_ones"]) if lq == 0 else []),
                   inc=v_wr)
        P.emit("gpsimd", lambda e: e.dma_start(out=wroute_dd, in_=wroute_bf[0:8, :]),
               waits=wv(v_wr), inc=s_wr, inc_k=16)
        if debug:
            for off, srcb in ((256, probs_all), (384, wta), (512, m1b), (640, m2b)):
                P.emit("sync", lambda e, off=off, srcb=srcb: e.dma_start(
                    out=dbg["router"][:, off:off + 128], in_=srcb[:]),
                    waits=wv(tp_v, tp_s), inc=sem("s_dbgr"), inc_k=16)
            P.emit("gpsimd", lambda e: e.dma_start(out=dbg["wroute"], in_=wroute_bf[0:8, :]),
                   waits=wv(v_wr), inc=sem("s_dbgw"), inc_k=16)

    s_acc = sem("s_acc")
    s_w1 = [sem("s_w1_0"), sem("s_w1_1")]
    s_w2 = [sem("s_w2_0"), sem("s_w2_1")]
    s_bcf = [sem("s_bcf0"), sem("s_bcf1")]
    p_e1 = sem("p_e1"); a_gel = sem("a_gel"); p_e2 = sem("p_e2"); p_w2c = sem("p_w2c")
    v_acc = sem("v_acc"); p_tr = sem("p_tr")
    s_xt = [sem("s_xt0"), sem("s_xt1")]
    s_stt = [sem("s_stt0"), sem("s_stt1")]
    a_xn2 = sem("a_xn2"); v_out = sem("v_out")
    s_out = [sem("s_out0"), sem("s_out1")]

    with contextlib.ExitStack() as p5:
        T5 = lambda *a, **k: p5.enter_context(nc.sbuf_tensor(*a, **k))
        Hbuf = T5("Hbuf", [128, 32 * 1024], BF16)
        acc = T5("acc", [128, 8 * 1024], F32)
        tscr = T5("tscr", [128, 1024], F32)
        out_sb = [T5(f"out_sb{i}", [128, 1024], F32) for i in range(2)]
        xt_b = [T5(f"xt_b{i}", [128, 1024], F32) for i in range(2)]
        xnt = T5("xnt", [128, 1024], F32)
        rst = [T5(f"rst{i}", [128, 1], F32) for i in range(2)]
        nmrst = [T5(f"nmrst{i}", [128, 1], F32) for i in range(2)]
        w1b = [p5.enter_context(nc.sbuf_tensor(f"w1b{i}", [128, 8 * 512], BF16, side="right"))
               for i in range(2)]
        w2b = [p5.enter_context(nc.sbuf_tensor(f"w2b{i}", [128, 512], BF16, side="right"))
               for i in range(2)]
        bcf = [p5.enter_context(nc.sbuf_tensor(f"bcf{i}", [128, 1024], F32, side="right"))
               for i in range(2)]
        PS = [p5.enter_context(nc.psum_tensor(f"PS{i}", [128, 1024], F32)) for i in range(4)]

        for g2 in range(2):
            lq0 = g2 * 1024
            for oc in range(8):
                P.emit("sync", lambda e, oc=oc, lq0=lq0: e.dma_start(
                    out=acc[:, oc * 1024:(oc + 1) * 1024], in_=attno_d[oc, :, lq0:lq0 + 1024]),
                    waits=(wv(s_wo[0], s_wo[1]) + wv(p_tr) if oc == 0 else []), inc=s_acc, inc_k=16)
            for mi in range(9):
                is_sh = (mi == 0)
                eidx = mi - 1
                w1src = w1s_d if is_sh else ew1_d[eidx]
                w2src = w2s_d if is_sh else ew2_d[eidx]
                if not is_sh:
                    P.emit("sync", lambda e, eidx=eidx, lq0=lq0, mi=mi: e.dma_start(
                        out=bcf[mi % 2][:],
                        in_=wroute_dd[eidx:eidx + 1, lq0:lq0 + 1024].broadcast_to([128, 1024])),
                        waits=wv(s_wr) + wb(v_acc, 8), inc=s_bcf[mi % 2], inc_k=16)
                for ig in range(8):
                    w1w = wb(p_e1, 4)
                    if g2 == 0 and mi == 0 and ig == 0:
                        w1w = w1w + wv(v_div)
                    P.emit("gpsimd", lambda e, ig=ig, w1src=w1src: e.dma_start(
                        out=w1b[ig % 2][:].rearrange("p (c i) -> p c i", c=8),
                        in_=w1src[:, ig * 512:(ig + 1) * 512].rearrange("(c p) i -> p c i", p=128)),
                        waits=w1w, inc=s_w1[ig % 2], inc_k=16)
                    for c2 in range(4):
                        ic = ig * 4 + c2
                        pp = ic % 2
                        for dc in range(8):
                            for th in range(2):
                                e1w = []
                                if dc == 0 and th == 0:
                                    e1w = wv(s_w1[ig % 2]) + wb(a_gel, 1)
                                    if ic == 0:
                                        e1w += wv(v_acc, v_out, p_e2, v_wr)
                                P.emit("tensor", lambda e, ig=ig, c2=c2, pp=pp, dc=dc, th=th, lq0=lq0: e.matmul(
                                    PS[pp][:, th * 512:(th + 1) * 512],
                                    w1b[ig % 2][:, dc * 512 + c2 * 128:dc * 512 + (c2 + 1) * 128],
                                    xn_all[:, dc * TOK + lq0 + th * 512:dc * TOK + lq0 + (th + 1) * 512],
                                    start=(dc == 0), stop=(dc == 7)),
                                    waits=e1w,
                                    inc=(p_e1 if (dc == 7 and th == 1) else None))
                        bsl = (b1l_s[:, ic:ic + 1] if is_sh
                               else eb1l_s[:, eidx * 32 + ic:eidx * 32 + ic + 1])
                        gw_ = wv(p_e1, s_const)
                        if ic == 0:
                            gw_ = gw_ + wv(p_w2c)
                        if g2 == 0 and mi == 0 and ic == 0:
                            gw_ = gw_ + wv(s_wr)
                            if debug:
                                gw_ = gw_ + wv(sem("s_dbgr"))
                        P.emit("scalar", lambda e, ic=ic, pp=pp, b=bsl: e.activation(
                            Hbuf[:, ic * 1024:(ic + 1) * 1024], PS[pp][:], AF.Gelu_apprx_tanh,
                            bias=b), waits=gw_, inc=a_gel)
                for dh in range(2):
                    for ic in range(32):
                        P.emit("gpsimd", lambda e, ic=ic, dh=dh, w2src=w2src: e.dma_start(
                            out=w2b[ic % 2][:],
                            in_=w2src[ic * 128:(ic + 1) * 128, dh * 512:(dh + 1) * 512]),
                            waits=wb(p_w2c, 1), inc=s_w2[ic % 2], inc_k=16)
                        for dc2 in range(4):
                            for th in range(2):
                                e2w = []
                                if ic == 0 and dc2 == 0 and th == 0:
                                    e2w = wv(s_w2[ic % 2], a_gel) + wv(v_acc)
                                elif dc2 == 0 and th == 0:
                                    e2w = wv(s_w2[ic % 2])
                                P.emit("tensor", lambda e, ic=ic, dc2=dc2, th=th: e.matmul(
                                    PS[dc2][:, th * 512:(th + 1) * 512],
                                    w2b[ic % 2][:, dc2 * 128:(dc2 + 1) * 128],
                                    Hbuf[:, ic * 1024 + th * 512:ic * 1024 + (th + 1) * 512],
                                    start=(ic == 0), stop=(ic == 31)),
                                    waits=e2w,
                                    inc=(p_w2c if (dc2 == 3 and th == 1) else None))
                    for dc2 in range(4):
                        dc = dh * 4 + dc2
                        if is_sh:
                            P.emit("vector", lambda e, dc2=dc2, dc=dc: e.tensor_add(
                                acc[:, dc * 1024:(dc + 1) * 1024],
                                acc[:, dc * 1024:(dc + 1) * 1024], PS[dc2][:]),
                                waits=(wv(p_w2c, s_acc) if dc2 == 0 else []),
                                inc=v_acc)
                        else:
                            P.emit("vector", lambda e, dc2=dc2, mi=mi: e.tensor_mul(
                                tscr[:], PS[dc2][:], bcf[mi % 2][:]),
                                waits=(wv(p_w2c, s_bcf[mi % 2]) if dc2 == 0 else []))
                            P.emit("vector", lambda e, dc2=dc2, dc=dc: e.tensor_add(
                                acc[:, dc * 1024:(dc + 1) * 1024],
                                acc[:, dc * 1024:(dc + 1) * 1024], tscr[:]),
                                inc=v_acc)
            for half in range(2):
                for dcc in range(4):
                    dc = half * 4 + dcc
                    for th in range(2):
                        bw = []
                        if dcc == 0 and th == 0:
                            bw = wv(s_cast, v_acc, v_wr)
                        P.emit("tensor", lambda e, dc=dc, dcc=dcc, th=th, lq0=lq0: e.matmul(
                            PS[dcc][:, th * 512:(th + 1) * 512],
                            eb2p_bf[:, dc * 128:(dc + 1) * 128],
                            wroute_bf[:, lq0 + th * 512:lq0 + (th + 1) * 512],
                            start=True, stop=True),
                            waits=bw, inc=(p_e2 if (dcc == 3 and th == 1) else None))
                for dcc in range(4):
                    dc = half * 4 + dcc
                    P.emit("vector", lambda e, dcc=dcc, dc=dc: e.tensor_add(
                        acc[:, dc * 1024:(dc + 1) * 1024],
                        acc[:, dc * 1024:(dc + 1) * 1024], PS[dcc][:]),
                        waits=(wv(p_e2) if dcc == 0 else []), inc=v_acc)
            if debug and g2 == 0:
                P.emit("sync", lambda e: e.dma_start(out=dbg["acc0"], in_=acc[:]),
                       waits=wv(v_acc), inc=sem("s_dbg"), inc_k=16)
            for tt in range(8):
                pp = tt % 2
                l0 = lq0 + tt * 128
                gl = OWN_PSLOT[l0 // QUAD] * QUAD + (l0 % QUAD)
                P.emit("sync", lambda e, pp=pp, l0=l0: e.dma_start(
                    out=xt_b[pp][:], in_=x_tm[l0:l0 + 128, :]),
                    waits=wb(v_out, 1), inc=s_xt[pp], inc_k=16)
                P.emit("sync", lambda e, pp=pp, gl=gl: e.dma_start(
                    out=rst[pp][:], in_=stats_d[1:2, gl:gl + 128].rearrange("o t -> t o")),
                    inc=s_stt[pp], inc_k=16)
                P.emit("sync", lambda e, pp=pp, gl=gl: e.dma_start(
                    out=nmrst[pp][:], in_=stats_d[2:3, gl:gl + 128].rearrange("o t -> t o")),
                    inc=s_stt[pp], inc_k=16)
                for oc in range(8):
                    P.emit("tensor", lambda e, oc=oc, tt=tt: e.transpose(
                        PS[tt % 2][:, oc * 128:(oc + 1) * 128],
                        acc[:, oc * 1024 + tt * 128:oc * 1024 + (tt + 1) * 128], identf_s[:]),
                        waits=(wv(v_acc) + wb(v_out, 1) if oc == 0 else []),
                        inc=(p_tr if oc == 7 else None))
                P.emit("scalar", lambda e, pp=pp: e.activation(
                    xnt[:], xt_b[pp][:], AF.Identity, bias=nmrst[pp][:, 0:1],
                    scale=rst[pp][:, 0:1]),
                    waits=wv(s_xt[pp], s_stt[pp], v_out), inc=a_xn2)
                P.emit("vector", lambda e, pp=pp, tt=tt: e.tensor_add(
                    out_sb[pp][:], PS[tt % 2][:], xt_b[pp][:]),
                       waits=wv(p_tr, s_xt[pp], s_out[pp]))
                P.emit("vector", lambda e, pp=pp: e.tensor_add(out_sb[pp][:], out_sb[pp][:], xnt[:]),
                       waits=wv(a_xn2), inc=v_out)
                P.emit("gpsimd", lambda e, pp=pp, l0=l0: e.dma_start(
                    out=y[l0:l0 + 128, :], in_=out_sb[pp][:]),
                    waits=wv(v_out), inc=s_out[pp], inc_k=16)

    fin = wv(s_out[0], s_out[1])
    if debug:
        fin = fin + wv(sem("s_dbg")) + wv(sem("s_dbgw")) + wv(sem("s_dbgr"))
    P.emit("gpsimd", None, waits=fin)

    with nc.Block() as block:
        for eng in Plan.ENGS:
            ops = P.ops[eng]

            def body(e, ops=ops):
                for fn in ops:
                    fn(e)
            getattr(block, eng)(body)
    ctx.close()
    return nc



def _pi_perm(j):
    own = [j, 7 - j, 8 + j, 15 - j]
    pi = []
    for i, o in enumerate(own):
        blk = [q for q in range(4 * i, 4 * i + 4) if q != o]
        pi.extend(blk + [o])
    return pi, own


def _host_constants(pi):
    bf = ml_dtypes.bfloat16
    pos = np.concatenate([np.arange(q * QUAD, (q + 1) * QUAD) for q in pi]).astype(np.float64)
    freqs = 1.0 / (1000.0 ** (np.arange(0, HD, 2, dtype=np.float64) / HD))
    pair = ((np.arange(128) % 64) // 2)
    ang = pos[None, :] * freqs[pair][:, None]
    c2 = np.cos(ang).astype(np.float32)
    s2 = np.sin(ang).astype(np.float32)
    sign = np.where(np.arange(128) % 2 == 0, -1.0, 1.0).astype(np.float32)
    s2 = s2 * sign[:, None]

    psw = np.zeros((128, 128), np.float32)
    for a in range(128):
        psw[a, a ^ 1] = 1.0
    identb = np.eye(128, dtype=np.float32)
    ones64 = np.ones((128, 64), np.float32)
    maskA = np.zeros((128, 64), np.float32); maskA[0:64] = 1.0
    maskB = np.zeros((128, 64), np.float32); maskB[64:128] = 1.0

    posi = pos.astype(np.int64)
    amask = np.zeros((64, 128, 1024), np.float32)
    for slot in range(4):
        qpos = posi[OWN_PSLOT[slot] * QUAD:(OWN_PSLOT[slot] + 1) * QUAD]
        T = TEXT_KT[slot]
        for w in range(16):
            jt = T - 16 + w
            kpos = posi[jt * 128:(jt + 1) * 128]
            m = (kpos[:, None] <= qpos[None, :]).astype(np.float32)
            amask[slot * 16 + w] = np.concatenate([m, m], axis=1)
    return dict(c2t=c2.astype(bf), s2t=s2.astype(bf), pswap=psw.astype(bf),
                identb=identb.astype(bf), identf=identb,
                ones64=ones64.astype(bf), maskA=maskA.astype(bf),
                maskB=maskB.astype(bf), amask=amask.astype(bf),
                onesrow=np.ones((1, 2048), np.float32).astype(bf))


_NC_CACHE = {}


def kernel(L0_attn_x, L0_attn_wq, L0_attn_wk, L0_attn_wv, L0_attn_wo,
           L0_ffn_gate_w, L0_ffn_gate_b, L0_ffn_sh_w1, L0_ffn_sh_b1,
           L0_ffn_sh_w2, L0_ffn_sh_b2, L0_ffn_ew1, L0_ffn_eb1,
           L0_ffn_ew2, L0_ffn_eb2):
    f32 = np.float32
    x = np.asarray(L0_attn_x, f32)
    wq = np.asarray(L0_attn_wq, f32)
    wk = np.asarray(L0_attn_wk, f32)
    wv = np.asarray(L0_attn_wv, f32)
    wo = np.asarray(L0_attn_wo, f32)
    gate_w = np.asarray(L0_ffn_gate_w, f32)
    gate_b = np.asarray(L0_ffn_gate_b, f32).reshape(1, NE)
    sh_w1 = np.asarray(L0_ffn_sh_w1, f32)
    sh_b1 = np.asarray(L0_ffn_sh_b1, f32)
    sh_w2 = np.asarray(L0_ffn_sh_w2, f32)
    sh_b2 = np.asarray(L0_ffn_sh_b2, f32)
    ew1 = np.ascontiguousarray(np.asarray(L0_ffn_ew1, f32))
    eb1 = np.asarray(L0_ffn_eb1, f32)
    ew2 = np.ascontiguousarray(np.asarray(L0_ffn_ew2, f32))
    eb2 = np.asarray(L0_ffn_eb2, f32)

    wqt = np.ascontiguousarray(wq.T)
    wkv = np.ascontiguousarray(np.concatenate([wk, wv], axis=1))
    wot = np.ascontiguousarray(wo.T)
    b1l = np.ascontiguousarray(sh_b1.reshape(32, 128).T)
    eb1l = np.ascontiguousarray(
        eb1.reshape(NE, 32, 128).transpose(2, 0, 1).reshape(128, NE * 32))
    eb2p = np.zeros((64, D), f32)
    eb2p[0:NE] = eb2
    eb2p[NE] = sh_b2

    if "nc" not in _NC_CACHE:
        _NC_CACHE["nc"] = build_program(debug=False)
    nc = _NC_CACHE["nc"]

    in_maps = []
    metas = []
    for core in range(8):
        beta = core // 4
        j = core % 4
        pi, own = _pi_perm(j)
        consts = _host_constants(pi)
        xb = x[beta]
        x_fm = np.ascontiguousarray(
            xb.T.reshape(D, 16, QUAD)[:, pi, :].reshape(D, S))
        x_tm = np.ascontiguousarray(
            np.concatenate([xb[q * QUAD:(q + 1) * QUAD] for q in own], axis=0))
        im = dict(consts)
        im.update(x_fm=x_fm, x_tm=x_tm, wqt=wqt, wkv=wkv, wot=wot,
                  gate_w=gate_w, gate_b=gate_b, sh_w1=sh_w1, sh_w2=sh_w2,
                  ew1=ew1, ew2=ew2, b1l=b1l, eb1l=eb1l, eb2p=eb2p)
        in_maps.append(im)
        metas.append((beta, own))

    res = run_bass_kernel_spmd(nc, in_maps, core_ids=list(range(8)))
    out = np.empty((B, S, D), f32)
    for core in range(8):
        beta, own = metas[core]
        yc = res.results[core]["y"]
        for i, q in enumerate(own):
            out[beta, q * QUAD:(q + 1) * QUAD] = yc[i * QUAD:(i + 1) * QUAD]
    return out


# version 06
# speedup vs baseline: 1.7951x; 1.7951x over previous
import contextlib
import numpy as np
import ml_dtypes
import concourse.bass as bass
import concourse.mybir as mybir
from concourse.bass_utils import run_bass_kernel_spmd

F32 = mybir.dt.float32
BF16 = mybir.dt.bfloat16
AF = mybir.ActivationFunctionType
ALU = mybir.AluOpType
AX = mybir.AxisListType

B, S, D = 2, 8192, 1024
H, HD = 16, 64
NI = 4096
NE = 8
QUAD = 512
NQ = 4
TOK = NQ * QUAD
SL = 256
NSL = S // SL
EPS = 1e-5
TEXT_KT = [16, 32, 48, 64]
OWN_PSLOT = [3, 7, 11, 15]


class Sem:
    __slots__ = ("h", "n")

    def __init__(self, h):
        self.h = h
        self.n = 0


class Plan:
    ENGS = ("tensor", "vector", "scalar", "gpsimd", "sync")

    def __init__(self):
        self.ops = {e: [] for e in self.ENGS}

    def emit(self, eng, fn=None, waits=(), inc=None, inc_k=1):
        w = [(s.h, v) for (s, v) in waits if v > 0]
        tgt = None
        if inc is not None:
            inc.n += inc_k
            tgt = (inc.h, inc_k)

        def run(e, w=w, fn=fn, tgt=tgt):
            for h, v in w:
                e.wait_ge(h, v)
            if fn is not None:
                ins = fn(e)
                if tgt is not None:
                    ins.then_inc(tgt[0], tgt[1])
            elif tgt is not None:
                e.sem_inc(tgt[0], tgt[1])

        self.ops[eng].append(run)

    @staticmethod
    def wv(*sems):
        return [(s, s.n) for s in sems]

    @staticmethod
    def wb(sem, back):
        return [(sem, max(0, sem.n - back))]


def build_program(debug=False):
    nc = bass.Bass("TRN2", target_bir_lowering=False)
    P = Plan()
    wv, wb = P.wv, P.wb

    def din(name, shape, dt=F32):
        return nc.dram_tensor(name, list(shape), dt, kind="ExternalInput").ap()

    x_fm = din("x_fm", (D, S))
    x_tm = din("x_tm", (TOK, D))
    c2t_d = din("c2t", (128, S), BF16)
    s2t_d = din("s2t", (128, S), BF16)
    pswap_d = din("pswap", (128, 128), BF16)
    identb_d = din("identb", (128, 128), BF16)
    identf_d = din("identf", (128, 128), F32)
    ones64_d = din("ones64", (128, 64), BF16)
    maskA_d = din("maskA", (128, 64), BF16)
    maskB_d = din("maskB", (128, 64), BF16)
    tri_d = din("tri", (128, 4 * 512), BF16)
    glog_d = din("glog", (128, 49))
    wqt_d = din("wqt", (128, 8 * 1024))
    wkv_d = din("wkv", (128, 8 * 128))
    wot_d = din("wot", (128, 8 * 1024))
    gw_d = din("gate_w", (D, NE))
    gb_d = din("gate_b", (1, NE))
    w1l_d = din("w1l", (9, 8, 128, 4096))
    w2l_d = din("w2l", (9, 2, 16, 128, 1024))
    b1l_d = din("b1l", (128, 32))
    eb1l_d = din("eb1l", (128, NE * 32))
    eb2p_d = din("eb2p", (64, D))
    onesrow_d = din("onesrow", (1, 2048), BF16)

    y = nc.dram_tensor("y", [TOK, D], F32, kind="ExternalOutput").ap()
    dbg = {}
    if debug:
        for nm, shp in (("xn", [128, 8 * TOK]), ("krope", [128, S]), ("vdup", [128, S]),
                        ("qall", [128, 8 * TOK]), ("attn", [128, 8 * TOK]),
                        ("wroute", [8, TOK]), ("acc0", [128, 8 * 1024]),
                        ("router", [128, 16 * 48])):
            dbg[nm] = nc.dram_tensor("dbg_" + nm, shp, F32, kind="ExternalOutput").ap()

    stats_d = nc.dram_tensor("stats_d", [3, S], F32, kind="Internal").ap()
    sums_d = nc.dram_tensor("sums_d", [64, QUAD], F32, kind="Internal").ap()
    cb_d = nc.dram_tensor("cb_d", [1, 1], F32, kind="Internal").ap()
    wroute_dd = nc.dram_tensor("wroute_dd", [NE, TOK], F32, kind="Internal").ap()
    attno_d = nc.dram_tensor("attno_d", [8, 128, TOK], F32, kind="Internal").ap()

    ctx = contextlib.ExitStack()
    sems = {}

    def sem(name):
        if name not in sems:
            sems[name] = Sem(ctx.enter_context(nc.semaphore(name)))
        return sems[name]

    ST = lambda *a, **k: ctx.enter_context(nc.sbuf_tensor(*a, **k))

    xn_all = ST("xn_all", [128, 8 * TOK], BF16)
    cbias = ST("cbias", [128, 1], F32)
    wroute_bf = ST("wroute_bf", [64, TOK], BF16)
    identb_s = ST("identb_s", [128, 128], BF16)
    identf_s = ST("identf_s", [128, 128], F32)
    pswap_s = ST("pswap_s", [128, 128], BF16)
    ones64_s = ST("ones64_s", [128, 64], BF16)
    maskA_s = ST("maskA_s", [128, 64], BF16)
    maskB_s = ST("maskB_s", [128, 64], BF16)
    tri_s = ST("tri_s", [128, 4 * 512], BF16)
    glog_s = ST("glog_s", [128, 49], F32)
    gbias = ST("gbias", [128, 49], F32)
    gw_bf = ST("gw_bf", [128, 64], BF16)
    gbc_s = ST("gbc_s", [128, NE], F32)
    b1l_s = ST("b1l_s", [128, 32], F32)
    eb1l_s = ST("eb1l_s", [128, NE * 32], F32)
    eb2p_bf = ST("eb2p_bf", [64, D], BF16)
    call_s = ST("call_s", [1, 64], F32)

    attn_ctx = contextlib.ExitStack()
    AT = lambda *a, **k: attn_ctx.enter_context(nc.sbuf_tensor(*a, **k))
    krope = AT("krope", [128, S], BF16, side="right")
    vdup = AT("vdup", [128, S], BF16, side="right")
    qall = AT("qall", [128, 8 * TOK], BF16, side="right")
    dstg = ST("dstg", [128, 2048], F32) if debug else None

    s_const = sem("s_const")
    s_cast = sem("s_cast")
    for o, i in ((identb_s, identb_d), (identf_s, identf_d), (pswap_s, pswap_d),
                 (ones64_s, ones64_d), (maskA_s, maskA_d), (maskB_s, maskB_d),
                 (b1l_s, b1l_d), (eb1l_s, eb1l_d), (tri_s, tri_d), (glog_s, glog_d)):
        P.emit("sync", lambda e, o=o, i=i: e.dma_start(out=o[:], in_=i), inc=s_const, inc_k=16)
    P.emit("sync", lambda e: e.dma_start(out=gbc_s[:], in_=gb_d.broadcast_to([128, NE])),
           inc=s_const, inc_k=16)
    P.emit("gpsimd", lambda e: e.dma_start(
        out=gw_bf[:].rearrange("p (c e) -> p c e", c=8),
        in_=gw_d.rearrange("(c p) e -> p c e", p=128)), inc=s_cast, inc_k=16)
    P.emit("gpsimd", lambda e: e.dma_start(out=eb2p_bf[:], in_=eb2p_d), inc=s_cast, inc_k=16)

    def dump(name, src, width, waitsems):
        if not debug:
            return
        s_dbg = sem("s_dbg")
        v_dbg = sem("v_dbg")
        for off in range(0, width, 2048):
            P.emit("vector", lambda e, src=src, off=off: e.tensor_copy(
                dstg[:], src[:, off:off + 2048]),
                waits=wv(*waitsems) + wv(s_dbg), inc=v_dbg)
            P.emit("sync", lambda e, name=name, off=off: e.dma_start(
                out=dbg[name][:, off:off + 2048], in_=dstg[:]),
                waits=wv(v_dbg), inc=s_dbg, inc_k=16)

    s_xf = [sem("s_xf0"), sem("s_xf1")]
    s_xb = [sem("s_xb0"), sem("s_xb1")]
    s_tab = [sem("s_tab0"), sem("s_tab1")]
    s_statw = sem("s_statw"); s_bc = sem("s_bc")
    p_st = sem("p_st"); p_kv = sem("p_kv"); p_ksw = sem("p_ksw"); p_vt = sem("p_vt")
    a_xsq = sem("a_xsq"); v_var = sem("v_var"); a_sq = sem("a_sq"); v_stats = sem("v_stats")
    v_xn = sem("v_xn"); a_k = sem("a_k"); v_v = sem("v_v"); v_k = sem("v_k"); v_vd = sem("v_vd")
    v_mrow = sem("v_mrow")

    with contextlib.ExitStack() as p1:
        T1 = lambda *a, **k: p1.enter_context(nc.sbuf_tensor(*a, **k))
        PT1 = lambda *a, **k: p1.enter_context(nc.psum_tensor(*a, **k))
        wkv_bf = T1("wkv_bf", [128, 8 * 128], BF16)
        P.emit("gpsimd", lambda e: e.dma_start(out=wkv_bf[:], in_=wkv_d),
               inc=s_cast, inc_k=16)
        xf = [T1(f"xf{i}", [128, 8 * SL], F32) for i in range(2)]
        xb = [T1(f"xb{i}", [128, 8 * SL], BF16) for i in range(2)]
        xsq = T1("xsq", [128, 8 * SL], BF16)
        c2b = [T1(f"c2b{i}", [128, SL], BF16) for i in range(2)]
        s2b = [T1(f"s2b{i}", [128, SL], BF16) for i in range(2)]
        xn_tmp = T1("xn_tmp", [128, 8 * SL], BF16)
        m_row = T1("m_row", [1, SL], F32)
        msq_row = T1("msq_row", [1, SL], F32)
        var_row = T1("var_row", [1, SL], F32)
        sq_row = T1("sq_row", [1, SL], F32)
        r_row = T1("r_row", [1, SL], F32)
        nmr_row = T1("nmr_row", [1, SL], F32)
        mbc = T1("mbc", [128, SL], F32)
        rbc = T1("rbc", [128, SL], F32)
        kbf = T1("kbf", [64, SL], BF16)
        kswb = T1("kswb", [64, SL], BF16)
        kt1 = T1("kt1", [64, SL], BF16)
        kt2 = T1("kt2", [64, SL], BF16)
        vbf = T1("vbf", [64, SL], BF16)
        s1p = [PT1(f"s1p{i}", [64, SL], F32) for i in range(2)]
        s2p = [PT1(f"s2p{i}", [64, SL], F32) for i in range(2)]
        kvp = [PT1(f"kvp{i}", [128, SL], F32) for i in range(2)]
        kswp = PT1("kswp", [64, SL], F32)
        vtp = PT1("vtp", [128, 128], BF16)

        for s in range(NSL):
            par = s % 2
            ts = s * SL
            pslot = ts // QUAD
            own = pslot in OWN_PSLOT
            dcol = (OWN_PSLOT.index(pslot) * QUAD + ts % QUAD) if own else 0
            xd = xn_all if own else xn_tmp

            def xsl(c, xd=xd, dcol=dcol):
                if xd is xn_all:
                    return xd[:, c * TOK + dcol:c * TOK + dcol + SL]
                return xd[:, c * SL:(c + 1) * SL]

            P.emit("sync", lambda e, p=par, ts=ts: e.dma_start(
                out=xf[p][:].rearrange("p (c t) -> p c t", c=8),
                in_=x_fm[:, ts:ts + SL].rearrange("(c p) t -> p c t", p=128)),
                waits=wb(v_xn, 1) if s >= 2 else [], inc=s_xf[par], inc_k=16)
            P.emit("gpsimd", lambda e, p=par, ts=ts: e.dma_start(
                out=xb[p][:].rearrange("p (c t) -> p c t", c=8),
                in_=x_fm[:, ts:ts + SL].rearrange("(c p) t -> p c t", p=128)),
                waits=wb(p_st, 1) if s >= 2 else [], inc=s_xb[par], inc_k=16)
            P.emit("sync", lambda e, p=par, ts=ts: e.dma_start(out=c2b[p][:], in_=c2t_d[:, ts:ts + SL]),
                   waits=wb(v_k, 1) if s >= 2 else [], inc=s_tab[par], inc_k=16)
            P.emit("sync", lambda e, p=par, ts=ts: e.dma_start(out=s2b[p][:], in_=s2t_d[:, ts:ts + SL]),
                   inc=s_tab[par], inc_k=16)

            P.emit("scalar", lambda e, p=par: e.activation(xsq[:], xb[p][:], AF.Square),
                   waits=wv(s_xb[par]) + wv(p_st), inc=a_xsq)
            for c in range(8):
                P.emit("tensor", lambda e, p=par, c=c: e.matmul(
                    s1p[p][:], ones64_s[:], xb[p][:, c * SL:(c + 1) * SL],
                    start=(c == 0), stop=(c == 7)),
                    waits=(wv(s_xb[par], s_const) + wb(v_mrow, 1) if c == 0 else []))
            for c in range(8):
                P.emit("tensor", lambda e, p=par, c=c: e.matmul(
                    s2p[p][:], ones64_s[:], xsq[:, c * SL:(c + 1) * SL],
                    start=(c == 0), stop=(c == 7)),
                    waits=(wv(a_xsq) + wb(v_var, 1) if c == 0 else []),
                    inc=(p_st if c == 7 else None))
            P.emit("vector", lambda e, p=par: e.tensor_scalar_mul(m_row[:], s1p[p][0:1, :], 1.0 / D),
                   waits=wv(p_st, s_statw), inc=v_mrow)
            P.emit("vector", lambda e: e.tensor_mul(msq_row[:], m_row[:], m_row[:]))
            P.emit("vector", lambda e, p=par: e.scalar_tensor_tensor(
                var_row[:], s2p[p][0:1, :], 1.0 / D, msq_row[:], op0=ALU.mult, op1=ALU.subtract),
                )
            P.emit("vector", lambda e: e.tensor_scalar_add(var_row[:], var_row[:], EPS),
                inc=v_var)
            P.emit("scalar", lambda e: e.activation(sq_row[:], var_row[:], AF.Sqrt),
                   waits=wv(v_var), inc=a_sq)
            P.emit("vector", lambda e: e.reciprocal(r_row[:], sq_row[:]), waits=wv(a_sq))
            P.emit("vector", lambda e: e.scalar_tensor_tensor(
                nmr_row[:], m_row[:], -1.0, r_row[:], op0=ALU.mult, op1=ALU.mult), inc=v_stats)
            P.emit("sync", lambda e, ts=ts: e.dma_start(out=stats_d[0:1, ts:ts + SL], in_=m_row[:]),
                   waits=wv(v_stats), inc=s_statw, inc_k=16)
            P.emit("sync", lambda e, ts=ts: e.dma_start(out=stats_d[1:2, ts:ts + SL], in_=r_row[:]),
                   inc=s_statw, inc_k=16)
            P.emit("sync", lambda e, ts=ts: e.dma_start(out=stats_d[2:3, ts:ts + SL], in_=nmr_row[:]),
                   inc=s_statw, inc_k=16)
            P.emit("sync", lambda e, ts=ts: e.dma_start(
                out=mbc[:], in_=stats_d[0:1, ts:ts + SL].broadcast_to([128, SL])),
                waits=wv(s_statw) + wv(v_xn), inc=s_bc, inc_k=16)
            P.emit("sync", lambda e, ts=ts: e.dma_start(
                out=rbc[:], in_=stats_d[1:2, ts:ts + SL].broadcast_to([128, SL])),
                inc=s_bc, inc_k=16)
            for c in range(8):
                P.emit("vector", lambda e, p=par, c=c: e.tensor_sub(
                    xf[p][:, c * SL:(c + 1) * SL], xf[p][:, c * SL:(c + 1) * SL], mbc[:]),
                    waits=(wv(s_bc, s_xf[par]) + wv(p_kv) if c == 0 else []))
            for c in range(8):
                P.emit("vector", lambda e, p=par, c=c, xsl=xsl: e.tensor_mul(
                    xsl(c), xf[p][:, c * SL:(c + 1) * SL], rbc[:]),
                    inc=(v_xn if c == 7 else None))
            for c in range(8):
                P.emit("tensor", lambda e, p=par, c=c, xsl=xsl: e.matmul(
                    kvp[p][:], wkv_bf[:, c * 128:(c + 1) * 128], xsl(c),
                    start=(c == 0), stop=(c == 7)),
                    waits=(wv(v_xn, s_cast) + wb(a_k, 1) + wb(v_v, 1) if c == 0 else []),
                    inc=(p_kv if c == 7 else None))
            P.emit("scalar", lambda e, p=par: e.activation(kbf[:], kvp[p][0:64, :], AF.Identity),
                   waits=wv(p_kv, p_ksw, v_k), inc=a_k)
            P.emit("vector", lambda e, p=par: e.tensor_copy(vbf[:], kvp[p][64:128, :]),
                   waits=wv(p_kv, p_vt), inc=v_v)
            P.emit("tensor", lambda e: e.matmul(kswp[:], pswap_s[0:64, 0:64], kbf[:],
                                                start=True, stop=True),
                   waits=wv(a_k, v_k), inc=p_ksw)
            P.emit("vector", lambda e, p=par: e.tensor_mul(kt1[:], kbf[:], c2b[p][0:64, :]),
                   waits=wv(s_tab[par], p_ksw))
            P.emit("vector", lambda e: e.tensor_copy(kswb[:], kswp[0:64, :]))
            P.emit("vector", lambda e, p=par: e.tensor_mul(kt2[:], kswb[:], s2b[p][0:64, :]))
            P.emit("vector", lambda e, ts=ts: e.tensor_add(krope[0:64, ts:ts + SL], kt1[:], kt2[:]))
            P.emit("vector", lambda e, ts=ts: e.tensor_copy(
                krope[64:128, ts:ts + SL], krope[0:64, ts:ts + SL]), inc=v_k)
            for k2 in range(2):
                P.emit("tensor", lambda e, k2=k2: e.transpose(
                    vtp[:, k2 * 64:(k2 + 1) * 64], vbf[:, k2 * 128:(k2 + 1) * 128], identb_s[0:64, 0:64]),
                    waits=(wv(v_v, s_const, v_vd) if k2 == 0 else []),
                    inc=(p_vt if k2 == 1 else None))
            for k2 in range(2):
                jt = ts // 128 + k2
                P.emit("vector", lambda e, k2=k2, jt=jt: e.tensor_copy(
                    vdup[:, jt * 128:jt * 128 + 64], vtp[:, k2 * 64:(k2 + 1) * 64]),
                    waits=(wv(p_vt) if k2 == 0 else []))
                P.emit("vector", lambda e, k2=k2, jt=jt: e.tensor_copy(
                    vdup[:, jt * 128 + 64:jt * 128 + 128], vtp[:, k2 * 64:(k2 + 1) * 64]),
                    inc=(v_vd if k2 == 1 else None))

    dump("xn", xn_all, 8 * TOK, (v_xn,))
    dump("krope", krope, S, (v_k,))
    dump("vdup", vdup, S, (v_vd,))

    p_q = sem("p_q"); a_q = sem("a_q"); p_qsw = sem("p_qsw"); v_q = sem("v_q")
    p_dg = sem("p_dg"); v_dp = sem("v_dp"); v_call = sem("v_call"); v_cb = sem("v_cb")
    s_cb = sem("s_cb"); s_qtab = sem("s_qtab")

    with contextlib.ExitStack() as p2:
        wqt_bf = p2.enter_context(nc.sbuf_tensor("wqt_bf", [128, 8 * 1024], BF16, side="right"))
        T2 = lambda *a, **k: p2.enter_context(nc.sbuf_tensor(*a, **k))
        qbf = T2("qbf", [128, 512], BF16)
        qswb = T2("qswb", [128, 512], BF16)
        qt1 = T2("qt1", [128, 512], BF16)
        qt2 = T2("qt2", [128, 512], BF16)
        dp = T2("dp", [128, 512], BF16)
        c2q = T2("c2q", [128, 512], BF16)
        s2q = T2("s2q", [128, 512], BF16)
        cbrow = T2("cbrow", [1, 1], F32)
        qp = [p2.enter_context(nc.psum_tensor(f"qp{i}", [128, 512], F32)) for i in range(2)]
        qswp = p2.enter_context(nc.psum_tensor("qswp", [128, 512], F32))
        dgA = p2.enter_context(nc.psum_tensor("dgA", [64, 512], F32))
        dgB = p2.enter_context(nc.psum_tensor("dgB", [64, 512], F32))

        s_wqt = sem("s_wqt")
        P.emit("gpsimd", lambda e: e.dma_start(out=wqt_bf[:], in_=wqt_d),
               inc=s_wqt, inc_k=16)

        for g in range(NQ):
            lq = g * QUAD
            ts = OWN_PSLOT[g] * QUAD
            P.emit("sync", lambda e, ts=ts: e.dma_start(out=c2q[:], in_=c2t_d[:, ts:ts + QUAD]),
                   waits=wv(v_q), inc=s_qtab, inc_k=16)
            P.emit("sync", lambda e, ts=ts: e.dma_start(out=s2q[:], in_=s2t_d[:, ts:ts + QUAD]),
                   inc=s_qtab, inc_k=16)
            for c in range(8):
                pp = c % 2
                for dc in range(8):
                    P.emit("tensor", lambda e, pp=pp, c=c, dc=dc, lq=lq: e.matmul(
                        qp[pp][:], wqt_bf[:, dc * 1024 + c * 128:dc * 1024 + (c + 1) * 128],
                        xn_all[:, dc * TOK + lq:dc * TOK + lq + QUAD],
                        start=(dc == 0), stop=(dc == 7)),
                        waits=(wv(s_wqt, v_xn) + wb(a_q, 1)
                               + (wv(v_mrow, v_var, v_k, v_vd, a_k, v_v)
                                  if (g == 0 and c == 0) else [])
                               if dc == 0 else []),
                        inc=(p_q if dc == 7 else None))
                P.emit("scalar", lambda e, pp=pp: e.activation(qbf[:], qp[pp][:], AF.Identity),
                       waits=wv(p_q, p_qsw, v_q), inc=a_q)
                P.emit("tensor", lambda e: e.matmul(qswp[:], pswap_s[:], qbf[:],
                                                    start=True, stop=True),
                       waits=wv(a_q, v_q), inc=p_qsw)
                P.emit("vector", lambda e: e.tensor_copy(qswb[:], qswp[:]), waits=wv(p_qsw))
                P.emit("vector", lambda e: e.tensor_mul(qt1[:], qbf[:], c2q[:]),
                       waits=wv(s_qtab))
                P.emit("vector", lambda e: e.tensor_mul(qt2[:], qswb[:], s2q[:]))
                P.emit("vector", lambda e, c=c, lq=lq: e.tensor_add(
                    qall[:, c * TOK + lq:c * TOK + lq + QUAD], qt1[:], qt2[:]), inc=v_q)
            for c in range(8):
                P.emit("vector", lambda e, c=c, lq=lq, ts=ts: e.tensor_mul(
                    dp[:], qall[:, c * TOK + lq:c * TOK + lq + QUAD], krope[:, ts:ts + QUAD]),
                    waits=wv(v_q, v_k) + wv(p_dg), inc=v_dp)
                P.emit("tensor", lambda e: e.matmul(dgA[:], maskA_s[:], dp[:], start=True, stop=True),
                       waits=wv(v_dp, v_call))
                P.emit("tensor", lambda e: e.matmul(dgB[:], maskB_s[:], dp[:], start=True, stop=True),
                       inc=p_dg)
                idx = g * 16 + 2 * c
                P.emit("vector", lambda e, idx=idx: e.reduce_max(
                    call_s[0:1, idx:idx + 1], dgA[0:1, :], axis=AX.X), waits=wv(p_dg), inc=v_call)
                P.emit("vector", lambda e, idx=idx: e.reduce_max(
                    call_s[0:1, idx + 1:idx + 2], dgB[0:1, :], axis=AX.X), inc=v_call)
        P.emit("vector", lambda e: e.reduce_max(cbrow[:], call_s[:], axis=AX.X),
               waits=wv(v_call), inc=v_cb)
        P.emit("vector", lambda e: e.tensor_scalar_mul(cbrow[:], cbrow[:], -0.125),
               waits=wv(v_cb), inc=v_cb)
        P.emit("sync", lambda e: e.dma_start(out=cb_d, in_=cbrow[:]),
               waits=wv(v_cb), inc=s_cb, inc_k=16)
        P.emit("sync", lambda e: e.dma_start(out=cbias[:], in_=cb_d.broadcast_to([128, 1])),
               waits=wv(s_cb), inc=s_cb, inc_k=16)

    a_gb = sem("a_gb")
    P.emit("scalar", lambda e: e.activation(gbias[:], glog_s[:], AF.Identity,
                                            bias=cbias[:, 0:1]),
           waits=wv(s_cb, s_const), inc=a_gb)

    dump("qall", qall, 8 * TOK, (v_q,))

    p_qk = sem("p_qk"); a_exp = sem("a_exp"); v_msk = sem("v_msk"); p_av = sem("p_av")
    v_oev = sem("v_oev"); v_rec = sem("v_rec"); s_sumw = sem("s_sumw"); s_rbc = sem("s_rbc")
    v_div = sem("v_div")

    with contextlib.ExitStack() as p3:
        T3 = lambda *a, **k: p3.enter_context(nc.sbuf_tensor(*a, **k))
        Ebuf = [T3(f"Ebuf{i}", [128, 1024], BF16) for i in range(3)]
        ostg = [T3(f"ostg{i}", [128, 512], F32) for i in range(2)]
        ssb = T3("ssb", [128, 512], F32)
        recA = T3("recA", [1, 512], F32)
        recB = T3("recB", [1, 512], F32)
        rbc2 = [T3(f"rbc2{i}", [128, 512], F32) for i in range(2)]
        scp = [p3.enter_context(nc.psum_tensor(f"scp{i}", [128, 1024], F32)) for i in range(3)]
        op_ = p3.enter_context(nc.psum_tensor("op_", [128, 512], F32))
        sump = p3.enter_context(nc.psum_tensor("sump", [128, 512], F32))

        steps = [(slot, hp, j) for slot in range(NQ) for hp in range(8)
                 for j in range(TEXT_KT[slot])]
        NST = len(steps)
        dep_tgt = {}

        def emit_epilogue(slot, hp):
            ow = hp % 2
            P.emit("vector", lambda e: e.tensor_copy(ssb[:], sump[:]),
                   waits=wv(p_av, s_sumw))
            P.emit("vector", lambda e, ow=ow: e.tensor_copy(ostg[ow][:], op_[:]),
                   waits=wb(v_div, 1), inc=v_oev)
            P.emit("vector", lambda e: e.reciprocal(recA[:], ssb[0:1, :]))
            P.emit("vector", lambda e: e.reciprocal(recB[:], ssb[64:65, :]), inc=v_rec)
            r0 = slot * 16 + hp * 2
            P.emit("sync", lambda e, r0=r0: e.dma_start(out=sums_d[r0:r0 + 1, :], in_=recA[:]),
                   waits=wv(v_rec), inc=s_sumw, inc_k=16)
            P.emit("sync", lambda e, r0=r0: e.dma_start(out=sums_d[r0 + 1:r0 + 2, :], in_=recB[:]),
                   inc=s_sumw, inc_k=16)
            P.emit("sync", lambda e, r0=r0: e.dma_start(
                out=rbc2[0][:], in_=sums_d[r0:r0 + 1, :].broadcast_to([128, 512])),
                waits=wv(s_sumw) + wv(v_div), inc=s_rbc, inc_k=16)
            P.emit("sync", lambda e, r0=r0: e.dma_start(
                out=rbc2[1][:], in_=sums_d[r0 + 1:r0 + 2, :].broadcast_to([128, 512])),
                inc=s_rbc, inc_k=16)
            lq = slot * QUAD
            P.emit("vector", lambda e, ow=ow, hp=hp, lq=lq: e.tensor_mul(
                qall[0:64, hp * TOK + lq:hp * TOK + lq + QUAD], ostg[ow][0:64, :],
                rbc2[0][0:64, :]), waits=wv(s_rbc))
            P.emit("vector", lambda e, ow=ow, hp=hp, lq=lq: e.tensor_mul(
                qall[64:128, hp * TOK + lq:hp * TOK + lq + QUAD], ostg[ow][64:128, :],
                rbc2[1][64:128, :]), inc=v_div)

        for it in range(NST + 2):
            if it < NST:
                slot, hp, j = steps[it]
                TKT = TEXT_KT[slot]
                W0 = TKT - 16
                lq = slot * QUAD
                buf = it % 3
                w = j - W0
                bcol = slot * 12 + w if (0 <= w < 12) else 48
                qk_waits = wb(a_exp, 2)
                if it == 0:
                    qk_waits = qk_waits + wv(s_cb, v_q, v_k, v_vd, v_call, a_q)
                P.emit("tensor", lambda e, j=j, hp=hp, lq=lq, buf=buf: e.matmul(
                    scp[buf][:, 0:512], krope[0:64, j * 128:(j + 1) * 128],
                    qall[0:64, hp * TOK + lq:hp * TOK + lq + QUAD],
                    start=True, stop=True, tile_position=(0, 0)), waits=qk_waits)
                P.emit("tensor", lambda e, j=j, hp=hp, lq=lq, buf=buf: e.matmul(
                    scp[buf][:, 512:1024], krope[64:128, j * 128:(j + 1) * 128],
                    qall[64:128, hp * TOK + lq:hp * TOK + lq + QUAD],
                    start=True, stop=True, tile_position=(64, 0)), inc=p_qk)
                exp_waits = wv(p_qk) + wv(p_av)
                if it == 0:
                    exp_waits = exp_waits + wv(a_gb)
                P.emit("scalar", lambda e, buf=buf, bcol=bcol: e.activation(
                    Ebuf[buf][:], scp[buf][:], AF.Exp, bias=gbias[:, bcol:bcol + 1],
                    scale=0.125), waits=exp_waits, inc=a_exp)
                if w >= 12:
                    tsl = tri_s[:, (w - 12) * 512:(w - 11) * 512]
                    P.emit("vector", lambda e, buf=buf, tsl=tsl: e.tensor_mul(
                        Ebuf[buf][:, 0:512], Ebuf[buf][:, 0:512], tsl),
                        waits=wv(a_exp) + (wv(s_const) if v_msk.n == 0 else []))
                    P.emit("vector", lambda e, buf=buf, tsl=tsl: e.tensor_mul(
                        Ebuf[buf][:, 512:1024], Ebuf[buf][:, 512:1024], tsl),
                        inc=v_msk)
                    dep_tgt[it] = (v_msk, v_msk.n)
                else:
                    dep_tgt[it] = (a_exp, a_exp.n)
            a = it - 2
            if a >= 0:
                slot, hp, j = steps[a]
                TKT = TEXT_KT[slot]
                buf = a % 3
                first = (j == 0)
                last = (j == TKT - 1)
                dsem, dval = dep_tgt.pop(a)
                dep = [(dsem, dval)]
                P.emit("tensor", lambda e, j=j, buf=buf, first=first, last=last: e.matmul(
                    op_[0:64, :], vdup[:, j * 128:j * 128 + 64], Ebuf[buf][:, 0:512],
                    start=first, stop=last, tile_position=(0, 0),
                    skip_group_check=True),
                    waits=dep + (wv(v_oev) if first else []))
                P.emit("tensor", lambda e, j=j, buf=buf, first=first, last=last: e.matmul(
                    op_[64:128, :], vdup[:, j * 128 + 64:j * 128 + 128], Ebuf[buf][:, 512:1024],
                    start=first, stop=last, tile_position=(0, 64),
                    skip_group_check=True))
                P.emit("tensor", lambda e, buf=buf, first=first, last=last: e.matmul(
                    sump[0:64, :], ones64_s[:], Ebuf[buf][:, 0:512],
                    start=first, stop=last, tile_position=(0, 0),
                    skip_group_check=True))
                P.emit("tensor", lambda e, buf=buf, first=first, last=last: e.matmul(
                    sump[64:128, :], ones64_s[:], Ebuf[buf][:, 512:1024],
                    start=first, stop=last, tile_position=(0, 64),
                    skip_group_check=True), inc=p_av)
                if last:
                    emit_epilogue(slot, hp)

    dump("attn", qall, 8 * TOK, (v_div,))

    p_wo = sem("p_wo"); v_wo = sem("v_wo")
    s_wo = [sem("s_wo0"), sem("s_wo1")]
    s_wot = sem("s_wot")
    with contextlib.ExitStack() as pwo:
        wot_bf = pwo.enter_context(nc.sbuf_tensor("wot_bf", [128, 8 * 1024], BF16, side="right"))
        wstg = [pwo.enter_context(nc.sbuf_tensor(f"wstg{i}", [128, 512], F32)) for i in range(2)]
        wop = [pwo.enter_context(nc.psum_tensor(f"wop{i}", [128, 512], F32)) for i in range(2)]
        P.emit("gpsimd", lambda e: e.dma_start(out=wot_bf[:], in_=wot_d),
               waits=wv(p_q), inc=s_wot, inc_k=16)
        it = 0
        for oc in range(8):
            for tsl in range(4):
                pp = it % 2
                lq = tsl * 512
                for ac in range(8):
                    P.emit("tensor", lambda e, pp=pp, oc=oc, ac=ac, lq=lq: e.matmul(
                        wop[pp][:], wot_bf[:, ac * 1024 + oc * 128:ac * 1024 + (oc + 1) * 128],
                        qall[:, ac * TOK + lq:ac * TOK + lq + 512],
                        start=(ac == 0), stop=(ac == 7)),
                        waits=(wv(v_div, s_wot) + wb(v_wo, 1) if ac == 0 else []),
                        inc=(p_wo if ac == 7 else None))
                P.emit("vector", lambda e, pp=pp: e.tensor_copy(wstg[pp][:], wop[pp][:]),
                       waits=wv(p_wo, s_wo[pp]), inc=v_wo)
                P.emit("sync", lambda e, pp=pp, oc=oc, lq=lq: e.dma_start(
                    out=attno_d[oc, :, lq:lq + 512], in_=wstg[pp][:]),
                    waits=wv(v_wo), inc=s_wo[pp], inc_k=16)
                it += 1

    attn_ctx.close()

    p_lg = sem("p_lg"); v_rt = sem("v_rt"); a_re = sem("a_re"); p_wt = sem("p_wt")
    v_wr = sem("v_wr"); s_wr = sem("s_wr"); g_v = sem("g_v"); v_init = sem("v_init")
    tp_v = sem("tp_v"); tp_s = sem("tp_s")
    with contextlib.ExitStack() as p4:
        T4 = lambda *a, **k: p4.enter_context(nc.sbuf_tensor(*a, **k))
        lgs = T4("lgs", [128, NE], F32)
        nmax = T4("nmax", [128, 1], F32)
        el = T4("el", [128, NE], F32)
        se = T4("se", [128, 1], F32)
        rec = T4("rec", [128, 1], F32)
        probs_all = T4("probs_all", [128, 128], F32)
        t4a = T4("t4a", [128, 64], F32)
        t2a = T4("t2a", [128, 32], F32)
        m1b = T4("m1b", [128, 128], F32)
        m2b = T4("m2b", [128, 128], F32)
        mk1a = T4("mk1a", [128, 128], F32)
        p2va = T4("p2va", [128, 128], F32)
        mk2a = T4("mk2a", [128, 128], F32)
        wsela = T4("wsela", [128, 128], F32)
        s12a = T4("s12a", [128, 128], F32)
        rdena = T4("rdena", [128, 128], F32)
        wta = T4("wta", [128, 128], F32)
        lgp = p4.enter_context(nc.psum_tensor("lgp", [128, NE], F32))
        wtp = p4.enter_context(nc.psum_tensor("wtp", [32, 128], F32))

        P.emit("vector", lambda e: e.memset(wroute_bf[:], 0.0), inc=v_init)
        P.emit("gpsimd", lambda e: e.dma_start(out=wroute_bf[8:9, :], in_=onesrow_d),
               waits=wv(v_init), inc=sem("s_ones"), inc_k=16)

        for tt in range(16):
            lq = tt * 128
            for dc in range(8):
                P.emit("tensor", lambda e, dc=dc, lq=lq: e.matmul(
                    lgp[:], xn_all[:, dc * TOK + lq:dc * TOK + lq + 128],
                    gw_bf[:, dc * 8:(dc + 1) * 8],
                    start=(dc == 0), stop=(dc == 7)),
                    waits=(wv(s_cast, v_rt) + (wv(v_wo, v_xn) if tt == 0 else [])
                           if dc == 0 else []),
                    inc=(p_lg if dc == 7 else None))
            P.emit("vector", lambda e: e.tensor_add(lgs[:], lgp[:], gbc_s[:]),
                   waits=wv(p_lg, s_const) + (wv(sem("s_dbgr")) if (debug and tt > 0) else []))
            P.emit("vector", lambda e: e.reduce_max(nmax[:], lgs[:], axis=AX.X, negate=True),
                   inc=g_v)
            P.emit("scalar", lambda e: e.activation(el[:], lgs[:], AF.Exp, bias=nmax[:, 0:1],
                                                    accum_out=se[:, 0:1]),
                   waits=wv(g_v), inc=a_re)
            P.emit("vector", lambda e: e.reciprocal(rec[:], se[:]), waits=wv(a_re), inc=g_v)
            P.emit("vector", lambda e, tt=tt: e.tensor_single_scalar(
                probs_all[:, tt * NE:(tt + 1) * NE], el[:], rec[:, 0:1],
                op=ALU.mult), waits=wv(g_v), inc=v_rt)
            if debug:
                s_dbgr = sem("s_dbgr")
                for slot_i, srcb in ((0, lgs),):
                    P.emit("sync", lambda e, tt=tt, slot_i=slot_i, srcb=srcb: e.dma_start(
                        out=dbg["router"][:, tt * 16 + slot_i * 8:tt * 16 + slot_i * 8 + 8],
                        in_=srcb[:, 0:8]),
                        waits=wv(v_rt), inc=s_dbgr, inc_k=16)

        rr8 = lambda t: t[:].rearrange("p (a e) -> p a e", e=8)
        rr4 = lambda t: t[:].rearrange("p (a e) -> p a e", e=4)
        rr2 = lambda t: t[:].rearrange("p (a e) -> p a e", e=2)
        step = {"i": 0}

        def chain(fn):
            i = step["i"]; step["i"] += 1
            P.emit("vector", fn,
                   waits=wv(tp_s) + (wv(v_rt) if i == 0 else []), inc=tp_v)
            P.emit("scalar", None, waits=wv(tp_v), inc=tp_s)

        def top_chain(src, dstb):
            chain(lambda e, src=src: e.tensor_tensor(
                rr4(t4a), rr8(src)[:, :, 0:4], rr8(src)[:, :, 4:8], op=ALU.max))
            chain(lambda e: e.tensor_tensor(
                rr2(t2a), rr4(t4a)[:, :, 0:2], rr4(t4a)[:, :, 2:4], op=ALU.max))
            chain(lambda e, dstb=dstb: e.tensor_tensor(
                rr8(dstb)[:, :, 0:1], rr2(t2a)[:, :, 0:1], rr2(t2a)[:, :, 1:2], op=ALU.max))
            chain(lambda e, dstb=dstb: e.tensor_copy(rr8(dstb)[:, :, 1:2], rr8(dstb)[:, :, 0:1]))
            chain(lambda e, dstb=dstb: e.tensor_copy(rr8(dstb)[:, :, 2:4], rr8(dstb)[:, :, 0:2]))
            chain(lambda e, dstb=dstb: e.tensor_copy(rr8(dstb)[:, :, 4:8], rr8(dstb)[:, :, 0:4]))

        top_chain(probs_all, m1b)
        chain(lambda e: e.tensor_tensor(mk1a[:], probs_all[:], m1b[:], op=ALU.is_ge))
        chain(lambda e: e.scalar_tensor_tensor(
            p2va[:], mk1a[:], -1e30, probs_all[:], op0=ALU.mult, op1=ALU.add))
        top_chain(p2va, m2b)
        chain(lambda e: e.tensor_tensor(mk2a[:], probs_all[:], m2b[:], op=ALU.is_ge))
        chain(lambda e: e.tensor_tensor(wsela[:], probs_all[:], mk2a[:], op=ALU.mult))
        chain(lambda e: e.tensor_add(s12a[:], m1b[:], m2b[:]))
        chain(lambda e: e.tensor_scalar_add(s12a[:], s12a[:], 1e-9))
        chain(lambda e: e.reciprocal(rdena[:], s12a[:]))
        chain(lambda e: e.tensor_tensor(wta[:], wsela[:], rdena[:], op=ALU.mult))

        for tt in range(16):
            lq = tt * 128
            P.emit("tensor", lambda e, tt=tt: e.transpose(
                wtp[0:8, :], wta[:, tt * 8:(tt + 1) * 8], identf_s[:]),
                waits=(wv(tp_v, tp_s, s_const) if tt == 0 else []) + wv(v_wr),
                inc=p_wt)
            P.emit("vector", lambda e, lq=lq: e.tensor_copy(wroute_bf[0:8, lq:lq + 128],
                                                            wtp[0:8, :]),
                   waits=wv(p_wt, v_init) + (wv(sems["s_ones"]) if lq == 0 else []),
                   inc=v_wr)
        P.emit("gpsimd", lambda e: e.dma_start(out=wroute_dd, in_=wroute_bf[0:8, :]),
               waits=wv(v_wr), inc=s_wr, inc_k=16)
        if debug:
            for off, srcb in ((256, probs_all), (384, wta), (512, m1b), (640, m2b)):
                P.emit("sync", lambda e, off=off, srcb=srcb: e.dma_start(
                    out=dbg["router"][:, off:off + 128], in_=srcb[:]),
                    waits=wv(tp_v, tp_s), inc=sem("s_dbgr"), inc_k=16)
            P.emit("gpsimd", lambda e: e.dma_start(out=dbg["wroute"], in_=wroute_bf[0:8, :]),
                   waits=wv(v_wr), inc=sem("s_dbgw"), inc_k=16)

    s_acc = sem("s_acc")
    s_w1 = [sem("s_w1_0"), sem("s_w1_1")]
    s_w2 = [sem("s_w2_0"), sem("s_w2_1")]
    s_bcf = [sem("s_bcf0"), sem("s_bcf1")]
    p_e1 = sem("p_e1"); a_gel = sem("a_gel"); p_e2 = sem("p_e2"); p_w2c = sem("p_w2c")
    v_acc = sem("v_acc"); p_tr = sem("p_tr")
    s_xt = [sem("s_xt0"), sem("s_xt1")]
    s_stt = [sem("s_stt0"), sem("s_stt1")]
    a_xn2 = sem("a_xn2"); v_out = sem("v_out")
    s_out = [sem("s_out0"), sem("s_out1")]

    with contextlib.ExitStack() as p5:
        T5 = lambda *a, **k: p5.enter_context(nc.sbuf_tensor(*a, **k))
        Hbuf = T5("Hbuf", [128, 32 * 1024], BF16)
        acc = T5("acc", [128, 8 * 1024], F32)
        tscr = T5("tscr", [128, 1024], F32)
        out_sb = [T5(f"out_sb{i}", [128, 1024], F32) for i in range(2)]
        xt_b = [T5(f"xt_b{i}", [128, 1024], F32) for i in range(2)]
        xnt = T5("xnt", [128, 1024], F32)
        rst = [T5(f"rst{i}", [128, 1], F32) for i in range(2)]
        nmrst = [T5(f"nmrst{i}", [128, 1], F32) for i in range(2)]
        w1b = [p5.enter_context(nc.sbuf_tensor(f"w1b{i}", [128, 8 * 512], BF16, side="right"))
               for i in range(2)]
        w2b = [p5.enter_context(nc.sbuf_tensor(f"w2b{i}", [128, 1024], BF16, side="right"))
               for i in range(2)]
        bcf = p5.enter_context(nc.sbuf_tensor("bcf", [128, 1024], F32, side="right"))
        PS = [p5.enter_context(nc.psum_tensor(f"PS{i}", [128, 1024], F32)) for i in range(4)]

        for g2 in range(2):
            lq0 = g2 * 1024
            for oc in range(8):
                P.emit("sync", lambda e, oc=oc, lq0=lq0: e.dma_start(
                    out=acc[:, oc * 1024:(oc + 1) * 1024], in_=attno_d[oc, :, lq0:lq0 + 1024]),
                    waits=(wv(s_wo[0], s_wo[1]) + wv(p_tr) if oc == 0 else []), inc=s_acc, inc_k=16)
            for mi in range(9):
                is_sh = (mi == 0)
                eidx = mi - 1
                for ig in range(8):
                    w1w = wb(p_e1, 4)
                    if g2 == 0 and mi == 0 and ig == 0:
                        w1w = w1w + wv(v_div)
                    P.emit("gpsimd", lambda e, ig=ig, mi=mi: e.dma_start(
                        out=w1b[ig % 2][:], in_=w1l_d[mi, ig]),
                        waits=w1w, inc=s_w1[ig % 2], inc_k=16)
                    for c2 in range(4):
                        ic = ig * 4 + c2
                        pp = ic % 2
                        for dc in range(8):
                            for th in range(2):
                                e1w = []
                                if dc == 0 and th == 0:
                                    e1w = wv(s_w1[ig % 2]) + wb(a_gel, 1)
                                    if ic == 0:
                                        e1w += wv(v_acc, v_out, p_e2, v_wr)
                                P.emit("tensor", lambda e, ig=ig, c2=c2, pp=pp, dc=dc, th=th, lq0=lq0: e.matmul(
                                    PS[pp][:, th * 512:(th + 1) * 512],
                                    w1b[ig % 2][:, dc * 512 + c2 * 128:dc * 512 + (c2 + 1) * 128],
                                    xn_all[:, dc * TOK + lq0 + th * 512:dc * TOK + lq0 + (th + 1) * 512],
                                    start=(dc == 0), stop=(dc == 7)),
                                    waits=e1w,
                                    inc=(p_e1 if (dc == 7 and th == 1) else None))
                        bsl = (b1l_s[:, ic:ic + 1] if is_sh
                               else eb1l_s[:, eidx * 32 + ic:eidx * 32 + ic + 1])
                        gw_ = wv(p_e1, s_const)
                        if ic == 0:
                            gw_ = gw_ + wv(p_w2c)
                        if g2 == 0 and mi == 0 and ic == 0:
                            gw_ = gw_ + wv(s_wr)
                            if debug:
                                gw_ = gw_ + wv(sem("s_dbgr"))
                        P.emit("scalar", lambda e, ic=ic, pp=pp, b=bsl: e.activation(
                            Hbuf[:, ic * 1024:(ic + 1) * 1024], PS[pp][:], AF.Gelu_apprx_tanh,
                            bias=b), waits=gw_, inc=a_gel)
                if not is_sh:
                    P.emit("sync", lambda e, eidx=eidx, lq0=lq0: e.dma_start(
                        out=bcf[:],
                        in_=wroute_dd[eidx:eidx + 1, lq0:lq0 + 1024].broadcast_to([128, 1024])),
                        waits=wv(s_wr, v_acc), inc=s_bcf[0], inc_k=16)
                for dh in range(2):
                    for blk in range(16):
                        w2w = wb(p_w2c, 2)
                        if g2 == 0 and mi == 0 and dh == 0 and blk == 0:
                            w2w = w2w + wv(v_div)
                        P.emit("gpsimd", lambda e, blk=blk, dh=dh, mi=mi: e.dma_start(
                            out=w2b[blk % 2][:], in_=w2l_d[mi, dh, blk]),
                            waits=w2w, inc=s_w2[blk % 2], inc_k=16)
                        for kk in range(2):
                            ic = blk * 2 + kk
                            for dc2 in range(4):
                                for th in range(2):
                                    e2w = []
                                    if ic == 0 and dc2 == 0 and th == 0:
                                        e2w = wv(s_w2[blk % 2], v_acc) + (
                                            wv(a_gel) if dh == 0 else [])
                                    elif kk == 0 and dc2 == 0 and th == 0:
                                        e2w = wv(s_w2[blk % 2])
                                    P.emit("tensor", lambda e, ic=ic, kk=kk, blk=blk, dc2=dc2, th=th: e.matmul(
                                        PS[dc2][:, th * 512:(th + 1) * 512],
                                        w2b[blk % 2][:, kk * 512 + dc2 * 128:kk * 512 + (dc2 + 1) * 128],
                                        Hbuf[:, ic * 1024 + th * 512:ic * 1024 + (th + 1) * 512],
                                        start=(ic == 0), stop=(ic == 31)),
                                        waits=e2w,
                                        inc=(p_w2c if (dc2 == 3 and th == 1) else None))
                    for dc2 in range(4):
                        dc = dh * 4 + dc2
                        if is_sh:
                            P.emit("vector", lambda e, dc2=dc2, dc=dc: e.tensor_add(
                                acc[:, dc * 1024:(dc + 1) * 1024],
                                acc[:, dc * 1024:(dc + 1) * 1024], PS[dc2][:]),
                                waits=(wv(p_w2c, s_acc) if dc2 == 0 else []),
                                inc=v_acc)
                        else:
                            P.emit("vector", lambda e, dc2=dc2: e.tensor_mul(
                                tscr[:], PS[dc2][:], bcf[:]),
                                waits=(wv(p_w2c, s_bcf[0]) if dc2 == 0 else []))
                            P.emit("vector", lambda e, dc2=dc2, dc=dc: e.tensor_add(
                                acc[:, dc * 1024:(dc + 1) * 1024],
                                acc[:, dc * 1024:(dc + 1) * 1024], tscr[:]),
                                inc=v_acc)
            for half in range(2):
                for dcc in range(4):
                    dc = half * 4 + dcc
                    for th in range(2):
                        bw = []
                        if dcc == 0 and th == 0:
                            bw = wv(s_cast, v_acc, v_wr)
                        P.emit("tensor", lambda e, dc=dc, dcc=dcc, th=th, lq0=lq0: e.matmul(
                            PS[dcc][:, th * 512:(th + 1) * 512],
                            eb2p_bf[:, dc * 128:(dc + 1) * 128],
                            wroute_bf[:, lq0 + th * 512:lq0 + (th + 1) * 512],
                            start=True, stop=True),
                            waits=bw, inc=(p_e2 if (dcc == 3 and th == 1) else None))
                for dcc in range(4):
                    dc = half * 4 + dcc
                    P.emit("vector", lambda e, dcc=dcc, dc=dc: e.tensor_add(
                        acc[:, dc * 1024:(dc + 1) * 1024],
                        acc[:, dc * 1024:(dc + 1) * 1024], PS[dcc][:]),
                        waits=(wv(p_e2) if dcc == 0 else []), inc=v_acc)
            if debug and g2 == 0:
                P.emit("sync", lambda e: e.dma_start(out=dbg["acc0"], in_=acc[:]),
                       waits=wv(v_acc), inc=sem("s_dbg"), inc_k=16)
            for tt in range(8):
                pp = tt % 2
                l0 = lq0 + tt * 128
                gl = OWN_PSLOT[l0 // QUAD] * QUAD + (l0 % QUAD)
                P.emit("sync", lambda e, pp=pp, l0=l0: e.dma_start(
                    out=xt_b[pp][:], in_=x_tm[l0:l0 + 128, :]),
                    waits=wb(v_out, 1), inc=s_xt[pp], inc_k=16)
                P.emit("sync", lambda e, pp=pp, gl=gl: e.dma_start(
                    out=rst[pp][:], in_=stats_d[1:2, gl:gl + 128].rearrange("o t -> t o")),
                    inc=s_stt[pp], inc_k=16)
                P.emit("sync", lambda e, pp=pp, gl=gl: e.dma_start(
                    out=nmrst[pp][:], in_=stats_d[2:3, gl:gl + 128].rearrange("o t -> t o")),
                    inc=s_stt[pp], inc_k=16)
                for oc in range(8):
                    P.emit("tensor", lambda e, oc=oc, tt=tt: e.transpose(
                        PS[tt % 2][:, oc * 128:(oc + 1) * 128],
                        acc[:, oc * 1024 + tt * 128:oc * 1024 + (tt + 1) * 128], identf_s[:]),
                        waits=(wv(v_acc) + wb(v_out, 1) if oc == 0 else []),
                        inc=(p_tr if oc == 7 else None))
                P.emit("scalar", lambda e, pp=pp: e.activation(
                    xnt[:], xt_b[pp][:], AF.Identity, bias=nmrst[pp][:, 0:1],
                    scale=rst[pp][:, 0:1]),
                    waits=wv(s_xt[pp], s_stt[pp], v_out), inc=a_xn2)
                P.emit("vector", lambda e, pp=pp, tt=tt: e.tensor_add(
                    out_sb[pp][:], PS[tt % 2][:], xt_b[pp][:]),
                       waits=wv(p_tr, s_xt[pp], s_out[pp]))
                P.emit("vector", lambda e, pp=pp: e.tensor_add(out_sb[pp][:], out_sb[pp][:], xnt[:]),
                       waits=wv(a_xn2), inc=v_out)
                P.emit("gpsimd", lambda e, pp=pp, l0=l0: e.dma_start(
                    out=y[l0:l0 + 128, :], in_=out_sb[pp][:]),
                    waits=wv(v_out), inc=s_out[pp], inc_k=16)

    fin = wv(s_out[0], s_out[1])
    if debug:
        fin = fin + wv(sem("s_dbg")) + wv(sem("s_dbgw")) + wv(sem("s_dbgr"))
    P.emit("gpsimd", None, waits=fin)

    with nc.Block() as block:
        for eng in Plan.ENGS:
            ops = P.ops[eng]

            def body(e, ops=ops):
                for fn in ops:
                    fn(e)
            getattr(block, eng)(body)
    ctx.close()
    return nc



def _pi_perm(j):
    own = [j, 7 - j, 8 + j, 15 - j]
    pi = []
    for i, o in enumerate(own):
        blk = [q for q in range(4 * i, 4 * i + 4) if q != o]
        pi.extend(blk + [o])
    return pi, own


def _host_constants(pi, own):
    bf = ml_dtypes.bfloat16
    pos = np.concatenate([np.arange(q * QUAD, (q + 1) * QUAD) for q in pi]).astype(np.float64)
    freqs = 1.0 / (1000.0 ** (np.arange(0, HD, 2, dtype=np.float64) / HD))
    pair = ((np.arange(128) % 64) // 2)
    ang = pos[None, :] * freqs[pair][:, None]
    c2 = np.cos(ang).astype(np.float32)
    s2 = np.sin(ang).astype(np.float32)
    sign = np.where(np.arange(128) % 2 == 0, -1.0, 1.0).astype(np.float32)
    s2 = s2 * sign[:, None]

    psw = np.zeros((128, 128), np.float32)
    for a in range(128):
        psw[a, a ^ 1] = 1.0
    identb = np.eye(128, dtype=np.float32)
    ones64 = np.ones((128, 64), np.float32)
    maskA = np.zeros((128, 64), np.float32); maskA[0:64] = 1.0
    maskB = np.zeros((128, 64), np.float32); maskB[64:128] = 1.0

    glog = np.zeros((128, 49), np.float32)
    for slot in range(4):
        for w in range(12):
            kq = pi[4 * slot + w // 4]
            glog[:, slot * 12 + w] = 0.0 if kq < own[slot] else -1e5
    tri = np.zeros((128, 4 * 512), np.float32)
    r = np.arange(128)
    q = np.arange(512)
    for t in range(4):
        m = ((t * 128 + r)[:, None] <= q[None, :]).astype(np.float32)
        tri[:, t * 512:(t + 1) * 512] = m
    return dict(c2t=c2.astype(bf), s2t=s2.astype(bf), pswap=psw.astype(bf),
                identb=identb.astype(bf), identf=identb,
                ones64=ones64.astype(bf), maskA=maskA.astype(bf),
                maskB=maskB.astype(bf), tri=tri.astype(bf), glog=glog,
                onesrow=np.ones((1, 2048), np.float32).astype(bf))


_NC_CACHE = {}


def kernel(L0_attn_x, L0_attn_wq, L0_attn_wk, L0_attn_wv, L0_attn_wo,
           L0_ffn_gate_w, L0_ffn_gate_b, L0_ffn_sh_w1, L0_ffn_sh_b1,
           L0_ffn_sh_w2, L0_ffn_sh_b2, L0_ffn_ew1, L0_ffn_eb1,
           L0_ffn_ew2, L0_ffn_eb2):
    f32 = np.float32
    x = np.asarray(L0_attn_x, f32)
    wq = np.asarray(L0_attn_wq, f32)
    wk = np.asarray(L0_attn_wk, f32)
    wv = np.asarray(L0_attn_wv, f32)
    wo = np.asarray(L0_attn_wo, f32)
    gate_w = np.asarray(L0_ffn_gate_w, f32)
    gate_b = np.asarray(L0_ffn_gate_b, f32).reshape(1, NE)
    sh_w1 = np.asarray(L0_ffn_sh_w1, f32)
    sh_b1 = np.asarray(L0_ffn_sh_b1, f32)
    sh_w2 = np.asarray(L0_ffn_sh_w2, f32)
    sh_b2 = np.asarray(L0_ffn_sh_b2, f32)
    ew1 = np.ascontiguousarray(np.asarray(L0_ffn_ew1, f32))
    eb1 = np.asarray(L0_ffn_eb1, f32)
    ew2 = np.ascontiguousarray(np.asarray(L0_ffn_ew2, f32))
    eb2 = np.asarray(L0_ffn_eb2, f32)

    wqt = np.ascontiguousarray(
        wq.T.reshape(8, 128, D).transpose(1, 0, 2).reshape(128, 8 * D))
    wkv0 = np.concatenate([wk, wv], axis=1)
    wkv = np.ascontiguousarray(
        wkv0.reshape(8, 128, 128).transpose(1, 0, 2).reshape(128, 8 * 128))
    wot = np.ascontiguousarray(
        wo.T.reshape(8, 128, D).transpose(1, 0, 2).reshape(128, 8 * D))
    w1_all = np.concatenate([sh_w1[None], ew1], axis=0)
    w1l = np.ascontiguousarray(
        w1_all.reshape(9, 8, 128, 8, 512).transpose(0, 3, 2, 1, 4).reshape(9, 8, 128, 4096))
    w2_all = np.concatenate([sh_w2[None], ew2], axis=0)
    w2l = np.ascontiguousarray(
        w2_all.reshape(9, 16, 2, 128, 2, 512).transpose(0, 4, 1, 3, 2, 5).reshape(9, 2, 16, 128, 1024))
    b1l = np.ascontiguousarray(sh_b1.reshape(32, 128).T)
    eb1l = np.ascontiguousarray(
        eb1.reshape(NE, 32, 128).transpose(2, 0, 1).reshape(128, NE * 32))
    eb2p = np.zeros((64, D), f32)
    eb2p[0:NE] = eb2
    eb2p[NE] = sh_b2

    if "nc" not in _NC_CACHE:
        _NC_CACHE["nc"] = build_program(debug=False)
    nc = _NC_CACHE["nc"]

    in_maps = []
    metas = []
    for core in range(8):
        beta = core // 4
        j = core % 4
        pi, own = _pi_perm(j)
        consts = _host_constants(pi, own)
        xb = x[beta]
        x_fm = np.ascontiguousarray(
            xb.T.reshape(D, 16, QUAD)[:, pi, :].reshape(D, S))
        x_tm = np.ascontiguousarray(
            np.concatenate([xb[q * QUAD:(q + 1) * QUAD] for q in own], axis=0))
        im = dict(consts)
        im.update(x_fm=x_fm, x_tm=x_tm, wqt=wqt, wkv=wkv, wot=wot,
                  gate_w=gate_w, gate_b=gate_b, w1l=w1l, w2l=w2l,
                  b1l=b1l, eb1l=eb1l, eb2p=eb2p)
        in_maps.append(im)
        metas.append((beta, own))

    res = run_bass_kernel_spmd(nc, in_maps, core_ids=list(range(8)))
    out = np.empty((B, S, D), f32)
    for core in range(8):
        beta, own = metas[core]
        yc = res.results[core]["y"]
        for i, q in enumerate(own):
            out[beta, q * QUAD:(q + 1) * QUAD] = yc[i * QUAD:(i + 1) * QUAD]
    return out


# version 07
# speedup vs baseline: 1.8281x; 1.0184x over previous
import contextlib
import numpy as np
import ml_dtypes
import concourse.bass as bass
import concourse.mybir as mybir
from concourse.bass_utils import run_bass_kernel_spmd

F32 = mybir.dt.float32
BF16 = mybir.dt.bfloat16
AF = mybir.ActivationFunctionType
ALU = mybir.AluOpType
AX = mybir.AxisListType

B, S, D = 2, 8192, 1024
H, HD = 16, 64
NI = 4096
NE = 8
QUAD = 512
NQ = 4
TOK = NQ * QUAD
SL = 256
NSL = S // SL
EPS = 1e-5
TEXT_KT = [16, 32, 48, 64]
OWN_PSLOT = [3, 7, 11, 15]


class Sem:
    __slots__ = ("h", "n")

    def __init__(self, h):
        self.h = h
        self.n = 0


class Plan:
    ENGS = ("tensor", "vector", "scalar", "gpsimd", "sync")

    def __init__(self):
        self.ops = {e: [] for e in self.ENGS}

    def emit(self, eng, fn=None, waits=(), inc=None, inc_k=1):
        w = [(s.h, v) for (s, v) in waits if v > 0]
        tgt = None
        if inc is not None:
            inc.n += inc_k
            tgt = (inc.h, inc_k)

        def run(e, w=w, fn=fn, tgt=tgt):
            for h, v in w:
                e.wait_ge(h, v)
            if fn is not None:
                ins = fn(e)
                if tgt is not None:
                    ins.then_inc(tgt[0], tgt[1])
            elif tgt is not None:
                e.sem_inc(tgt[0], tgt[1])

        self.ops[eng].append(run)

    @staticmethod
    def wv(*sems):
        return [(s, s.n) for s in sems]

    @staticmethod
    def wb(sem, back):
        return [(sem, max(0, sem.n - back))]


def build_program(debug=False):
    nc = bass.Bass("TRN2", target_bir_lowering=False)
    P = Plan()
    wv, wb = P.wv, P.wb

    def din(name, shape, dt=F32):
        return nc.dram_tensor(name, list(shape), dt, kind="ExternalInput").ap()

    x_fm = din("x_fm", (D, S))
    x_tm = din("x_tm", (TOK, D))
    c2t_d = din("c2t", (128, S), BF16)
    s2t_d = din("s2t", (128, S), BF16)
    pswap_d = din("pswap", (128, 128), BF16)
    identb_d = din("identb", (128, 128), BF16)
    identf_d = din("identf", (128, 128), F32)
    ones64_d = din("ones64", (128, 64), BF16)
    maskA_d = din("maskA", (128, 64), BF16)
    maskB_d = din("maskB", (128, 64), BF16)
    tri_d = din("tri", (128, 4 * 512), BF16)
    glog_d = din("glog", (128, 49))
    wqt_d = din("wqt", (128, 8 * 1024))
    wkv_d = din("wkv", (128, 8 * 128))
    wot_d = din("wot", (128, 8 * 1024))
    gw_d = din("gate_w", (D, NE))
    gb_d = din("gate_b", (1, NE))
    w1l_d = din("w1l", (9, 8, 128, 4096))
    w2l_d = din("w2l", (9, 2, 16, 128, 1024))
    b1l_d = din("b1l", (128, 32))
    eb1l_d = din("eb1l", (128, NE * 32))
    eb2p_d = din("eb2p", (64, D))
    onesrow_d = din("onesrow", (1, 2048), BF16)

    y = nc.dram_tensor("y", [TOK, D], F32, kind="ExternalOutput").ap()
    dbg = {}
    if debug:
        for nm, shp in (("xn", [128, 8 * TOK]), ("krope", [128, S]), ("vdup", [128, S]),
                        ("qall", [128, 8 * TOK]), ("attn", [128, 8 * TOK]),
                        ("wroute", [8, TOK]), ("acc0", [128, 8 * 1024]),
                        ("router", [128, 16 * 48])):
            dbg[nm] = nc.dram_tensor("dbg_" + nm, shp, F32, kind="ExternalOutput").ap()

    stats_d = nc.dram_tensor("stats_d", [3, S], F32, kind="Internal").ap()
    sums_d = nc.dram_tensor("sums_d", [64, QUAD], F32, kind="Internal").ap()
    cb_d = nc.dram_tensor("cb_d", [1, 1], F32, kind="Internal").ap()
    wroute_dd = nc.dram_tensor("wroute_dd", [NE, TOK], F32, kind="Internal").ap()
    attno_d = nc.dram_tensor("attno_d", [8, 128, TOK], F32, kind="Internal").ap()

    ctx = contextlib.ExitStack()
    sems = {}

    def sem(name):
        if name not in sems:
            sems[name] = Sem(ctx.enter_context(nc.semaphore(name)))
        return sems[name]

    ST = lambda *a, **k: ctx.enter_context(nc.sbuf_tensor(*a, **k))

    xn_all = ST("xn_all", [128, 8 * TOK], BF16)
    cbias = ST("cbias", [128, 1], F32)
    wroute_bf = ST("wroute_bf", [64, TOK], BF16)
    identb_s = ST("identb_s", [128, 128], BF16)
    identf_s = ST("identf_s", [128, 128], F32)
    pswap_s = ST("pswap_s", [128, 128], BF16)
    ones64_s = ST("ones64_s", [128, 64], BF16)
    maskA_s = ST("maskA_s", [128, 64], BF16)
    maskB_s = ST("maskB_s", [128, 64], BF16)
    tri_s = ST("tri_s", [128, 4 * 512], BF16)
    glog_s = ST("glog_s", [128, 49], F32)
    gbias = ST("gbias", [128, 49], F32)
    gw_bf = ST("gw_bf", [128, 64], BF16)
    gbc_s = ST("gbc_s", [128, NE], F32)
    b1l_s = ST("b1l_s", [128, 32], F32)
    eb1l_s = ST("eb1l_s", [128, NE * 32], F32)
    eb2p_bf = ST("eb2p_bf", [64, D], BF16)
    call_s = ST("call_s", [1, 64], F32)

    attn_ctx = contextlib.ExitStack()
    AT = lambda *a, **k: attn_ctx.enter_context(nc.sbuf_tensor(*a, **k))
    krope = AT("krope", [128, S], BF16, side="right")
    vdup = AT("vdup", [128, S], BF16, side="right")
    qall = AT("qall", [128, 8 * TOK], BF16, side="right")
    dstg = ST("dstg", [128, 2048], F32) if debug else None

    s_const = sem("s_const")
    s_cast = sem("s_cast")
    for o, i in ((identb_s, identb_d), (identf_s, identf_d), (pswap_s, pswap_d),
                 (ones64_s, ones64_d), (maskA_s, maskA_d), (maskB_s, maskB_d),
                 (b1l_s, b1l_d), (eb1l_s, eb1l_d), (tri_s, tri_d), (glog_s, glog_d)):
        P.emit("sync", lambda e, o=o, i=i: e.dma_start(out=o[:], in_=i), inc=s_const, inc_k=16)
    P.emit("sync", lambda e: e.dma_start(out=gbc_s[:], in_=gb_d.broadcast_to([128, NE])),
           inc=s_const, inc_k=16)
    P.emit("gpsimd", lambda e: e.dma_start(
        out=gw_bf[:].rearrange("p (c e) -> p c e", c=8),
        in_=gw_d.rearrange("(c p) e -> p c e", p=128)), inc=s_cast, inc_k=16)
    P.emit("gpsimd", lambda e: e.dma_start(out=eb2p_bf[:], in_=eb2p_d), inc=s_cast, inc_k=16)

    def dump(name, src, width, waitsems):
        if not debug:
            return
        s_dbg = sem("s_dbg")
        v_dbg = sem("v_dbg")
        for off in range(0, width, 2048):
            P.emit("vector", lambda e, src=src, off=off: e.tensor_copy(
                dstg[:], src[:, off:off + 2048]),
                waits=wv(*waitsems) + wv(s_dbg), inc=v_dbg)
            P.emit("sync", lambda e, name=name, off=off: e.dma_start(
                out=dbg[name][:, off:off + 2048], in_=dstg[:]),
                waits=wv(v_dbg), inc=s_dbg, inc_k=16)

    s_xf = [sem("s_xf0"), sem("s_xf1")]
    s_xb = [sem("s_xb0"), sem("s_xb1")]
    s_tab = [sem("s_tab0"), sem("s_tab1")]
    s_statw = [sem("s_statw0"), sem("s_statw1")]; s_bc = [sem("s_bc0"), sem("s_bc1")]
    p_st = sem("p_st"); p_kv = sem("p_kv"); p_ksw = sem("p_ksw"); p_vt = sem("p_vt")
    a_xsq = sem("a_xsq"); v_var = sem("v_var"); a_sq = sem("a_sq"); v_stats = sem("v_stats")
    v_xn = sem("v_xn"); a_k = sem("a_k"); v_v = sem("v_v"); v_k = sem("v_k"); v_vd = sem("v_vd")
    v_mrow = sem("v_mrow")

    with contextlib.ExitStack() as p1:
        T1 = lambda *a, **k: p1.enter_context(nc.sbuf_tensor(*a, **k))
        PT1 = lambda *a, **k: p1.enter_context(nc.psum_tensor(*a, **k))
        wkv_bf = T1("wkv_bf", [128, 8 * 128], BF16)
        P.emit("gpsimd", lambda e: e.dma_start(out=wkv_bf[:], in_=wkv_d),
               inc=s_cast, inc_k=16)
        xf = [T1(f"xf{i}", [128, 8 * SL], F32) for i in range(2)]
        xb = [T1(f"xb{i}", [128, 8 * SL], BF16) for i in range(2)]
        xsq = T1("xsq", [128, 8 * SL], BF16)
        c2b = [T1(f"c2b{i}", [128, SL], BF16) for i in range(2)]
        s2b = [T1(f"s2b{i}", [128, SL], BF16) for i in range(2)]
        xn_tmp = [T1(f"xn_tmp{i}", [128, 8 * SL], BF16) for i in range(2)]
        m_row = [T1(f"m_row{i}", [1, SL], F32) for i in range(2)]
        msq_row = [T1(f"msq_row{i}", [1, SL], F32) for i in range(2)]
        var_row = [T1(f"var_row{i}", [1, SL], F32) for i in range(2)]
        sq_row = [T1(f"sq_row{i}", [1, SL], F32) for i in range(2)]
        r_row = [T1(f"r_row{i}", [1, SL], F32) for i in range(2)]
        nmr_row = [T1(f"nmr_row{i}", [1, SL], F32) for i in range(2)]
        mbc = [T1(f"mbc{i}", [128, SL], F32) for i in range(2)]
        rbc = [T1(f"rbc{i}", [128, SL], F32) for i in range(2)]
        kbf = T1("kbf", [64, SL], BF16)
        kswb = T1("kswb", [64, SL], BF16)
        kt1 = T1("kt1", [64, SL], BF16)
        kt2 = T1("kt2", [64, SL], BF16)
        vbf = T1("vbf", [64, SL], BF16)
        s1p = [PT1(f"s1p{i}", [64, SL], F32) for i in range(2)]
        s2p = [PT1(f"s2p{i}", [64, SL], F32) for i in range(2)]
        kvp = [PT1(f"kvp{i}", [128, SL], F32) for i in range(2)]
        kswp = PT1("kswp", [64, SL], F32)
        vtp = PT1("vtp", [128, 128], BF16)

        for s in range(NSL):
            par = s % 2
            ts = s * SL
            pslot = ts // QUAD
            own = pslot in OWN_PSLOT
            dcol = (OWN_PSLOT.index(pslot) * QUAD + ts % QUAD) if own else 0
            xd = xn_all if own else xn_tmp[par]

            def xsl(c, xd=xd, dcol=dcol):
                if xd is xn_all:
                    return xd[:, c * TOK + dcol:c * TOK + dcol + SL]
                return xd[:, c * SL:(c + 1) * SL]

            P.emit("sync", lambda e, p=par, ts=ts: e.dma_start(
                out=xf[p][:].rearrange("p (c t) -> p c t", c=8),
                in_=x_fm[:, ts:ts + SL].rearrange("(c p) t -> p c t", p=128)),
                waits=wb(v_xn, 1) if s >= 2 else [], inc=s_xf[par], inc_k=16)
            P.emit("gpsimd", lambda e, p=par, ts=ts: e.dma_start(
                out=xb[p][:].rearrange("p (c t) -> p c t", c=8),
                in_=x_fm[:, ts:ts + SL].rearrange("(c p) t -> p c t", p=128)),
                waits=wb(p_st, 1) if s >= 2 else [], inc=s_xb[par], inc_k=16)
            P.emit("sync", lambda e, p=par, ts=ts: e.dma_start(out=c2b[p][:], in_=c2t_d[:, ts:ts + SL]),
                   waits=wb(v_k, 1) if s >= 2 else [], inc=s_tab[par], inc_k=16)
            P.emit("sync", lambda e, p=par, ts=ts: e.dma_start(out=s2b[p][:], in_=s2t_d[:, ts:ts + SL]),
                   inc=s_tab[par], inc_k=16)

            P.emit("scalar", lambda e, p=par: e.activation(xsq[:], xb[p][:], AF.Square),
                   waits=wv(s_xb[par]) + wv(p_st), inc=a_xsq)
            for c in range(8):
                P.emit("tensor", lambda e, p=par, c=c: e.matmul(
                    s1p[p][:], ones64_s[:], xb[p][:, c * SL:(c + 1) * SL],
                    start=(c == 0), stop=(c == 7)),
                    waits=(wv(s_xb[par], s_const) + wb(v_mrow, 1) if c == 0 else []))
            for c in range(8):
                P.emit("tensor", lambda e, p=par, c=c: e.matmul(
                    s2p[p][:], ones64_s[:], xsq[:, c * SL:(c + 1) * SL],
                    start=(c == 0), stop=(c == 7)),
                    waits=(wv(a_xsq) + wb(v_var, 1) if c == 0 else []),
                    inc=(p_st if c == 7 else None))
            P.emit("vector", lambda e, p=par: e.tensor_scalar_mul(m_row[p][:], s1p[p][0:1, :], 1.0 / D),
                   waits=wv(p_st) + wv(s_statw[par]), inc=v_mrow)
            P.emit("vector", lambda e, p=par: e.tensor_mul(msq_row[p][:], m_row[p][:], m_row[p][:]))
            P.emit("vector", lambda e, p=par: e.scalar_tensor_tensor(
                var_row[p][:], s2p[p][0:1, :], 1.0 / D, msq_row[p][:], op0=ALU.mult, op1=ALU.subtract),
                )
            P.emit("vector", lambda e, p=par: e.tensor_scalar_add(var_row[p][:], var_row[p][:], EPS),
                inc=v_var)
            P.emit("scalar", lambda e, p=par: e.activation(sq_row[p][:], var_row[p][:], AF.Sqrt),
                   waits=wv(v_var), inc=a_sq)
            P.emit("vector", lambda e, p=par: e.reciprocal(r_row[p][:], sq_row[p][:]), waits=wv(a_sq))
            P.emit("vector", lambda e, p=par: e.scalar_tensor_tensor(
                nmr_row[p][:], m_row[p][:], -1.0, r_row[p][:], op0=ALU.mult, op1=ALU.mult), inc=v_stats)
            P.emit("sync", lambda e, p=par, ts=ts: e.dma_start(out=stats_d[0:1, ts:ts + SL], in_=m_row[p][:]),
                   waits=wv(v_stats), inc=s_statw[par], inc_k=16)
            P.emit("sync", lambda e, p=par, ts=ts: e.dma_start(out=stats_d[1:2, ts:ts + SL], in_=r_row[p][:]),
                   inc=s_statw[par], inc_k=16)
            P.emit("sync", lambda e, p=par, ts=ts: e.dma_start(out=stats_d[2:3, ts:ts + SL], in_=nmr_row[p][:]),
                   inc=s_statw[par], inc_k=16)
            P.emit("sync", lambda e, p=par, ts=ts: e.dma_start(
                out=mbc[p][:], in_=stats_d[0:1, ts:ts + SL].broadcast_to([128, SL])),
                waits=wv(s_statw[par]) + wb(v_xn, 1), inc=s_bc[par], inc_k=16)
            P.emit("sync", lambda e, p=par, ts=ts: e.dma_start(
                out=rbc[p][:], in_=stats_d[1:2, ts:ts + SL].broadcast_to([128, SL])),
                inc=s_bc[par], inc_k=16)
            for c in range(8):
                P.emit("vector", lambda e, p=par, c=c: e.tensor_sub(
                    xf[p][:, c * SL:(c + 1) * SL], xf[p][:, c * SL:(c + 1) * SL], mbc[p][:]),
                    waits=(wv(s_bc[par], s_xf[par]) + wb(p_kv, 1) if c == 0 else []))
            for c in range(8):
                P.emit("vector", lambda e, p=par, c=c, xsl=xsl: e.tensor_mul(
                    xsl(c), xf[p][:, c * SL:(c + 1) * SL], rbc[p][:]),
                    inc=(v_xn if c == 7 else None))
            for c in range(8):
                P.emit("tensor", lambda e, p=par, c=c, xsl=xsl: e.matmul(
                    kvp[p][:], wkv_bf[:, c * 128:(c + 1) * 128], xsl(c),
                    start=(c == 0), stop=(c == 7)),
                    waits=(wv(v_xn, s_cast) + wb(a_k, 1) + wb(v_v, 1) if c == 0 else []),
                    inc=(p_kv if c == 7 else None))
            P.emit("scalar", lambda e, p=par: e.activation(kbf[:], kvp[p][0:64, :], AF.Identity),
                   waits=wv(p_kv, p_ksw, v_k), inc=a_k)
            P.emit("vector", lambda e, p=par: e.tensor_copy(vbf[:], kvp[p][64:128, :]),
                   waits=wv(p_kv, p_vt), inc=v_v)
            P.emit("tensor", lambda e: e.matmul(kswp[:], pswap_s[0:64, 0:64], kbf[:],
                                                start=True, stop=True),
                   waits=wv(a_k, v_k), inc=p_ksw)
            P.emit("vector", lambda e, p=par: e.tensor_mul(kt1[:], kbf[:], c2b[p][0:64, :]),
                   waits=wv(s_tab[par], p_ksw))
            P.emit("vector", lambda e: e.tensor_copy(kswb[:], kswp[0:64, :]))
            P.emit("vector", lambda e, p=par: e.tensor_mul(kt2[:], kswb[:], s2b[p][0:64, :]))
            P.emit("vector", lambda e, ts=ts: e.tensor_add(krope[0:64, ts:ts + SL], kt1[:], kt2[:]))
            P.emit("vector", lambda e, ts=ts: e.tensor_copy(
                krope[64:128, ts:ts + SL], krope[0:64, ts:ts + SL]), inc=v_k)
            for k2 in range(2):
                P.emit("tensor", lambda e, k2=k2: e.transpose(
                    vtp[:, k2 * 64:(k2 + 1) * 64], vbf[:, k2 * 128:(k2 + 1) * 128], identb_s[0:64, 0:64]),
                    waits=(wv(v_v, s_const, v_vd) if k2 == 0 else []),
                    inc=(p_vt if k2 == 1 else None))
            for k2 in range(2):
                jt = ts // 128 + k2
                P.emit("vector", lambda e, k2=k2, jt=jt: e.tensor_copy(
                    vdup[:, jt * 128:jt * 128 + 64], vtp[:, k2 * 64:(k2 + 1) * 64]),
                    waits=(wv(p_vt) if k2 == 0 else []))
                P.emit("vector", lambda e, k2=k2, jt=jt: e.tensor_copy(
                    vdup[:, jt * 128 + 64:jt * 128 + 128], vtp[:, k2 * 64:(k2 + 1) * 64]),
                    inc=(v_vd if k2 == 1 else None))

    dump("xn", xn_all, 8 * TOK, (v_xn,))
    dump("krope", krope, S, (v_k,))
    dump("vdup", vdup, S, (v_vd,))

    p_q = sem("p_q"); a_q = sem("a_q"); p_qsw = sem("p_qsw"); v_q = sem("v_q")
    p_dg = sem("p_dg"); v_dp = sem("v_dp"); v_call = sem("v_call"); v_cb = sem("v_cb")
    s_cb = sem("s_cb"); s_qtab = sem("s_qtab")

    with contextlib.ExitStack() as p2:
        wqt_bf = p2.enter_context(nc.sbuf_tensor("wqt_bf", [128, 8 * 1024], BF16, side="right"))
        T2 = lambda *a, **k: p2.enter_context(nc.sbuf_tensor(*a, **k))
        qbf = T2("qbf", [128, 512], BF16)
        qswb = T2("qswb", [128, 512], BF16)
        qt1 = T2("qt1", [128, 512], BF16)
        qt2 = T2("qt2", [128, 512], BF16)
        dp = T2("dp", [128, 512], BF16)
        c2q = T2("c2q", [128, 512], BF16)
        s2q = T2("s2q", [128, 512], BF16)
        cbrow = T2("cbrow", [1, 1], F32)
        qp = [p2.enter_context(nc.psum_tensor(f"qp{i}", [128, 512], F32)) for i in range(2)]
        qswp = p2.enter_context(nc.psum_tensor("qswp", [128, 512], F32))
        dgA = p2.enter_context(nc.psum_tensor("dgA", [64, 512], F32))
        dgB = p2.enter_context(nc.psum_tensor("dgB", [64, 512], F32))

        s_wqt = sem("s_wqt")
        P.emit("gpsimd", lambda e: e.dma_start(out=wqt_bf[:], in_=wqt_d),
               inc=s_wqt, inc_k=16)

        for g in range(NQ):
            lq = g * QUAD
            ts = OWN_PSLOT[g] * QUAD
            P.emit("sync", lambda e, ts=ts: e.dma_start(out=c2q[:], in_=c2t_d[:, ts:ts + QUAD]),
                   waits=wv(v_q), inc=s_qtab, inc_k=16)
            P.emit("sync", lambda e, ts=ts: e.dma_start(out=s2q[:], in_=s2t_d[:, ts:ts + QUAD]),
                   inc=s_qtab, inc_k=16)
            for c in range(8):
                pp = c % 2
                for dc in range(8):
                    P.emit("tensor", lambda e, pp=pp, c=c, dc=dc, lq=lq: e.matmul(
                        qp[pp][:], wqt_bf[:, dc * 1024 + c * 128:dc * 1024 + (c + 1) * 128],
                        xn_all[:, dc * TOK + lq:dc * TOK + lq + QUAD],
                        start=(dc == 0), stop=(dc == 7)),
                        waits=(wv(s_wqt, v_xn) + wb(a_q, 1)
                               + (wv(v_mrow, v_var, v_k, v_vd, a_k, v_v)
                                  if (g == 0 and c == 0) else [])
                               if dc == 0 else []),
                        inc=(p_q if dc == 7 else None))
                P.emit("scalar", lambda e, pp=pp: e.activation(qbf[:], qp[pp][:], AF.Identity),
                       waits=wv(p_q, p_qsw, v_q), inc=a_q)
                P.emit("tensor", lambda e: e.matmul(qswp[:], pswap_s[:], qbf[:],
                                                    start=True, stop=True),
                       waits=wv(a_q, v_q), inc=p_qsw)
                P.emit("vector", lambda e: e.tensor_copy(qswb[:], qswp[:]), waits=wv(p_qsw))
                P.emit("vector", lambda e: e.tensor_mul(qt1[:], qbf[:], c2q[:]),
                       waits=wv(s_qtab))
                P.emit("vector", lambda e: e.tensor_mul(qt2[:], qswb[:], s2q[:]))
                P.emit("vector", lambda e, c=c, lq=lq: e.tensor_add(
                    qall[:, c * TOK + lq:c * TOK + lq + QUAD], qt1[:], qt2[:]), inc=v_q)
            for c in range(8):
                P.emit("vector", lambda e, c=c, lq=lq, ts=ts: e.tensor_mul(
                    dp[:], qall[:, c * TOK + lq:c * TOK + lq + QUAD], krope[:, ts:ts + QUAD]),
                    waits=wv(v_q, v_k) + wv(p_dg), inc=v_dp)
                P.emit("tensor", lambda e: e.matmul(dgA[:], maskA_s[:], dp[:], start=True, stop=True),
                       waits=wv(v_dp, v_call))
                P.emit("tensor", lambda e: e.matmul(dgB[:], maskB_s[:], dp[:], start=True, stop=True),
                       inc=p_dg)
                idx = g * 16 + 2 * c
                P.emit("vector", lambda e, idx=idx: e.reduce_max(
                    call_s[0:1, idx:idx + 1], dgA[0:1, :], axis=AX.X), waits=wv(p_dg), inc=v_call)
                P.emit("vector", lambda e, idx=idx: e.reduce_max(
                    call_s[0:1, idx + 1:idx + 2], dgB[0:1, :], axis=AX.X), inc=v_call)
        P.emit("vector", lambda e: e.reduce_max(cbrow[:], call_s[:], axis=AX.X),
               waits=wv(v_call), inc=v_cb)
        P.emit("vector", lambda e: e.tensor_scalar_mul(cbrow[:], cbrow[:], -0.125),
               waits=wv(v_cb), inc=v_cb)
        P.emit("sync", lambda e: e.dma_start(out=cb_d, in_=cbrow[:]),
               waits=wv(v_cb), inc=s_cb, inc_k=16)
        P.emit("sync", lambda e: e.dma_start(out=cbias[:], in_=cb_d.broadcast_to([128, 1])),
               waits=wv(s_cb), inc=s_cb, inc_k=16)

    a_gb = sem("a_gb")
    P.emit("scalar", lambda e: e.activation(gbias[:], glog_s[:], AF.Identity,
                                            bias=cbias[:, 0:1]),
           waits=wv(s_cb, s_const), inc=a_gb)

    dump("qall", qall, 8 * TOK, (v_q,))

    p_qk = sem("p_qk"); a_exp = sem("a_exp"); v_msk = sem("v_msk"); p_av = sem("p_av")
    v_oev = sem("v_oev"); v_rec = sem("v_rec"); s_sumw = sem("s_sumw"); s_rbc = sem("s_rbc")
    v_div = sem("v_div")

    with contextlib.ExitStack() as p3:
        T3 = lambda *a, **k: p3.enter_context(nc.sbuf_tensor(*a, **k))
        Ebuf = [T3(f"Ebuf{i}", [128, 1024], BF16) for i in range(3)]
        ostg = [T3(f"ostg{i}", [128, 512], F32) for i in range(2)]
        ssb = T3("ssb", [128, 512], F32)
        recA = T3("recA", [1, 512], F32)
        recB = T3("recB", [1, 512], F32)
        rbc2 = [T3(f"rbc2{i}", [128, 512], F32) for i in range(2)]
        scp = [p3.enter_context(nc.psum_tensor(f"scp{i}", [128, 1024], F32)) for i in range(3)]
        op_ = p3.enter_context(nc.psum_tensor("op_", [128, 512], F32))
        sump = p3.enter_context(nc.psum_tensor("sump", [128, 512], F32))

        steps = [(slot, hp, j) for slot in range(NQ) for hp in range(8)
                 for j in range(TEXT_KT[slot])]
        NST = len(steps)
        dep_tgt = {}

        def emit_epilogue(slot, hp):
            ow = hp % 2
            P.emit("vector", lambda e: e.tensor_copy(ssb[:], sump[:]),
                   waits=wv(p_av, s_sumw))
            P.emit("vector", lambda e, ow=ow: e.tensor_copy(ostg[ow][:], op_[:]),
                   waits=wb(v_div, 1), inc=v_oev)
            P.emit("vector", lambda e: e.reciprocal(recA[:], ssb[0:1, :]))
            P.emit("vector", lambda e: e.reciprocal(recB[:], ssb[64:65, :]), inc=v_rec)
            r0 = slot * 16 + hp * 2
            P.emit("sync", lambda e, r0=r0: e.dma_start(out=sums_d[r0:r0 + 1, :], in_=recA[:]),
                   waits=wv(v_rec), inc=s_sumw, inc_k=16)
            P.emit("sync", lambda e, r0=r0: e.dma_start(out=sums_d[r0 + 1:r0 + 2, :], in_=recB[:]),
                   inc=s_sumw, inc_k=16)
            P.emit("sync", lambda e, r0=r0: e.dma_start(
                out=rbc2[0][:], in_=sums_d[r0:r0 + 1, :].broadcast_to([128, 512])),
                waits=wv(s_sumw) + wv(v_div), inc=s_rbc, inc_k=16)
            P.emit("sync", lambda e, r0=r0: e.dma_start(
                out=rbc2[1][:], in_=sums_d[r0 + 1:r0 + 2, :].broadcast_to([128, 512])),
                inc=s_rbc, inc_k=16)
            lq = slot * QUAD
            P.emit("vector", lambda e, ow=ow, hp=hp, lq=lq: e.tensor_mul(
                qall[0:64, hp * TOK + lq:hp * TOK + lq + QUAD], ostg[ow][0:64, :],
                rbc2[0][0:64, :]), waits=wv(s_rbc))
            P.emit("vector", lambda e, ow=ow, hp=hp, lq=lq: e.tensor_mul(
                qall[64:128, hp * TOK + lq:hp * TOK + lq + QUAD], ostg[ow][64:128, :],
                rbc2[1][64:128, :]), inc=v_div)

        for it in range(NST + 2):
            if it < NST:
                slot, hp, j = steps[it]
                TKT = TEXT_KT[slot]
                W0 = TKT - 16
                lq = slot * QUAD
                buf = it % 3
                w = j - W0
                bcol = slot * 12 + w if (0 <= w < 12) else 48
                qk_waits = wb(a_exp, 2)
                if it == 0:
                    qk_waits = qk_waits + wv(s_cb, v_q, v_k, v_vd, v_call, a_q)
                P.emit("tensor", lambda e, j=j, hp=hp, lq=lq, buf=buf: e.matmul(
                    scp[buf][:, 0:512], krope[0:64, j * 128:(j + 1) * 128],
                    qall[0:64, hp * TOK + lq:hp * TOK + lq + QUAD],
                    start=True, stop=True, tile_position=(0, 0)), waits=qk_waits)
                P.emit("tensor", lambda e, j=j, hp=hp, lq=lq, buf=buf: e.matmul(
                    scp[buf][:, 512:1024], krope[64:128, j * 128:(j + 1) * 128],
                    qall[64:128, hp * TOK + lq:hp * TOK + lq + QUAD],
                    start=True, stop=True, tile_position=(64, 0)), inc=p_qk)
                exp_waits = wv(p_qk) + wv(p_av)
                if it == 0:
                    exp_waits = exp_waits + wv(a_gb)
                P.emit("scalar", lambda e, buf=buf, bcol=bcol: e.activation(
                    Ebuf[buf][:], scp[buf][:], AF.Exp, bias=gbias[:, bcol:bcol + 1],
                    scale=0.125), waits=exp_waits, inc=a_exp)
                if w >= 12:
                    tsl = tri_s[:, (w - 12) * 512:(w - 11) * 512]
                    P.emit("vector", lambda e, buf=buf, tsl=tsl: e.tensor_mul(
                        Ebuf[buf][:, 0:512], Ebuf[buf][:, 0:512], tsl),
                        waits=wv(a_exp) + (wv(s_const) if v_msk.n == 0 else []))
                    P.emit("vector", lambda e, buf=buf, tsl=tsl: e.tensor_mul(
                        Ebuf[buf][:, 512:1024], Ebuf[buf][:, 512:1024], tsl),
                        inc=v_msk)
                    dep_tgt[it] = (v_msk, v_msk.n)
                else:
                    dep_tgt[it] = (a_exp, a_exp.n)
            a = it - 2
            if a >= 0:
                slot, hp, j = steps[a]
                TKT = TEXT_KT[slot]
                buf = a % 3
                first = (j == 0)
                last = (j == TKT - 1)
                dsem, dval = dep_tgt.pop(a)
                dep = [(dsem, dval)]
                P.emit("tensor", lambda e, j=j, buf=buf, first=first, last=last: e.matmul(
                    op_[0:64, :], vdup[:, j * 128:j * 128 + 64], Ebuf[buf][:, 0:512],
                    start=first, stop=last, tile_position=(0, 0),
                    skip_group_check=True),
                    waits=dep + (wv(v_oev) if first else []))
                P.emit("tensor", lambda e, j=j, buf=buf, first=first, last=last: e.matmul(
                    op_[64:128, :], vdup[:, j * 128 + 64:j * 128 + 128], Ebuf[buf][:, 512:1024],
                    start=first, stop=last, tile_position=(0, 64),
                    skip_group_check=True))
                P.emit("tensor", lambda e, buf=buf, first=first, last=last: e.matmul(
                    sump[0:64, :], ones64_s[:], Ebuf[buf][:, 0:512],
                    start=first, stop=last, tile_position=(0, 0),
                    skip_group_check=True))
                P.emit("tensor", lambda e, buf=buf, first=first, last=last: e.matmul(
                    sump[64:128, :], ones64_s[:], Ebuf[buf][:, 512:1024],
                    start=first, stop=last, tile_position=(0, 64),
                    skip_group_check=True), inc=p_av)
                if last:
                    emit_epilogue(slot, hp)

    dump("attn", qall, 8 * TOK, (v_div,))

    p_wo = sem("p_wo"); v_wo = sem("v_wo")
    s_wo = [sem("s_wo0"), sem("s_wo1")]
    s_wot = sem("s_wot")
    with contextlib.ExitStack() as pwo:
        wot_bf = pwo.enter_context(nc.sbuf_tensor("wot_bf", [128, 8 * 1024], BF16, side="right"))
        wstg = [pwo.enter_context(nc.sbuf_tensor(f"wstg{i}", [128, 512], F32)) for i in range(2)]
        wop = [pwo.enter_context(nc.psum_tensor(f"wop{i}", [128, 512], F32)) for i in range(2)]
        P.emit("gpsimd", lambda e: e.dma_start(out=wot_bf[:], in_=wot_d),
               waits=wv(p_q), inc=s_wot, inc_k=16)
        it = 0
        for oc in range(8):
            for tsl in range(4):
                pp = it % 2
                lq = tsl * 512
                for ac in range(8):
                    P.emit("tensor", lambda e, pp=pp, oc=oc, ac=ac, lq=lq: e.matmul(
                        wop[pp][:], wot_bf[:, ac * 1024 + oc * 128:ac * 1024 + (oc + 1) * 128],
                        qall[:, ac * TOK + lq:ac * TOK + lq + 512],
                        start=(ac == 0), stop=(ac == 7)),
                        waits=(wv(v_div, s_wot) + wb(v_wo, 1) if ac == 0 else []),
                        inc=(p_wo if ac == 7 else None))
                P.emit("vector", lambda e, pp=pp: e.tensor_copy(wstg[pp][:], wop[pp][:]),
                       waits=wv(p_wo, s_wo[pp]), inc=v_wo)
                P.emit("sync", lambda e, pp=pp, oc=oc, lq=lq: e.dma_start(
                    out=attno_d[oc, :, lq:lq + 512], in_=wstg[pp][:]),
                    waits=wv(v_wo), inc=s_wo[pp], inc_k=16)
                it += 1

    attn_ctx.close()

    p_lg = sem("p_lg"); v_rt = sem("v_rt"); a_re = sem("a_re"); p_wt = sem("p_wt")
    v_wr = sem("v_wr"); s_wr = sem("s_wr"); g_v = sem("g_v"); v_init = sem("v_init")
    tp_v = sem("tp_v"); tp_s = sem("tp_s")
    with contextlib.ExitStack() as p4:
        T4 = lambda *a, **k: p4.enter_context(nc.sbuf_tensor(*a, **k))
        lgs = T4("lgs", [128, NE], F32)
        nmax = T4("nmax", [128, 1], F32)
        el = T4("el", [128, NE], F32)
        se = T4("se", [128, 1], F32)
        rec = T4("rec", [128, 1], F32)
        probs_all = T4("probs_all", [128, 128], F32)
        t4a = T4("t4a", [128, 64], F32)
        t2a = T4("t2a", [128, 32], F32)
        m1b = T4("m1b", [128, 128], F32)
        m2b = T4("m2b", [128, 128], F32)
        mk1a = T4("mk1a", [128, 128], F32)
        p2va = T4("p2va", [128, 128], F32)
        mk2a = T4("mk2a", [128, 128], F32)
        wsela = T4("wsela", [128, 128], F32)
        s12a = T4("s12a", [128, 128], F32)
        rdena = T4("rdena", [128, 128], F32)
        wta = T4("wta", [128, 128], F32)
        lgp = p4.enter_context(nc.psum_tensor("lgp", [128, NE], F32))
        wtp = p4.enter_context(nc.psum_tensor("wtp", [32, 128], F32))

        P.emit("vector", lambda e: e.memset(wroute_bf[:], 0.0), inc=v_init)
        P.emit("gpsimd", lambda e: e.dma_start(out=wroute_bf[8:9, :], in_=onesrow_d),
               waits=wv(v_init), inc=sem("s_ones"), inc_k=16)

        for tt in range(16):
            lq = tt * 128
            for dc in range(8):
                P.emit("tensor", lambda e, dc=dc, lq=lq: e.matmul(
                    lgp[:], xn_all[:, dc * TOK + lq:dc * TOK + lq + 128],
                    gw_bf[:, dc * 8:(dc + 1) * 8],
                    start=(dc == 0), stop=(dc == 7)),
                    waits=(wv(s_cast, v_rt) + (wv(v_wo, v_xn) if tt == 0 else [])
                           if dc == 0 else []),
                    inc=(p_lg if dc == 7 else None))
            P.emit("vector", lambda e: e.tensor_add(lgs[:], lgp[:], gbc_s[:]),
                   waits=wv(p_lg, s_const) + (wv(sem("s_dbgr")) if (debug and tt > 0) else []))
            P.emit("vector", lambda e: e.reduce_max(nmax[:], lgs[:], axis=AX.X, negate=True),
                   inc=g_v)
            P.emit("scalar", lambda e: e.activation(el[:], lgs[:], AF.Exp, bias=nmax[:, 0:1],
                                                    accum_out=se[:, 0:1]),
                   waits=wv(g_v), inc=a_re)
            P.emit("vector", lambda e: e.reciprocal(rec[:], se[:]), waits=wv(a_re), inc=g_v)
            P.emit("vector", lambda e, tt=tt: e.tensor_single_scalar(
                probs_all[:, tt * NE:(tt + 1) * NE], el[:], rec[:, 0:1],
                op=ALU.mult), waits=wv(g_v), inc=v_rt)
            if debug:
                s_dbgr = sem("s_dbgr")
                for slot_i, srcb in ((0, lgs),):
                    P.emit("sync", lambda e, tt=tt, slot_i=slot_i, srcb=srcb: e.dma_start(
                        out=dbg["router"][:, tt * 16 + slot_i * 8:tt * 16 + slot_i * 8 + 8],
                        in_=srcb[:, 0:8]),
                        waits=wv(v_rt), inc=s_dbgr, inc_k=16)

        rr8 = lambda t: t[:].rearrange("p (a e) -> p a e", e=8)
        rr4 = lambda t: t[:].rearrange("p (a e) -> p a e", e=4)
        rr2 = lambda t: t[:].rearrange("p (a e) -> p a e", e=2)
        step = {"i": 0}

        def chain(fn):
            i = step["i"]; step["i"] += 1
            P.emit("vector", fn,
                   waits=wv(tp_s) + (wv(v_rt) if i == 0 else []), inc=tp_v)
            P.emit("scalar", None, waits=wv(tp_v), inc=tp_s)

        def top_chain(src, dstb):
            chain(lambda e, src=src: e.tensor_tensor(
                rr4(t4a), rr8(src)[:, :, 0:4], rr8(src)[:, :, 4:8], op=ALU.max))
            chain(lambda e: e.tensor_tensor(
                rr2(t2a), rr4(t4a)[:, :, 0:2], rr4(t4a)[:, :, 2:4], op=ALU.max))
            chain(lambda e, dstb=dstb: e.tensor_tensor(
                rr8(dstb)[:, :, 0:1], rr2(t2a)[:, :, 0:1], rr2(t2a)[:, :, 1:2], op=ALU.max))
            chain(lambda e, dstb=dstb: e.tensor_copy(rr8(dstb)[:, :, 1:2], rr8(dstb)[:, :, 0:1]))
            chain(lambda e, dstb=dstb: e.tensor_copy(rr8(dstb)[:, :, 2:4], rr8(dstb)[:, :, 0:2]))
            chain(lambda e, dstb=dstb: e.tensor_copy(rr8(dstb)[:, :, 4:8], rr8(dstb)[:, :, 0:4]))

        top_chain(probs_all, m1b)
        chain(lambda e: e.tensor_tensor(mk1a[:], probs_all[:], m1b[:], op=ALU.is_ge))
        chain(lambda e: e.scalar_tensor_tensor(
            p2va[:], mk1a[:], -1e30, probs_all[:], op0=ALU.mult, op1=ALU.add))
        top_chain(p2va, m2b)
        chain(lambda e: e.tensor_tensor(mk2a[:], probs_all[:], m2b[:], op=ALU.is_ge))
        chain(lambda e: e.tensor_tensor(wsela[:], probs_all[:], mk2a[:], op=ALU.mult))
        chain(lambda e: e.tensor_add(s12a[:], m1b[:], m2b[:]))
        chain(lambda e: e.tensor_scalar_add(s12a[:], s12a[:], 1e-9))
        chain(lambda e: e.reciprocal(rdena[:], s12a[:]))
        chain(lambda e: e.tensor_tensor(wta[:], wsela[:], rdena[:], op=ALU.mult))

        for tt in range(16):
            lq = tt * 128
            P.emit("tensor", lambda e, tt=tt: e.transpose(
                wtp[0:8, :], wta[:, tt * 8:(tt + 1) * 8], identf_s[:]),
                waits=(wv(tp_v, tp_s, s_const) if tt == 0 else []) + wv(v_wr),
                inc=p_wt)
            P.emit("vector", lambda e, lq=lq: e.tensor_copy(wroute_bf[0:8, lq:lq + 128],
                                                            wtp[0:8, :]),
                   waits=wv(p_wt, v_init) + (wv(sems["s_ones"]) if lq == 0 else []),
                   inc=v_wr)
        P.emit("gpsimd", lambda e: e.dma_start(out=wroute_dd, in_=wroute_bf[0:8, :]),
               waits=wv(v_wr), inc=s_wr, inc_k=16)
        if debug:
            for off, srcb in ((256, probs_all), (384, wta), (512, m1b), (640, m2b)):
                P.emit("sync", lambda e, off=off, srcb=srcb: e.dma_start(
                    out=dbg["router"][:, off:off + 128], in_=srcb[:]),
                    waits=wv(tp_v, tp_s), inc=sem("s_dbgr"), inc_k=16)
            P.emit("gpsimd", lambda e: e.dma_start(out=dbg["wroute"], in_=wroute_bf[0:8, :]),
                   waits=wv(v_wr), inc=sem("s_dbgw"), inc_k=16)

    s_acc = sem("s_acc")
    s_w1 = [sem("s_w1_0"), sem("s_w1_1")]
    s_w2 = [sem("s_w2_0"), sem("s_w2_1")]
    s_bcf = [sem("s_bcf0"), sem("s_bcf1")]
    p_e1 = sem("p_e1"); a_gel = sem("a_gel"); p_e2 = sem("p_e2"); p_w2c = sem("p_w2c")
    v_acc = sem("v_acc"); p_tr = sem("p_tr")
    s_xt = [sem("s_xt0"), sem("s_xt1")]
    s_stt = [sem("s_stt0"), sem("s_stt1")]
    a_xn2 = sem("a_xn2"); v_out = sem("v_out")
    s_out = [sem("s_out0"), sem("s_out1")]

    with contextlib.ExitStack() as p5:
        T5 = lambda *a, **k: p5.enter_context(nc.sbuf_tensor(*a, **k))
        Hbuf = T5("Hbuf", [128, 32 * 1024], BF16)
        acc = T5("acc", [128, 8 * 1024], F32)
        tscr = T5("tscr", [128, 1024], F32)
        out_sb = [T5(f"out_sb{i}", [128, 1024], F32) for i in range(2)]
        xt_b = [T5(f"xt_b{i}", [128, 1024], F32) for i in range(2)]
        xnt = T5("xnt", [128, 1024], F32)
        rst = [T5(f"rst{i}", [128, 1], F32) for i in range(2)]
        nmrst = [T5(f"nmrst{i}", [128, 1], F32) for i in range(2)]
        w1b = [p5.enter_context(nc.sbuf_tensor(f"w1b{i}", [128, 8 * 512], BF16, side="right"))
               for i in range(2)]
        w2b = [p5.enter_context(nc.sbuf_tensor(f"w2b{i}", [128, 1024], BF16, side="right"))
               for i in range(2)]
        bcf = p5.enter_context(nc.sbuf_tensor("bcf", [128, 1024], F32, side="right"))
        PS = [p5.enter_context(nc.psum_tensor(f"PS{i}", [128, 1024], F32)) for i in range(4)]

        for g2 in range(2):
            lq0 = g2 * 1024
            for oc in range(8):
                P.emit("sync", lambda e, oc=oc, lq0=lq0: e.dma_start(
                    out=acc[:, oc * 1024:(oc + 1) * 1024], in_=attno_d[oc, :, lq0:lq0 + 1024]),
                    waits=(wv(s_wo[0], s_wo[1]) + wv(p_tr) if oc == 0 else []), inc=s_acc, inc_k=16)
            for mi in range(9):
                is_sh = (mi == 0)
                eidx = mi - 1
                for ig in range(8):
                    w1w = wb(p_e1, 4)
                    if g2 == 0 and mi == 0 and ig == 0:
                        w1w = w1w + wv(v_div)
                    P.emit("gpsimd", lambda e, ig=ig, mi=mi: e.dma_start(
                        out=w1b[ig % 2][:], in_=w1l_d[mi, ig]),
                        waits=w1w, inc=s_w1[ig % 2], inc_k=16)
                    for c2 in range(4):
                        ic = ig * 4 + c2
                        pp = ic % 2
                        for dc in range(8):
                            for th in range(2):
                                e1w = []
                                if dc == 0 and th == 0:
                                    e1w = wv(s_w1[ig % 2]) + wb(a_gel, 1)
                                    if ic == 0:
                                        e1w += wb(v_acc, 3) + wv(v_out, p_e2, v_wr)
                                    elif ic == 1:
                                        e1w += wb(v_acc, 2)
                                P.emit("tensor", lambda e, ig=ig, c2=c2, pp=pp, dc=dc, th=th, lq0=lq0: e.matmul(
                                    PS[pp][:, th * 512:(th + 1) * 512],
                                    w1b[ig % 2][:, dc * 512 + c2 * 128:dc * 512 + (c2 + 1) * 128],
                                    xn_all[:, dc * TOK + lq0 + th * 512:dc * TOK + lq0 + (th + 1) * 512],
                                    start=(dc == 0), stop=(dc == 7)),
                                    waits=e1w,
                                    inc=(p_e1 if (dc == 7 and th == 1) else None))
                        bsl = (b1l_s[:, ic:ic + 1] if is_sh
                               else eb1l_s[:, eidx * 32 + ic:eidx * 32 + ic + 1])
                        gw_ = wv(p_e1, s_const)
                        if ic == 0:
                            gw_ = gw_ + wv(p_w2c)
                        if g2 == 0 and mi == 0 and ic == 0:
                            gw_ = gw_ + wv(s_wr)
                            if debug:
                                gw_ = gw_ + wv(sem("s_dbgr"))
                        P.emit("scalar", lambda e, ic=ic, pp=pp, b=bsl: e.activation(
                            Hbuf[:, ic * 1024:(ic + 1) * 1024], PS[pp][:], AF.Gelu_apprx_tanh,
                            bias=b), waits=gw_, inc=a_gel)
                if not is_sh:
                    P.emit("sync", lambda e, eidx=eidx, lq0=lq0: e.dma_start(
                        out=bcf[:],
                        in_=wroute_dd[eidx:eidx + 1, lq0:lq0 + 1024].broadcast_to([128, 1024])),
                        waits=wv(s_wr, v_acc), inc=s_bcf[0], inc_k=16)
                for dh in range(2):
                    for blk in range(16):
                        w2w = wb(p_w2c, 2)
                        if g2 == 0 and mi == 0 and dh == 0 and blk == 0:
                            w2w = w2w + wv(v_div)
                        P.emit("gpsimd", lambda e, blk=blk, dh=dh, mi=mi: e.dma_start(
                            out=w2b[blk % 2][:], in_=w2l_d[mi, dh, blk]),
                            waits=w2w, inc=s_w2[blk % 2], inc_k=16)
                        for kk in range(2):
                            ic = blk * 2 + kk
                            for dc2 in range(4):
                                for th in range(2):
                                    e2w = []
                                    if ic == 0 and dc2 == 0 and th == 0:
                                        e2w = wv(s_w2[blk % 2], v_acc) + (
                                            wv(a_gel) if dh == 0 else [])
                                    elif kk == 0 and dc2 == 0 and th == 0:
                                        e2w = wv(s_w2[blk % 2])
                                    P.emit("tensor", lambda e, ic=ic, kk=kk, blk=blk, dc2=dc2, th=th: e.matmul(
                                        PS[dc2][:, th * 512:(th + 1) * 512],
                                        w2b[blk % 2][:, kk * 512 + dc2 * 128:kk * 512 + (dc2 + 1) * 128],
                                        Hbuf[:, ic * 1024 + th * 512:ic * 1024 + (th + 1) * 512],
                                        start=(ic == 0), stop=(ic == 31)),
                                        waits=e2w,
                                        inc=(p_w2c if (dc2 == 3 and th == 1) else None))
                    for dc2 in range(4):
                        dc = dh * 4 + dc2
                        if is_sh:
                            P.emit("vector", lambda e, dc2=dc2, dc=dc: e.tensor_add(
                                acc[:, dc * 1024:(dc + 1) * 1024],
                                acc[:, dc * 1024:(dc + 1) * 1024], PS[dc2][:]),
                                waits=(wv(p_w2c, s_acc) if dc2 == 0 else []),
                                inc=v_acc)
                        else:
                            P.emit("vector", lambda e, dc2=dc2: e.tensor_mul(
                                tscr[:], PS[dc2][:], bcf[:]),
                                waits=(wv(p_w2c, s_bcf[0]) if dc2 == 0 else []))
                            P.emit("vector", lambda e, dc2=dc2, dc=dc: e.tensor_add(
                                acc[:, dc * 1024:(dc + 1) * 1024],
                                acc[:, dc * 1024:(dc + 1) * 1024], tscr[:]),
                                inc=v_acc)
            for half in range(2):
                for dcc in range(4):
                    dc = half * 4 + dcc
                    for th in range(2):
                        bw = []
                        if dcc == 0 and th == 0:
                            bw = wv(s_cast, v_acc, v_wr)
                        P.emit("tensor", lambda e, dc=dc, dcc=dcc, th=th, lq0=lq0: e.matmul(
                            PS[dcc][:, th * 512:(th + 1) * 512],
                            eb2p_bf[:, dc * 128:(dc + 1) * 128],
                            wroute_bf[:, lq0 + th * 512:lq0 + (th + 1) * 512],
                            start=True, stop=True),
                            waits=bw, inc=(p_e2 if (dcc == 3 and th == 1) else None))
                for dcc in range(4):
                    dc = half * 4 + dcc
                    P.emit("vector", lambda e, dcc=dcc, dc=dc: e.tensor_add(
                        acc[:, dc * 1024:(dc + 1) * 1024],
                        acc[:, dc * 1024:(dc + 1) * 1024], PS[dcc][:]),
                        waits=(wv(p_e2) if dcc == 0 else []), inc=v_acc)
            if debug and g2 == 0:
                P.emit("sync", lambda e: e.dma_start(out=dbg["acc0"], in_=acc[:]),
                       waits=wv(v_acc), inc=sem("s_dbg"), inc_k=16)
            for tt in range(8):
                pp = tt % 2
                l0 = lq0 + tt * 128
                gl = OWN_PSLOT[l0 // QUAD] * QUAD + (l0 % QUAD)
                P.emit("sync", lambda e, pp=pp, l0=l0: e.dma_start(
                    out=xt_b[pp][:], in_=x_tm[l0:l0 + 128, :]),
                    waits=wb(v_out, 1), inc=s_xt[pp], inc_k=16)
                P.emit("sync", lambda e, pp=pp, gl=gl: e.dma_start(
                    out=rst[pp][:], in_=stats_d[1:2, gl:gl + 128].rearrange("o t -> t o")),
                    inc=s_stt[pp], inc_k=16)
                P.emit("sync", lambda e, pp=pp, gl=gl: e.dma_start(
                    out=nmrst[pp][:], in_=stats_d[2:3, gl:gl + 128].rearrange("o t -> t o")),
                    inc=s_stt[pp], inc_k=16)
                for oc in range(8):
                    P.emit("tensor", lambda e, oc=oc, tt=tt: e.transpose(
                        PS[tt % 2][:, oc * 128:(oc + 1) * 128],
                        acc[:, oc * 1024 + tt * 128:oc * 1024 + (tt + 1) * 128], identf_s[:]),
                        waits=(wv(v_acc) + wb(v_out, 1) if oc == 0 else []),
                        inc=(p_tr if oc == 7 else None))
                P.emit("scalar", lambda e, pp=pp: e.activation(
                    xnt[:], xt_b[pp][:], AF.Identity, bias=nmrst[pp][:, 0:1],
                    scale=rst[pp][:, 0:1]),
                    waits=wv(s_xt[pp], s_stt[pp], v_out), inc=a_xn2)
                P.emit("vector", lambda e, pp=pp, tt=tt: e.tensor_add(
                    out_sb[pp][:], PS[tt % 2][:], xt_b[pp][:]),
                       waits=wv(p_tr, s_xt[pp], s_out[pp]))
                P.emit("vector", lambda e, pp=pp: e.tensor_add(out_sb[pp][:], out_sb[pp][:], xnt[:]),
                       waits=wv(a_xn2), inc=v_out)
                P.emit("gpsimd", lambda e, pp=pp, l0=l0: e.dma_start(
                    out=y[l0:l0 + 128, :], in_=out_sb[pp][:]),
                    waits=wv(v_out), inc=s_out[pp], inc_k=16)

    fin = wv(s_out[0], s_out[1])
    if debug:
        fin = fin + wv(sem("s_dbg")) + wv(sem("s_dbgw")) + wv(sem("s_dbgr"))
    P.emit("gpsimd", None, waits=fin)

    with nc.Block() as block:
        for eng in Plan.ENGS:
            ops = P.ops[eng]

            def body(e, ops=ops):
                for fn in ops:
                    fn(e)
            getattr(block, eng)(body)
    ctx.close()
    return nc



def _pi_perm(j):
    own = [j, 7 - j, 8 + j, 15 - j]
    pi = []
    for i, o in enumerate(own):
        blk = [q for q in range(4 * i, 4 * i + 4) if q != o]
        pi.extend(blk + [o])
    return pi, own


def _host_constants(pi, own):
    bf = ml_dtypes.bfloat16
    pos = np.concatenate([np.arange(q * QUAD, (q + 1) * QUAD) for q in pi]).astype(np.float64)
    freqs = 1.0 / (1000.0 ** (np.arange(0, HD, 2, dtype=np.float64) / HD))
    pair = ((np.arange(128) % 64) // 2)
    ang = pos[None, :] * freqs[pair][:, None]
    c2 = np.cos(ang).astype(np.float32)
    s2 = np.sin(ang).astype(np.float32)
    sign = np.where(np.arange(128) % 2 == 0, -1.0, 1.0).astype(np.float32)
    s2 = s2 * sign[:, None]

    psw = np.zeros((128, 128), np.float32)
    for a in range(128):
        psw[a, a ^ 1] = 1.0
    identb = np.eye(128, dtype=np.float32)
    ones64 = np.ones((128, 64), np.float32)
    maskA = np.zeros((128, 64), np.float32); maskA[0:64] = 1.0
    maskB = np.zeros((128, 64), np.float32); maskB[64:128] = 1.0

    glog = np.zeros((128, 49), np.float32)
    for slot in range(4):
        for w in range(12):
            kq = pi[4 * slot + w // 4]
            glog[:, slot * 12 + w] = 0.0 if kq < own[slot] else -1e5
    tri = np.zeros((128, 4 * 512), np.float32)
    r = np.arange(128)
    q = np.arange(512)
    for t in range(4):
        m = ((t * 128 + r)[:, None] <= q[None, :]).astype(np.float32)
        tri[:, t * 512:(t + 1) * 512] = m
    return dict(c2t=c2.astype(bf), s2t=s2.astype(bf), pswap=psw.astype(bf),
                identb=identb.astype(bf), identf=identb,
                ones64=ones64.astype(bf), maskA=maskA.astype(bf),
                maskB=maskB.astype(bf), tri=tri.astype(bf), glog=glog,
                onesrow=np.ones((1, 2048), np.float32).astype(bf))


_NC_CACHE = {}


def kernel(L0_attn_x, L0_attn_wq, L0_attn_wk, L0_attn_wv, L0_attn_wo,
           L0_ffn_gate_w, L0_ffn_gate_b, L0_ffn_sh_w1, L0_ffn_sh_b1,
           L0_ffn_sh_w2, L0_ffn_sh_b2, L0_ffn_ew1, L0_ffn_eb1,
           L0_ffn_ew2, L0_ffn_eb2):
    f32 = np.float32
    x = np.asarray(L0_attn_x, f32)
    wq = np.asarray(L0_attn_wq, f32)
    wk = np.asarray(L0_attn_wk, f32)
    wv = np.asarray(L0_attn_wv, f32)
    wo = np.asarray(L0_attn_wo, f32)
    gate_w = np.asarray(L0_ffn_gate_w, f32)
    gate_b = np.asarray(L0_ffn_gate_b, f32).reshape(1, NE)
    sh_w1 = np.asarray(L0_ffn_sh_w1, f32)
    sh_b1 = np.asarray(L0_ffn_sh_b1, f32)
    sh_w2 = np.asarray(L0_ffn_sh_w2, f32)
    sh_b2 = np.asarray(L0_ffn_sh_b2, f32)
    ew1 = np.ascontiguousarray(np.asarray(L0_ffn_ew1, f32))
    eb1 = np.asarray(L0_ffn_eb1, f32)
    ew2 = np.ascontiguousarray(np.asarray(L0_ffn_ew2, f32))
    eb2 = np.asarray(L0_ffn_eb2, f32)

    wqt = np.ascontiguousarray(
        wq.T.reshape(8, 128, D).transpose(1, 0, 2).reshape(128, 8 * D))
    wkv0 = np.concatenate([wk, wv], axis=1)
    wkv = np.ascontiguousarray(
        wkv0.reshape(8, 128, 128).transpose(1, 0, 2).reshape(128, 8 * 128))
    wot = np.ascontiguousarray(
        wo.T.reshape(8, 128, D).transpose(1, 0, 2).reshape(128, 8 * D))
    w1_all = np.concatenate([sh_w1[None], ew1], axis=0)
    w1l = np.ascontiguousarray(
        w1_all.reshape(9, 8, 128, 8, 512).transpose(0, 3, 2, 1, 4).reshape(9, 8, 128, 4096))
    w2_all = np.concatenate([sh_w2[None], ew2], axis=0)
    w2l = np.ascontiguousarray(
        w2_all.reshape(9, 16, 2, 128, 2, 512).transpose(0, 4, 1, 3, 2, 5).reshape(9, 2, 16, 128, 1024))
    b1l = np.ascontiguousarray(sh_b1.reshape(32, 128).T)
    eb1l = np.ascontiguousarray(
        eb1.reshape(NE, 32, 128).transpose(2, 0, 1).reshape(128, NE * 32))
    eb2p = np.zeros((64, D), f32)
    eb2p[0:NE] = eb2
    eb2p[NE] = sh_b2

    if "nc" not in _NC_CACHE:
        _NC_CACHE["nc"] = build_program(debug=False)
    nc = _NC_CACHE["nc"]

    in_maps = []
    metas = []
    for core in range(8):
        beta = core // 4
        j = core % 4
        pi, own = _pi_perm(j)
        consts = _host_constants(pi, own)
        xb = x[beta]
        x_fm = np.ascontiguousarray(
            xb.T.reshape(D, 16, QUAD)[:, pi, :].reshape(D, S))
        x_tm = np.ascontiguousarray(
            np.concatenate([xb[q * QUAD:(q + 1) * QUAD] for q in own], axis=0))
        im = dict(consts)
        im.update(x_fm=x_fm, x_tm=x_tm, wqt=wqt, wkv=wkv, wot=wot,
                  gate_w=gate_w, gate_b=gate_b, w1l=w1l, w2l=w2l,
                  b1l=b1l, eb1l=eb1l, eb2p=eb2p)
        in_maps.append(im)
        metas.append((beta, own))

    res = run_bass_kernel_spmd(nc, in_maps, core_ids=list(range(8)))
    out = np.empty((B, S, D), f32)
    for core in range(8):
        beta, own = metas[core]
        yc = res.results[core]["y"]
        for i, q in enumerate(own):
            out[beta, q * QUAD:(q + 1) * QUAD] = yc[i * QUAD:(i + 1) * QUAD]
    return out


# version 08 (final)
# speedup vs baseline: 1.8786x; 1.0276x over previous
import contextlib
import numpy as np
import ml_dtypes
import concourse.bass as bass
import concourse.mybir as mybir
from concourse.bass_utils import run_bass_kernel_spmd

F32 = mybir.dt.float32
BF16 = mybir.dt.bfloat16
AF = mybir.ActivationFunctionType
ALU = mybir.AluOpType
AX = mybir.AxisListType

B, S, D = 2, 8192, 1024
H, HD = 16, 64
NI = 4096
NE = 8
QUAD = 512
NQ = 4
TOK = NQ * QUAD
SL = 256
NSL = S // SL
EPS = 1e-5
TEXT_KT = [16, 32, 48, 64]
OWN_PSLOT = [3, 7, 11, 15]


class Sem:
    __slots__ = ("h", "n")

    def __init__(self, h):
        self.h = h
        self.n = 0


class Plan:
    ENGS = ("tensor", "vector", "scalar", "gpsimd", "sync")

    def __init__(self):
        self.ops = {e: [] for e in self.ENGS}

    def emit(self, eng, fn=None, waits=(), inc=None, inc_k=1):
        w = [(s.h, v) for (s, v) in waits if v > 0]
        tgt = None
        if inc is not None:
            inc.n += inc_k
            tgt = (inc.h, inc_k)

        def run(e, w=w, fn=fn, tgt=tgt):
            for h, v in w:
                e.wait_ge(h, v)
            if fn is not None:
                ins = fn(e)
                if tgt is not None:
                    ins.then_inc(tgt[0], tgt[1])
            elif tgt is not None:
                e.sem_inc(tgt[0], tgt[1])

        self.ops[eng].append(run)

    @staticmethod
    def wv(*sems):
        return [(s, s.n) for s in sems]

    @staticmethod
    def wb(sem, back):
        return [(sem, max(0, sem.n - back))]


def build_program(debug=False):
    nc = bass.Bass("TRN2", target_bir_lowering=False)
    P = Plan()
    wv, wb = P.wv, P.wb

    def din(name, shape, dt=F32):
        return nc.dram_tensor(name, list(shape), dt, kind="ExternalInput").ap()

    x_fm = din("x_fm", (D, S))
    x_tm = din("x_tm", (TOK, D))
    c2t_d = din("c2t", (128, S), BF16)
    s2t_d = din("s2t", (128, S), BF16)
    pswap_d = din("pswap", (128, 128), BF16)
    identb_d = din("identb", (128, 128), BF16)
    identf_d = din("identf", (128, 128), F32)
    ones64_d = din("ones64", (128, 64), BF16)
    maskA_d = din("maskA", (128, 64), BF16)
    maskB_d = din("maskB", (128, 64), BF16)
    tri_d = din("tri", (128, 4 * 512), BF16)
    glog_d = din("glog", (128, 49))
    wqt_d = din("wqt", (128, 8 * 1024))
    wkv_d = din("wkv", (128, 8 * 128))
    wot_d = din("wot", (128, 8 * 1024))
    gw_d = din("gate_w", (D, NE))
    gb_d = din("gate_b", (1, NE))
    w1l_d = din("w1l", (9, 8, 128, 4096))
    w2l_d = din("w2l", (9, 2, 16, 128, 1024))
    b1l_d = din("b1l", (128, 32))
    eb1l_d = din("eb1l", (128, NE * 32))
    eb2p_d = din("eb2p", (64, D))
    onesrow_d = din("onesrow", (1, 2048), BF16)

    y = nc.dram_tensor("y", [TOK, D], F32, kind="ExternalOutput").ap()
    dbg = {}
    if debug:
        for nm, shp in (("xn", [128, 8 * TOK]), ("krope", [128, S]), ("vdup", [128, S]),
                        ("qall", [128, 8 * TOK]), ("attn", [128, 8 * TOK]),
                        ("wroute", [8, TOK]), ("acc0", [128, 8 * 1024]),
                        ("router", [128, 16 * 48])):
            dbg[nm] = nc.dram_tensor("dbg_" + nm, shp, F32, kind="ExternalOutput").ap()

    stats_d = nc.dram_tensor("stats_d", [3, S], F32, kind="Internal").ap()
    sums_d = nc.dram_tensor("sums_d", [64, QUAD], F32, kind="Internal").ap()
    cb_d = nc.dram_tensor("cb_d", [1, 1], F32, kind="Internal").ap()
    wroute_dd = nc.dram_tensor("wroute_dd", [NE, TOK], F32, kind="Internal").ap()
    attno_d = nc.dram_tensor("attno_d", [8, 128, TOK], F32, kind="Internal").ap()

    ctx = contextlib.ExitStack()
    sems = {}

    def sem(name):
        if name not in sems:
            sems[name] = Sem(ctx.enter_context(nc.semaphore(name)))
        return sems[name]

    ST = lambda *a, **k: ctx.enter_context(nc.sbuf_tensor(*a, **k))

    xn_all = ST("xn_all", [128, 8 * TOK], BF16)
    cbias = ST("cbias", [128, 1], F32)
    wroute_bf = ST("wroute_bf", [64, TOK], BF16)
    identb_s = ST("identb_s", [128, 128], BF16)
    identf_s = ST("identf_s", [128, 128], F32)
    pswap_s = ST("pswap_s", [128, 128], BF16)
    ones64_s = ST("ones64_s", [128, 64], BF16)
    maskA_s = ST("maskA_s", [128, 64], BF16)
    maskB_s = ST("maskB_s", [128, 64], BF16)
    tri_s = ST("tri_s", [128, 4 * 512], BF16)
    glog_s = ST("glog_s", [128, 49], F32)
    gbias = ST("gbias", [128, 49], F32)
    gw_bf = ST("gw_bf", [128, 64], BF16)
    gbc_s = ST("gbc_s", [128, NE], F32)
    b1l_s = ST("b1l_s", [128, 32], F32)
    eb1l_s = ST("eb1l_s", [128, NE * 32], F32)
    eb2p_bf = ST("eb2p_bf", [64, D], BF16)
    call_s = ST("call_s", [1, 64], F32)

    attn_ctx = contextlib.ExitStack()
    AT = lambda *a, **k: attn_ctx.enter_context(nc.sbuf_tensor(*a, **k))
    krope = AT("krope", [128, S], BF16, side="right")
    vdup = AT("vdup", [128, S], BF16, side="right")
    qall = AT("qall", [128, 8 * TOK], BF16, side="right")
    dstg = ST("dstg", [128, 2048], F32) if debug else None

    s_const = sem("s_const")
    s_cast = sem("s_cast")
    for o, i in ((identb_s, identb_d), (identf_s, identf_d), (pswap_s, pswap_d),
                 (ones64_s, ones64_d), (maskA_s, maskA_d), (maskB_s, maskB_d),
                 (b1l_s, b1l_d), (eb1l_s, eb1l_d), (tri_s, tri_d), (glog_s, glog_d)):
        P.emit("sync", lambda e, o=o, i=i: e.dma_start(out=o[:], in_=i), inc=s_const, inc_k=16)
    P.emit("sync", lambda e: e.dma_start(out=gbc_s[:], in_=gb_d.broadcast_to([128, NE])),
           inc=s_const, inc_k=16)
    P.emit("gpsimd", lambda e: e.dma_start(
        out=gw_bf[:].rearrange("p (c e) -> p c e", c=8),
        in_=gw_d.rearrange("(c p) e -> p c e", p=128)), inc=s_cast, inc_k=16)
    P.emit("gpsimd", lambda e: e.dma_start(out=eb2p_bf[:], in_=eb2p_d), inc=s_cast, inc_k=16)

    def dump(name, src, width, waitsems):
        if not debug:
            return
        s_dbg = sem("s_dbg")
        v_dbg = sem("v_dbg")
        for off in range(0, width, 2048):
            P.emit("vector", lambda e, src=src, off=off: e.tensor_copy(
                dstg[:], src[:, off:off + 2048]),
                waits=wv(*waitsems) + wv(s_dbg), inc=v_dbg)
            P.emit("sync", lambda e, name=name, off=off: e.dma_start(
                out=dbg[name][:, off:off + 2048], in_=dstg[:]),
                waits=wv(v_dbg), inc=s_dbg, inc_k=16)

    s_xf = [sem(f"s_xf{i}") for i in range(4)]
    s_xb = [sem("s_xb0"), sem("s_xb1")]
    s_tab = [sem(f"s_tab{i}") for i in range(4)]
    s_statw = [sem("s_statw0"), sem("s_statw1")]
    s_bc = [sem(f"s_bc{i}") for i in range(4)]
    p_st = sem("p_st"); p_kv = sem("p_kv"); p_ksw = sem("p_ksw"); p_vt = sem("p_vt")
    a_xsq = sem("a_xsq"); v_var = sem("v_var"); a_sq = sem("a_sq"); v_stats = sem("v_stats")
    v_xn = sem("v_xn"); a_k = sem("a_k"); v_v = sem("v_v"); v_k = sem("v_k"); v_vd = sem("v_vd")
    v_mrow = sem("v_mrow")

    with contextlib.ExitStack() as p1:
        T1 = lambda *a, **k: p1.enter_context(nc.sbuf_tensor(*a, **k))
        PT1 = lambda *a, **k: p1.enter_context(nc.psum_tensor(*a, **k))
        wkv_bf = T1("wkv_bf", [128, 8 * 128], BF16)
        P.emit("gpsimd", lambda e: e.dma_start(out=wkv_bf[:], in_=wkv_d),
               inc=s_cast, inc_k=16)
        xf = [T1(f"xf{i}", [128, 8 * SL], F32) for i in range(4)]
        xb = [T1(f"xb{i}", [128, 8 * SL], BF16) for i in range(2)]
        xsq = T1("xsq", [128, 8 * SL], BF16)
        c2b = [T1(f"c2b{i}", [128, SL], BF16) for i in range(4)]
        s2b = [T1(f"s2b{i}", [128, SL], BF16) for i in range(4)]
        xn_tmp = [T1(f"xn_tmp{i}", [128, 8 * SL], BF16) for i in range(2)]
        m_row = [T1(f"m_row{i}", [1, SL], F32) for i in range(2)]
        msq_row = [T1(f"msq_row{i}", [1, SL], F32) for i in range(2)]
        var_row = [T1(f"var_row{i}", [1, SL], F32) for i in range(2)]
        sq_row = [T1(f"sq_row{i}", [1, SL], F32) for i in range(2)]
        r_row = [T1(f"r_row{i}", [1, SL], F32) for i in range(2)]
        nmr_row = [T1(f"nmr_row{i}", [1, SL], F32) for i in range(2)]
        mbc = [T1(f"mbc{i}", [128, SL], F32) for i in range(4)]
        rbc = [T1(f"rbc{i}", [128, SL], F32) for i in range(4)]
        kbf = T1("kbf", [64, SL], BF16)
        kswb = T1("kswb", [64, SL], BF16)
        kt1 = T1("kt1", [64, SL], BF16)
        kt2 = T1("kt2", [64, SL], BF16)
        vbf = T1("vbf", [64, SL], BF16)
        s1p = [PT1(f"s1p{i}", [64, SL], F32) for i in range(2)]
        s2p = [PT1(f"s2p{i}", [64, SL], F32) for i in range(2)]
        kvp = [PT1(f"kvp{i}", [128, SL], F32) for i in range(2)]
        kswp = PT1("kswp", [64, SL], F32)
        vtp = PT1("vtp", [128, 128], BF16)

        def emitAB(s):
            par = s % 2
            q4 = s % 4
            ts = s * SL
            P.emit("sync", lambda e, q4=q4, ts=ts: e.dma_start(
                out=xf[q4][:].rearrange("p (c t) -> p c t", c=8),
                in_=x_fm[:, ts:ts + SL].rearrange("(c p) t -> p c t", p=128)),
                waits=wb(v_xn, 1), inc=s_xf[q4], inc_k=16)
            P.emit("gpsimd", lambda e, p=par, ts=ts: e.dma_start(
                out=xb[p][:].rearrange("p (c t) -> p c t", c=8),
                in_=x_fm[:, ts:ts + SL].rearrange("(c p) t -> p c t", p=128)),
                waits=wb(p_st, 1), inc=s_xb[par], inc_k=16)
            P.emit("sync", lambda e, q4=q4, ts=ts: e.dma_start(out=c2b[q4][:], in_=c2t_d[:, ts:ts + SL]),
                   waits=wb(v_k, 1), inc=s_tab[q4], inc_k=16)
            P.emit("sync", lambda e, q4=q4, ts=ts: e.dma_start(out=s2b[q4][:], in_=s2t_d[:, ts:ts + SL]),
                   inc=s_tab[q4], inc_k=16)

            P.emit("scalar", lambda e, p=par: e.activation(xsq[:], xb[p][:], AF.Square),
                   waits=wv(s_xb[par]) + wv(p_st), inc=a_xsq)
            for c in range(8):
                P.emit("tensor", lambda e, p=par, c=c: e.matmul(
                    s1p[p][:], ones64_s[:], xb[p][:, c * SL:(c + 1) * SL],
                    start=(c == 0), stop=(c == 7)),
                    waits=(wv(s_xb[par], s_const) + wb(v_mrow, 1) if c == 0 else []))
            for c in range(8):
                P.emit("tensor", lambda e, p=par, c=c: e.matmul(
                    s2p[p][:], ones64_s[:], xsq[:, c * SL:(c + 1) * SL],
                    start=(c == 0), stop=(c == 7)),
                    waits=(wv(a_xsq) + wb(v_var, 1) if c == 0 else []),
                    inc=(p_st if c == 7 else None))
            P.emit("vector", lambda e, p=par: e.tensor_scalar_mul(m_row[p][:], s1p[p][0:1, :], 1.0 / D),
                   waits=wv(p_st) + wv(s_statw[par]), inc=v_mrow)
            P.emit("vector", lambda e, p=par: e.tensor_mul(msq_row[p][:], m_row[p][:], m_row[p][:]))
            P.emit("vector", lambda e, p=par: e.scalar_tensor_tensor(
                var_row[p][:], s2p[p][0:1, :], 1.0 / D, msq_row[p][:], op0=ALU.mult, op1=ALU.subtract),
                )
            P.emit("vector", lambda e, p=par: e.tensor_scalar_add(var_row[p][:], var_row[p][:], EPS),
                inc=v_var)
            P.emit("scalar", lambda e, p=par: e.activation(sq_row[p][:], var_row[p][:], AF.Sqrt),
                   waits=wv(v_var), inc=a_sq)
            P.emit("vector", lambda e, p=par: e.reciprocal(r_row[p][:], sq_row[p][:]), waits=wv(a_sq))
            P.emit("vector", lambda e, p=par: e.scalar_tensor_tensor(
                nmr_row[p][:], m_row[p][:], -1.0, r_row[p][:], op0=ALU.mult, op1=ALU.mult), inc=v_stats)
            P.emit("sync", lambda e, p=par, ts=ts: e.dma_start(out=stats_d[0:1, ts:ts + SL], in_=m_row[p][:]),
                   waits=wv(v_stats), inc=s_statw[par], inc_k=16)
            P.emit("sync", lambda e, p=par, ts=ts: e.dma_start(out=stats_d[1:2, ts:ts + SL], in_=r_row[p][:]),
                   inc=s_statw[par], inc_k=16)
            P.emit("sync", lambda e, p=par, ts=ts: e.dma_start(out=stats_d[2:3, ts:ts + SL], in_=nmr_row[p][:]),
                   inc=s_statw[par], inc_k=16)
            P.emit("sync", lambda e, p=par, q4=q4, ts=ts: e.dma_start(
                out=mbc[q4][:], in_=stats_d[0:1, ts:ts + SL].broadcast_to([128, SL])),
                waits=wv(s_statw[par]) + wb(v_xn, 1), inc=s_bc[q4], inc_k=16)
            P.emit("sync", lambda e, p=par, q4=q4, ts=ts: e.dma_start(
                out=rbc[q4][:], in_=stats_d[1:2, ts:ts + SL].broadcast_to([128, SL])),
                inc=s_bc[q4], inc_k=16)

        def emitC(s):
            par = s % 2
            q4 = s % 4
            ts = s * SL
            pslot = ts // QUAD
            own = pslot in OWN_PSLOT
            dcol = (OWN_PSLOT.index(pslot) * QUAD + ts % QUAD) if own else 0
            xd = xn_all if own else xn_tmp[par]

            def xsl(c, xd=xd, dcol=dcol):
                if xd is xn_all:
                    return xd[:, c * TOK + dcol:c * TOK + dcol + SL]
                return xd[:, c * SL:(c + 1) * SL]

            for c in range(8):
                P.emit("vector", lambda e, q4=q4, c=c: e.tensor_sub(
                    xf[q4][:, c * SL:(c + 1) * SL], xf[q4][:, c * SL:(c + 1) * SL], mbc[q4][:]),
                    waits=(wv(s_bc[q4], s_xf[q4]) + wb(p_kv, 1) if c == 0 else []))
            for c in range(8):
                P.emit("vector", lambda e, q4=q4, c=c, xsl=xsl: e.tensor_mul(
                    xsl(c), xf[q4][:, c * SL:(c + 1) * SL], rbc[q4][:]),
                    inc=(v_xn if c == 7 else None))
            for c in range(8):
                P.emit("tensor", lambda e, p=par, c=c, xsl=xsl: e.matmul(
                    kvp[p][:], wkv_bf[:, c * 128:(c + 1) * 128], xsl(c),
                    start=(c == 0), stop=(c == 7)),
                    waits=(wv(v_xn, s_cast) + wb(a_k, 1) + wb(v_v, 1) if c == 0 else []),
                    inc=(p_kv if c == 7 else None))
            P.emit("scalar", lambda e, p=par: e.activation(kbf[:], kvp[p][0:64, :], AF.Identity),
                   waits=wv(p_kv, p_ksw, v_k), inc=a_k)
            P.emit("vector", lambda e, p=par: e.tensor_copy(vbf[:], kvp[p][64:128, :]),
                   waits=wv(p_kv, p_vt), inc=v_v)
            P.emit("tensor", lambda e: e.matmul(kswp[:], pswap_s[0:64, 0:64], kbf[:],
                                                start=True, stop=True),
                   waits=wv(a_k, v_k), inc=p_ksw)
            P.emit("vector", lambda e, q4=q4: e.tensor_mul(kt1[:], kbf[:], c2b[q4][0:64, :]),
                   waits=wv(s_tab[q4], p_ksw))
            P.emit("vector", lambda e: e.tensor_copy(kswb[:], kswp[0:64, :]))
            P.emit("vector", lambda e, q4=q4: e.tensor_mul(kt2[:], kswb[:], s2b[q4][0:64, :]))
            P.emit("vector", lambda e, ts=ts: e.tensor_add(krope[0:64, ts:ts + SL], kt1[:], kt2[:]))
            P.emit("vector", lambda e, ts=ts: e.tensor_copy(
                krope[64:128, ts:ts + SL], krope[0:64, ts:ts + SL]), inc=v_k)
            for k2 in range(2):
                P.emit("tensor", lambda e, k2=k2: e.transpose(
                    vtp[:, k2 * 64:(k2 + 1) * 64], vbf[:, k2 * 128:(k2 + 1) * 128], identb_s[0:64, 0:64]),
                    waits=(wv(v_v, s_const, v_vd) if k2 == 0 else []),
                    inc=(p_vt if k2 == 1 else None))
            for k2 in range(2):
                jt = ts // 128 + k2
                P.emit("vector", lambda e, k2=k2, jt=jt: e.tensor_copy(
                    vdup[:, jt * 128:jt * 128 + 64], vtp[:, k2 * 64:(k2 + 1) * 64]),
                    waits=(wv(p_vt) if k2 == 0 else []))
                P.emit("vector", lambda e, k2=k2, jt=jt: e.tensor_copy(
                    vdup[:, jt * 128 + 64:jt * 128 + 128], vtp[:, k2 * 64:(k2 + 1) * 64]),
                    inc=(v_vd if k2 == 1 else None))

        for it in range(NSL + 2):
            if it < NSL:
                emitAB(it)
            if it >= 2:
                emitC(it - 2)

    dump("xn", xn_all, 8 * TOK, (v_xn,))
    dump("krope", krope, S, (v_k,))
    dump("vdup", vdup, S, (v_vd,))

    p_q = sem("p_q"); a_q = sem("a_q"); p_qsw = sem("p_qsw"); v_q = sem("v_q")
    p_dg = sem("p_dg"); v_dp = sem("v_dp"); v_call = sem("v_call"); v_cb = sem("v_cb")
    s_cb = sem("s_cb"); s_qtab = sem("s_qtab")

    with contextlib.ExitStack() as p2:
        wqt_bf = p2.enter_context(nc.sbuf_tensor("wqt_bf", [128, 8 * 1024], BF16, side="right"))
        T2 = lambda *a, **k: p2.enter_context(nc.sbuf_tensor(*a, **k))
        qbf = T2("qbf", [128, 512], BF16)
        qswb = T2("qswb", [128, 512], BF16)
        qt1 = T2("qt1", [128, 512], BF16)
        qt2 = T2("qt2", [128, 512], BF16)
        dp = T2("dp", [128, 512], BF16)
        c2q = T2("c2q", [128, 512], BF16)
        s2q = T2("s2q", [128, 512], BF16)
        cbrow = T2("cbrow", [1, 1], F32)
        qp = [p2.enter_context(nc.psum_tensor(f"qp{i}", [128, 512], F32)) for i in range(2)]
        qswp = p2.enter_context(nc.psum_tensor("qswp", [128, 512], F32))
        dgA = p2.enter_context(nc.psum_tensor("dgA", [64, 512], F32))
        dgB = p2.enter_context(nc.psum_tensor("dgB", [64, 512], F32))

        s_wqt = sem("s_wqt")
        P.emit("gpsimd", lambda e: e.dma_start(out=wqt_bf[:], in_=wqt_d),
               inc=s_wqt, inc_k=16)

        for g in range(NQ):
            lq = g * QUAD
            ts = OWN_PSLOT[g] * QUAD
            P.emit("sync", lambda e, ts=ts: e.dma_start(out=c2q[:], in_=c2t_d[:, ts:ts + QUAD]),
                   waits=wv(v_q), inc=s_qtab, inc_k=16)
            P.emit("sync", lambda e, ts=ts: e.dma_start(out=s2q[:], in_=s2t_d[:, ts:ts + QUAD]),
                   inc=s_qtab, inc_k=16)
            for c in range(8):
                pp = c % 2
                for dc in range(8):
                    P.emit("tensor", lambda e, pp=pp, c=c, dc=dc, lq=lq: e.matmul(
                        qp[pp][:], wqt_bf[:, dc * 1024 + c * 128:dc * 1024 + (c + 1) * 128],
                        xn_all[:, dc * TOK + lq:dc * TOK + lq + QUAD],
                        start=(dc == 0), stop=(dc == 7)),
                        waits=(wv(s_wqt, v_xn) + wb(a_q, 1)
                               + (wv(v_mrow, v_var, v_k, v_vd, a_k, v_v)
                                  if (g == 0 and c == 0) else [])
                               if dc == 0 else []),
                        inc=(p_q if dc == 7 else None))
                P.emit("scalar", lambda e, pp=pp: e.activation(qbf[:], qp[pp][:], AF.Identity),
                       waits=wv(p_q, p_qsw, v_q), inc=a_q)
                P.emit("tensor", lambda e: e.matmul(qswp[:], pswap_s[:], qbf[:],
                                                    start=True, stop=True),
                       waits=wv(a_q, v_q), inc=p_qsw)
                P.emit("vector", lambda e: e.tensor_copy(qswb[:], qswp[:]), waits=wv(p_qsw))
                P.emit("vector", lambda e: e.tensor_mul(qt1[:], qbf[:], c2q[:]),
                       waits=wv(s_qtab))
                P.emit("vector", lambda e: e.tensor_mul(qt2[:], qswb[:], s2q[:]))
                P.emit("vector", lambda e, c=c, lq=lq: e.tensor_add(
                    qall[:, c * TOK + lq:c * TOK + lq + QUAD], qt1[:], qt2[:]), inc=v_q)
            for c in range(8):
                P.emit("vector", lambda e, c=c, lq=lq, ts=ts: e.tensor_mul(
                    dp[:], qall[:, c * TOK + lq:c * TOK + lq + QUAD], krope[:, ts:ts + QUAD]),
                    waits=wv(v_q, v_k) + wv(p_dg), inc=v_dp)
                P.emit("tensor", lambda e: e.matmul(dgA[:], maskA_s[:], dp[:], start=True, stop=True),
                       waits=wv(v_dp, v_call))
                P.emit("tensor", lambda e: e.matmul(dgB[:], maskB_s[:], dp[:], start=True, stop=True),
                       inc=p_dg)
                idx = g * 16 + 2 * c
                P.emit("vector", lambda e, idx=idx: e.reduce_max(
                    call_s[0:1, idx:idx + 1], dgA[0:1, :], axis=AX.X), waits=wv(p_dg), inc=v_call)
                P.emit("vector", lambda e, idx=idx: e.reduce_max(
                    call_s[0:1, idx + 1:idx + 2], dgB[0:1, :], axis=AX.X), inc=v_call)
        P.emit("vector", lambda e: e.reduce_max(cbrow[:], call_s[:], axis=AX.X),
               waits=wv(v_call), inc=v_cb)
        P.emit("vector", lambda e: e.tensor_scalar_mul(cbrow[:], cbrow[:], -0.125),
               waits=wv(v_cb), inc=v_cb)
        P.emit("sync", lambda e: e.dma_start(out=cb_d, in_=cbrow[:]),
               waits=wv(v_cb), inc=s_cb, inc_k=16)
        P.emit("sync", lambda e: e.dma_start(out=cbias[:], in_=cb_d.broadcast_to([128, 1])),
               waits=wv(s_cb), inc=s_cb, inc_k=16)

    a_gb = sem("a_gb")
    P.emit("scalar", lambda e: e.activation(gbias[:], glog_s[:], AF.Identity,
                                            bias=cbias[:, 0:1]),
           waits=wv(s_cb, s_const), inc=a_gb)

    dump("qall", qall, 8 * TOK, (v_q,))

    p_qk = sem("p_qk"); a_exp = sem("a_exp"); v_msk = sem("v_msk"); p_av = sem("p_av")
    v_oev = sem("v_oev"); v_rec = sem("v_rec"); s_sumw = sem("s_sumw"); s_rbc = sem("s_rbc")
    v_div = sem("v_div")

    with contextlib.ExitStack() as p3:
        T3 = lambda *a, **k: p3.enter_context(nc.sbuf_tensor(*a, **k))
        Ebuf = [T3(f"Ebuf{i}", [128, 1024], BF16) for i in range(3)]
        ostg = [T3(f"ostg{i}", [128, 512], F32) for i in range(2)]
        ssb = T3("ssb", [128, 512], F32)
        recA = T3("recA", [1, 512], F32)
        recB = T3("recB", [1, 512], F32)
        rbc2 = [T3(f"rbc2{i}", [128, 512], F32) for i in range(2)]
        scp = [p3.enter_context(nc.psum_tensor(f"scp{i}", [128, 1024], F32)) for i in range(3)]
        op_ = p3.enter_context(nc.psum_tensor("op_", [128, 512], F32))
        sump = p3.enter_context(nc.psum_tensor("sump", [128, 512], F32))

        steps = [(slot, hp, j) for slot in range(NQ) for hp in range(8)
                 for j in range(TEXT_KT[slot])]
        NST = len(steps)
        dep_tgt = {}

        def emit_epilogue(slot, hp):
            ow = hp % 2
            P.emit("vector", lambda e: e.tensor_copy(ssb[:], sump[:]),
                   waits=wv(p_av, s_sumw))
            P.emit("vector", lambda e, ow=ow: e.tensor_copy(ostg[ow][:], op_[:]),
                   waits=wb(v_div, 1), inc=v_oev)
            P.emit("vector", lambda e: e.reciprocal(recA[:], ssb[0:1, :]))
            P.emit("vector", lambda e: e.reciprocal(recB[:], ssb[64:65, :]), inc=v_rec)
            r0 = slot * 16 + hp * 2
            P.emit("sync", lambda e, r0=r0: e.dma_start(out=sums_d[r0:r0 + 1, :], in_=recA[:]),
                   waits=wv(v_rec), inc=s_sumw, inc_k=16)
            P.emit("sync", lambda e, r0=r0: e.dma_start(out=sums_d[r0 + 1:r0 + 2, :], in_=recB[:]),
                   inc=s_sumw, inc_k=16)
            P.emit("sync", lambda e, r0=r0: e.dma_start(
                out=rbc2[0][:], in_=sums_d[r0:r0 + 1, :].broadcast_to([128, 512])),
                waits=wv(s_sumw) + wv(v_div), inc=s_rbc, inc_k=16)
            P.emit("sync", lambda e, r0=r0: e.dma_start(
                out=rbc2[1][:], in_=sums_d[r0 + 1:r0 + 2, :].broadcast_to([128, 512])),
                inc=s_rbc, inc_k=16)
            lq = slot * QUAD
            P.emit("vector", lambda e, ow=ow, hp=hp, lq=lq: e.tensor_mul(
                qall[0:64, hp * TOK + lq:hp * TOK + lq + QUAD], ostg[ow][0:64, :],
                rbc2[0][0:64, :]), waits=wv(s_rbc))
            P.emit("vector", lambda e, ow=ow, hp=hp, lq=lq: e.tensor_mul(
                qall[64:128, hp * TOK + lq:hp * TOK + lq + QUAD], ostg[ow][64:128, :],
                rbc2[1][64:128, :]), inc=v_div)

        for it in range(NST + 2):
            if it < NST:
                slot, hp, j = steps[it]
                TKT = TEXT_KT[slot]
                W0 = TKT - 16
                lq = slot * QUAD
                buf = it % 3
                w = j - W0
                bcol = slot * 12 + w if (0 <= w < 12) else 48
                qk_waits = wb(a_exp, 2)
                if it == 0:
                    qk_waits = qk_waits + wv(s_cb, v_q, v_k, v_vd, v_call, a_q)
                P.emit("tensor", lambda e, j=j, hp=hp, lq=lq, buf=buf: e.matmul(
                    scp[buf][:, 0:512], krope[0:64, j * 128:(j + 1) * 128],
                    qall[0:64, hp * TOK + lq:hp * TOK + lq + QUAD],
                    start=True, stop=True, tile_position=(0, 0)), waits=qk_waits)
                P.emit("tensor", lambda e, j=j, hp=hp, lq=lq, buf=buf: e.matmul(
                    scp[buf][:, 512:1024], krope[64:128, j * 128:(j + 1) * 128],
                    qall[64:128, hp * TOK + lq:hp * TOK + lq + QUAD],
                    start=True, stop=True, tile_position=(64, 0)), inc=p_qk)
                exp_waits = wv(p_qk) + wv(p_av)
                if it == 0:
                    exp_waits = exp_waits + wv(a_gb)
                P.emit("scalar", lambda e, buf=buf, bcol=bcol: e.activation(
                    Ebuf[buf][:], scp[buf][:], AF.Exp, bias=gbias[:, bcol:bcol + 1],
                    scale=0.125), waits=exp_waits, inc=a_exp)
                if w >= 12:
                    tsl = tri_s[:, (w - 12) * 512:(w - 11) * 512]
                    P.emit("vector", lambda e, buf=buf, tsl=tsl: e.tensor_mul(
                        Ebuf[buf][:, 0:512], Ebuf[buf][:, 0:512], tsl),
                        waits=wv(a_exp) + (wv(s_const) if v_msk.n == 0 else []))
                    P.emit("vector", lambda e, buf=buf, tsl=tsl: e.tensor_mul(
                        Ebuf[buf][:, 512:1024], Ebuf[buf][:, 512:1024], tsl),
                        inc=v_msk)
                    dep_tgt[it] = (v_msk, v_msk.n)
                else:
                    dep_tgt[it] = (a_exp, a_exp.n)
            a = it - 2
            if a >= 0:
                slot, hp, j = steps[a]
                TKT = TEXT_KT[slot]
                buf = a % 3
                first = (j == 0)
                last = (j == TKT - 1)
                dsem, dval = dep_tgt.pop(a)
                dep = [(dsem, dval)]
                P.emit("tensor", lambda e, j=j, buf=buf, first=first, last=last: e.matmul(
                    op_[0:64, :], vdup[:, j * 128:j * 128 + 64], Ebuf[buf][:, 0:512],
                    start=first, stop=last, tile_position=(0, 0),
                    skip_group_check=True),
                    waits=dep + (wv(v_oev) if first else []))
                P.emit("tensor", lambda e, j=j, buf=buf, first=first, last=last: e.matmul(
                    op_[64:128, :], vdup[:, j * 128 + 64:j * 128 + 128], Ebuf[buf][:, 512:1024],
                    start=first, stop=last, tile_position=(0, 64),
                    skip_group_check=True))
                P.emit("tensor", lambda e, buf=buf, first=first, last=last: e.matmul(
                    sump[0:64, :], ones64_s[:], Ebuf[buf][:, 0:512],
                    start=first, stop=last, tile_position=(0, 0),
                    skip_group_check=True))
                P.emit("tensor", lambda e, buf=buf, first=first, last=last: e.matmul(
                    sump[64:128, :], ones64_s[:], Ebuf[buf][:, 512:1024],
                    start=first, stop=last, tile_position=(0, 64),
                    skip_group_check=True), inc=p_av)
                if last:
                    emit_epilogue(slot, hp)

    dump("attn", qall, 8 * TOK, (v_div,))

    p_wo = sem("p_wo"); v_wo = sem("v_wo")
    s_wo = [sem("s_wo0"), sem("s_wo1")]
    s_wot = sem("s_wot")
    with contextlib.ExitStack() as pwo:
        wot_bf = pwo.enter_context(nc.sbuf_tensor("wot_bf", [128, 8 * 1024], BF16, side="right"))
        wstg = [pwo.enter_context(nc.sbuf_tensor(f"wstg{i}", [128, 512], F32)) for i in range(2)]
        wop = [pwo.enter_context(nc.psum_tensor(f"wop{i}", [128, 512], F32)) for i in range(2)]
        P.emit("gpsimd", lambda e: e.dma_start(out=wot_bf[:], in_=wot_d),
               waits=wv(p_q), inc=s_wot, inc_k=16)
        it = 0
        for oc in range(8):
            for tsl in range(4):
                pp = it % 2
                lq = tsl * 512
                for ac in range(8):
                    P.emit("tensor", lambda e, pp=pp, oc=oc, ac=ac, lq=lq: e.matmul(
                        wop[pp][:], wot_bf[:, ac * 1024 + oc * 128:ac * 1024 + (oc + 1) * 128],
                        qall[:, ac * TOK + lq:ac * TOK + lq + 512],
                        start=(ac == 0), stop=(ac == 7)),
                        waits=(wv(v_div, s_wot) + wb(v_wo, 1) if ac == 0 else []),
                        inc=(p_wo if ac == 7 else None))
                P.emit("vector", lambda e, pp=pp: e.tensor_copy(wstg[pp][:], wop[pp][:]),
                       waits=wv(p_wo, s_wo[pp]), inc=v_wo)
                P.emit("sync", lambda e, pp=pp, oc=oc, lq=lq: e.dma_start(
                    out=attno_d[oc, :, lq:lq + 512], in_=wstg[pp][:]),
                    waits=wv(v_wo), inc=s_wo[pp], inc_k=16)
                it += 1

    attn_ctx.close()

    p_lg = sem("p_lg"); v_rt = sem("v_rt"); a_re = sem("a_re"); p_wt = sem("p_wt")
    v_wr = sem("v_wr"); s_wr = sem("s_wr"); g_v = sem("g_v"); v_init = sem("v_init")
    tp_v = sem("tp_v"); tp_s = sem("tp_s")
    with contextlib.ExitStack() as p4:
        T4 = lambda *a, **k: p4.enter_context(nc.sbuf_tensor(*a, **k))
        lgs = T4("lgs", [128, NE], F32)
        nmax = T4("nmax", [128, 1], F32)
        el = T4("el", [128, NE], F32)
        se = T4("se", [128, 1], F32)
        rec = T4("rec", [128, 1], F32)
        probs_all = T4("probs_all", [128, 128], F32)
        t4a = T4("t4a", [128, 64], F32)
        t2a = T4("t2a", [128, 32], F32)
        m1b = T4("m1b", [128, 128], F32)
        m2b = T4("m2b", [128, 128], F32)
        mk1a = T4("mk1a", [128, 128], F32)
        p2va = T4("p2va", [128, 128], F32)
        mk2a = T4("mk2a", [128, 128], F32)
        wsela = T4("wsela", [128, 128], F32)
        s12a = T4("s12a", [128, 128], F32)
        rdena = T4("rdena", [128, 128], F32)
        wta = T4("wta", [128, 128], F32)
        lgp = p4.enter_context(nc.psum_tensor("lgp", [128, NE], F32))
        wtp = p4.enter_context(nc.psum_tensor("wtp", [32, 128], F32))

        P.emit("vector", lambda e: e.memset(wroute_bf[:], 0.0), inc=v_init)
        P.emit("gpsimd", lambda e: e.dma_start(out=wroute_bf[8:9, :], in_=onesrow_d),
               waits=wv(v_init), inc=sem("s_ones"), inc_k=16)

        for tt in range(16):
            lq = tt * 128
            for dc in range(8):
                P.emit("tensor", lambda e, dc=dc, lq=lq: e.matmul(
                    lgp[:], xn_all[:, dc * TOK + lq:dc * TOK + lq + 128],
                    gw_bf[:, dc * 8:(dc + 1) * 8],
                    start=(dc == 0), stop=(dc == 7)),
                    waits=(wv(s_cast, v_rt) + (wv(v_wo, v_xn) if tt == 0 else [])
                           if dc == 0 else []),
                    inc=(p_lg if dc == 7 else None))
            P.emit("vector", lambda e: e.tensor_add(lgs[:], lgp[:], gbc_s[:]),
                   waits=wv(p_lg, s_const) + (wv(sem("s_dbgr")) if (debug and tt > 0) else []))
            P.emit("vector", lambda e: e.reduce_max(nmax[:], lgs[:], axis=AX.X, negate=True),
                   inc=g_v)
            P.emit("scalar", lambda e: e.activation(el[:], lgs[:], AF.Exp, bias=nmax[:, 0:1],
                                                    accum_out=se[:, 0:1]),
                   waits=wv(g_v), inc=a_re)
            P.emit("vector", lambda e: e.reciprocal(rec[:], se[:]), waits=wv(a_re), inc=g_v)
            P.emit("vector", lambda e, tt=tt: e.tensor_single_scalar(
                probs_all[:, tt * NE:(tt + 1) * NE], el[:], rec[:, 0:1],
                op=ALU.mult), waits=wv(g_v), inc=v_rt)
            if debug:
                s_dbgr = sem("s_dbgr")
                for slot_i, srcb in ((0, lgs),):
                    P.emit("sync", lambda e, tt=tt, slot_i=slot_i, srcb=srcb: e.dma_start(
                        out=dbg["router"][:, tt * 16 + slot_i * 8:tt * 16 + slot_i * 8 + 8],
                        in_=srcb[:, 0:8]),
                        waits=wv(v_rt), inc=s_dbgr, inc_k=16)

        rr8 = lambda t: t[:].rearrange("p (a e) -> p a e", e=8)
        rr4 = lambda t: t[:].rearrange("p (a e) -> p a e", e=4)
        rr2 = lambda t: t[:].rearrange("p (a e) -> p a e", e=2)
        step = {"i": 0}

        def chain(fn):
            i = step["i"]; step["i"] += 1
            P.emit("vector", fn,
                   waits=wv(tp_s) + (wv(v_rt) if i == 0 else []), inc=tp_v)
            P.emit("scalar", None, waits=wv(tp_v), inc=tp_s)

        def top_chain(src, dstb):
            chain(lambda e, src=src: e.tensor_tensor(
                rr4(t4a), rr8(src)[:, :, 0:4], rr8(src)[:, :, 4:8], op=ALU.max))
            chain(lambda e: e.tensor_tensor(
                rr2(t2a), rr4(t4a)[:, :, 0:2], rr4(t4a)[:, :, 2:4], op=ALU.max))
            chain(lambda e, dstb=dstb: e.tensor_tensor(
                rr8(dstb)[:, :, 0:1], rr2(t2a)[:, :, 0:1], rr2(t2a)[:, :, 1:2], op=ALU.max))
            chain(lambda e, dstb=dstb: e.tensor_copy(rr8(dstb)[:, :, 1:2], rr8(dstb)[:, :, 0:1]))
            chain(lambda e, dstb=dstb: e.tensor_copy(rr8(dstb)[:, :, 2:4], rr8(dstb)[:, :, 0:2]))
            chain(lambda e, dstb=dstb: e.tensor_copy(rr8(dstb)[:, :, 4:8], rr8(dstb)[:, :, 0:4]))

        top_chain(probs_all, m1b)
        chain(lambda e: e.tensor_tensor(mk1a[:], probs_all[:], m1b[:], op=ALU.is_ge))
        chain(lambda e: e.scalar_tensor_tensor(
            p2va[:], mk1a[:], -1e30, probs_all[:], op0=ALU.mult, op1=ALU.add))
        top_chain(p2va, m2b)
        chain(lambda e: e.tensor_tensor(mk2a[:], probs_all[:], m2b[:], op=ALU.is_ge))
        chain(lambda e: e.tensor_tensor(wsela[:], probs_all[:], mk2a[:], op=ALU.mult))
        chain(lambda e: e.tensor_add(s12a[:], m1b[:], m2b[:]))
        chain(lambda e: e.tensor_scalar_add(s12a[:], s12a[:], 1e-9))
        chain(lambda e: e.reciprocal(rdena[:], s12a[:]))
        chain(lambda e: e.tensor_tensor(wta[:], wsela[:], rdena[:], op=ALU.mult))

        for tt in range(16):
            lq = tt * 128
            P.emit("tensor", lambda e, tt=tt: e.transpose(
                wtp[0:8, :], wta[:, tt * 8:(tt + 1) * 8], identf_s[:]),
                waits=(wv(tp_v, tp_s, s_const) if tt == 0 else []) + wv(v_wr),
                inc=p_wt)
            P.emit("vector", lambda e, lq=lq: e.tensor_copy(wroute_bf[0:8, lq:lq + 128],
                                                            wtp[0:8, :]),
                   waits=wv(p_wt, v_init) + (wv(sems["s_ones"]) if lq == 0 else []),
                   inc=v_wr)
        P.emit("gpsimd", lambda e: e.dma_start(out=wroute_dd, in_=wroute_bf[0:8, :]),
               waits=wv(v_wr), inc=s_wr, inc_k=16)
        if debug:
            for off, srcb in ((256, probs_all), (384, wta), (512, m1b), (640, m2b)):
                P.emit("sync", lambda e, off=off, srcb=srcb: e.dma_start(
                    out=dbg["router"][:, off:off + 128], in_=srcb[:]),
                    waits=wv(tp_v, tp_s), inc=sem("s_dbgr"), inc_k=16)
            P.emit("gpsimd", lambda e: e.dma_start(out=dbg["wroute"], in_=wroute_bf[0:8, :]),
                   waits=wv(v_wr), inc=sem("s_dbgw"), inc_k=16)

    s_acc = sem("s_acc")
    s_w1 = [sem("s_w1_0"), sem("s_w1_1")]
    s_w2 = [sem("s_w2_0"), sem("s_w2_1")]
    s_bcf = [sem("s_bcf0"), sem("s_bcf1")]
    p_e1 = sem("p_e1"); a_gel = sem("a_gel"); p_e2 = sem("p_e2"); p_w2c = sem("p_w2c")
    v_acc = sem("v_acc"); v_drain = sem("v_drain"); p_tr = sem("p_tr")
    s_xt = [sem("s_xt0"), sem("s_xt1")]
    s_stt = [sem("s_stt0"), sem("s_stt1")]
    a_xn2 = sem("a_xn2"); v_out = sem("v_out")
    s_out = [sem("s_out0"), sem("s_out1")]

    with contextlib.ExitStack() as p5:
        T5 = lambda *a, **k: p5.enter_context(nc.sbuf_tensor(*a, **k))
        Hbuf = T5("Hbuf", [128, 32 * 1024], BF16)
        acc = T5("acc", [128, 8 * 1024], F32)
        tscr = T5("tscr", [128, 1024], F32)
        out_sb = [T5(f"out_sb{i}", [128, 1024], F32) for i in range(2)]
        xt_b = [T5(f"xt_b{i}", [128, 1024], F32) for i in range(2)]
        xnt = T5("xnt", [128, 1024], F32)
        rst = [T5(f"rst{i}", [128, 1], F32) for i in range(2)]
        nmrst = [T5(f"nmrst{i}", [128, 1], F32) for i in range(2)]
        w1b = [p5.enter_context(nc.sbuf_tensor(f"w1b{i}", [128, 8 * 512], BF16, side="right"))
               for i in range(2)]
        w2b = [p5.enter_context(nc.sbuf_tensor(f"w2b{i}", [128, 1024], BF16, side="right"))
               for i in range(2)]
        bcf = p5.enter_context(nc.sbuf_tensor("bcf", [128, 1024], F32, side="right"))
        PS = [p5.enter_context(nc.psum_tensor(f"PS{i}", [128, 1024], F32)) for i in range(4)]

        for g2 in range(2):
            lq0 = g2 * 1024
            for oc in range(8):
                P.emit("sync", lambda e, oc=oc, lq0=lq0: e.dma_start(
                    out=acc[:, oc * 1024:(oc + 1) * 1024], in_=attno_d[oc, :, lq0:lq0 + 1024]),
                    waits=(wv(s_wo[0], s_wo[1]) + wv(p_tr) if oc == 0 else []), inc=s_acc, inc_k=16)
            for mi in range(9):
                is_sh = (mi == 0)
                eidx = mi - 1
                for ig in range(8):
                    w1w = wb(p_e1, 4)
                    if g2 == 0 and mi == 0 and ig == 0:
                        w1w = w1w + wv(v_div)
                    P.emit("gpsimd", lambda e, ig=ig, mi=mi: e.dma_start(
                        out=w1b[ig % 2][:], in_=w1l_d[mi, ig]),
                        waits=w1w, inc=s_w1[ig % 2], inc_k=16)
                    for c2 in range(4):
                        ic = ig * 4 + c2
                        pp = ic % 2
                        for dc in range(8):
                            for th in range(2):
                                e1w = []
                                if dc == 0 and th == 0:
                                    e1w = wv(s_w1[ig % 2]) + wb(a_gel, 1)
                                    if ic == 0:
                                        e1w += wb(v_drain, 3) + wv(v_out, p_e2, v_wr)
                                    elif ic == 1:
                                        e1w += wb(v_drain, 2)
                                P.emit("tensor", lambda e, ig=ig, c2=c2, pp=pp, dc=dc, th=th, lq0=lq0: e.matmul(
                                    PS[pp][:, th * 512:(th + 1) * 512],
                                    w1b[ig % 2][:, dc * 512 + c2 * 128:dc * 512 + (c2 + 1) * 128],
                                    xn_all[:, dc * TOK + lq0 + th * 512:dc * TOK + lq0 + (th + 1) * 512],
                                    start=(dc == 0), stop=(dc == 7)),
                                    waits=e1w,
                                    inc=(p_e1 if (dc == 7 and th == 1) else None))
                        bsl = (b1l_s[:, ic:ic + 1] if is_sh
                               else eb1l_s[:, eidx * 32 + ic:eidx * 32 + ic + 1])
                        gw_ = wv(p_e1, s_const)
                        if ic == 0:
                            gw_ = gw_ + wv(p_w2c)
                        if g2 == 0 and mi == 0 and ic == 0:
                            gw_ = gw_ + wv(s_wr)
                            if debug:
                                gw_ = gw_ + wv(sem("s_dbgr"))
                        P.emit("scalar", lambda e, ic=ic, pp=pp, b=bsl: e.activation(
                            Hbuf[:, ic * 1024:(ic + 1) * 1024], PS[pp][:], AF.Gelu_apprx_tanh,
                            bias=b), waits=gw_, inc=a_gel)
                if not is_sh:
                    P.emit("sync", lambda e, eidx=eidx, lq0=lq0: e.dma_start(
                        out=bcf[:],
                        in_=wroute_dd[eidx:eidx + 1, lq0:lq0 + 1024].broadcast_to([128, 1024])),
                        waits=wv(s_wr, v_acc), inc=s_bcf[0], inc_k=16)
                for dh in range(2):
                    for blk in range(16):
                        w2w = wb(p_w2c, 2)
                        if g2 == 0 and mi == 0 and dh == 0 and blk == 0:
                            w2w = w2w + wv(v_div)
                        P.emit("gpsimd", lambda e, blk=blk, dh=dh, mi=mi: e.dma_start(
                            out=w2b[blk % 2][:], in_=w2l_d[mi, dh, blk]),
                            waits=w2w, inc=s_w2[blk % 2], inc_k=16)
                        for kk in range(2):
                            ic = blk * 2 + kk
                            for dc2 in range(4):
                                for th in range(2):
                                    e2w = []
                                    if ic == 0 and th == 0:
                                        if dc2 == 0:
                                            e2w = wv(s_w2[blk % 2])
                                            e2w += (wv(a_gel, v_acc) if dh == 0
                                                    else wb(v_drain, 3))
                                        elif dh == 1:
                                            e2w = wb(v_drain, 3 - dc2)
                                    elif kk == 0 and dc2 == 0 and th == 0:
                                        e2w = wv(s_w2[blk % 2])
                                    P.emit("tensor", lambda e, ic=ic, kk=kk, blk=blk, dc2=dc2, th=th: e.matmul(
                                        PS[dc2][:, th * 512:(th + 1) * 512],
                                        w2b[blk % 2][:, kk * 512 + dc2 * 128:kk * 512 + (dc2 + 1) * 128],
                                        Hbuf[:, ic * 1024 + th * 512:ic * 1024 + (th + 1) * 512],
                                        start=(ic == 0), stop=(ic == 31)),
                                        waits=e2w,
                                        inc=(p_w2c if (dc2 == 3 and th == 1) else None))
                    for dc2 in range(4):
                        dc = dh * 4 + dc2
                        if is_sh:
                            P.emit("vector", lambda e, dc2=dc2, dc=dc: e.tensor_add(
                                acc[:, dc * 1024:(dc + 1) * 1024],
                                acc[:, dc * 1024:(dc + 1) * 1024], PS[dc2][:]),
                                waits=(wv(p_w2c, s_acc) if dc2 == 0 else []),
                                inc=v_drain)
                            P.emit("vector", None, inc=v_acc)
                        else:
                            P.emit("vector", lambda e, dc2=dc2: e.tensor_mul(
                                tscr[:], PS[dc2][:], bcf[:]),
                                waits=(wv(p_w2c, s_bcf[0]) if dc2 == 0 else []),
                                inc=v_drain)
                            P.emit("vector", lambda e, dc2=dc2, dc=dc: e.tensor_add(
                                acc[:, dc * 1024:(dc + 1) * 1024],
                                acc[:, dc * 1024:(dc + 1) * 1024], tscr[:]),
                                inc=v_acc)
            for half in range(2):
                for dcc in range(4):
                    dc = half * 4 + dcc
                    for th in range(2):
                        bw = []
                        if dcc == 0 and th == 0:
                            bw = wv(s_cast, v_wr) + wb(v_drain, 3)
                        elif th == 0:
                            bw = wb(v_drain, 3 - dcc)
                        P.emit("tensor", lambda e, dc=dc, dcc=dcc, th=th, lq0=lq0: e.matmul(
                            PS[dcc][:, th * 512:(th + 1) * 512],
                            eb2p_bf[:, dc * 128:(dc + 1) * 128],
                            wroute_bf[:, lq0 + th * 512:lq0 + (th + 1) * 512],
                            start=True, stop=True),
                            waits=bw, inc=(p_e2 if (dcc == 3 and th == 1) else None))
                for dcc in range(4):
                    dc = half * 4 + dcc
                    P.emit("vector", lambda e, dcc=dcc, dc=dc: e.tensor_add(
                        acc[:, dc * 1024:(dc + 1) * 1024],
                        acc[:, dc * 1024:(dc + 1) * 1024], PS[dcc][:]),
                        waits=(wv(p_e2) if dcc == 0 else []), inc=v_drain)
                    P.emit("vector", None, inc=v_acc)
            if debug and g2 == 0:
                P.emit("sync", lambda e: e.dma_start(out=dbg["acc0"], in_=acc[:]),
                       waits=wv(v_acc), inc=sem("s_dbg"), inc_k=16)
            for tt in range(8):
                pp = tt % 2
                l0 = lq0 + tt * 128
                gl = OWN_PSLOT[l0 // QUAD] * QUAD + (l0 % QUAD)
                P.emit("sync", lambda e, pp=pp, l0=l0: e.dma_start(
                    out=xt_b[pp][:], in_=x_tm[l0:l0 + 128, :]),
                    waits=wb(v_out, 1), inc=s_xt[pp], inc_k=16)
                P.emit("sync", lambda e, pp=pp, gl=gl: e.dma_start(
                    out=rst[pp][:], in_=stats_d[1:2, gl:gl + 128].rearrange("o t -> t o")),
                    inc=s_stt[pp], inc_k=16)
                P.emit("sync", lambda e, pp=pp, gl=gl: e.dma_start(
                    out=nmrst[pp][:], in_=stats_d[2:3, gl:gl + 128].rearrange("o t -> t o")),
                    inc=s_stt[pp], inc_k=16)
                for oc in range(8):
                    P.emit("tensor", lambda e, oc=oc, tt=tt: e.transpose(
                        PS[tt % 2][:, oc * 128:(oc + 1) * 128],
                        acc[:, oc * 1024 + tt * 128:oc * 1024 + (tt + 1) * 128], identf_s[:]),
                        waits=(wv(v_acc) + wb(v_out, 1) if oc == 0 else []),
                        inc=(p_tr if oc == 7 else None))
                P.emit("scalar", lambda e, pp=pp: e.activation(
                    xnt[:], xt_b[pp][:], AF.Identity, bias=nmrst[pp][:, 0:1],
                    scale=rst[pp][:, 0:1]),
                    waits=wv(s_xt[pp], s_stt[pp], v_out), inc=a_xn2)
                P.emit("vector", lambda e, pp=pp, tt=tt: e.tensor_add(
                    out_sb[pp][:], PS[tt % 2][:], xt_b[pp][:]),
                       waits=wv(p_tr, s_xt[pp], s_out[pp]))
                P.emit("vector", lambda e, pp=pp: e.tensor_add(out_sb[pp][:], out_sb[pp][:], xnt[:]),
                       waits=wv(a_xn2), inc=v_out)
                P.emit("gpsimd", lambda e, pp=pp, l0=l0: e.dma_start(
                    out=y[l0:l0 + 128, :], in_=out_sb[pp][:]),
                    waits=wv(v_out), inc=s_out[pp], inc_k=16)

    fin = wv(s_out[0], s_out[1])
    if debug:
        fin = fin + wv(sem("s_dbg")) + wv(sem("s_dbgw")) + wv(sem("s_dbgr"))
    P.emit("gpsimd", None, waits=fin)

    with nc.Block() as block:
        for eng in Plan.ENGS:
            ops = P.ops[eng]

            def body(e, ops=ops):
                for fn in ops:
                    fn(e)
            getattr(block, eng)(body)
    ctx.close()
    return nc



def _pi_perm(j):
    own = [j, 7 - j, 8 + j, 15 - j]
    pi = []
    for i, o in enumerate(own):
        blk = [q for q in range(4 * i, 4 * i + 4) if q != o]
        pi.extend(blk + [o])
    return pi, own


def _host_constants(pi, own):
    bf = ml_dtypes.bfloat16
    pos = np.concatenate([np.arange(q * QUAD, (q + 1) * QUAD) for q in pi]).astype(np.float64)
    freqs = 1.0 / (1000.0 ** (np.arange(0, HD, 2, dtype=np.float64) / HD))
    pair = ((np.arange(128) % 64) // 2)
    ang = pos[None, :] * freqs[pair][:, None]
    c2 = np.cos(ang).astype(np.float32)
    s2 = np.sin(ang).astype(np.float32)
    sign = np.where(np.arange(128) % 2 == 0, -1.0, 1.0).astype(np.float32)
    s2 = s2 * sign[:, None]

    psw = np.zeros((128, 128), np.float32)
    for a in range(128):
        psw[a, a ^ 1] = 1.0
    identb = np.eye(128, dtype=np.float32)
    ones64 = np.ones((128, 64), np.float32)
    maskA = np.zeros((128, 64), np.float32); maskA[0:64] = 1.0
    maskB = np.zeros((128, 64), np.float32); maskB[64:128] = 1.0

    glog = np.zeros((128, 49), np.float32)
    for slot in range(4):
        for w in range(12):
            kq = pi[4 * slot + w // 4]
            glog[:, slot * 12 + w] = 0.0 if kq < own[slot] else -1e5
    tri = np.zeros((128, 4 * 512), np.float32)
    r = np.arange(128)
    q = np.arange(512)
    for t in range(4):
        m = ((t * 128 + r)[:, None] <= q[None, :]).astype(np.float32)
        tri[:, t * 512:(t + 1) * 512] = m
    return dict(c2t=c2.astype(bf), s2t=s2.astype(bf), pswap=psw.astype(bf),
                identb=identb.astype(bf), identf=identb,
                ones64=ones64.astype(bf), maskA=maskA.astype(bf),
                maskB=maskB.astype(bf), tri=tri.astype(bf), glog=glog,
                onesrow=np.ones((1, 2048), np.float32).astype(bf))


_NC_CACHE = {}


def kernel(L0_attn_x, L0_attn_wq, L0_attn_wk, L0_attn_wv, L0_attn_wo,
           L0_ffn_gate_w, L0_ffn_gate_b, L0_ffn_sh_w1, L0_ffn_sh_b1,
           L0_ffn_sh_w2, L0_ffn_sh_b2, L0_ffn_ew1, L0_ffn_eb1,
           L0_ffn_ew2, L0_ffn_eb2):
    f32 = np.float32
    x = np.asarray(L0_attn_x, f32)
    wq = np.asarray(L0_attn_wq, f32)
    wk = np.asarray(L0_attn_wk, f32)
    wv = np.asarray(L0_attn_wv, f32)
    wo = np.asarray(L0_attn_wo, f32)
    gate_w = np.asarray(L0_ffn_gate_w, f32)
    gate_b = np.asarray(L0_ffn_gate_b, f32).reshape(1, NE)
    sh_w1 = np.asarray(L0_ffn_sh_w1, f32)
    sh_b1 = np.asarray(L0_ffn_sh_b1, f32)
    sh_w2 = np.asarray(L0_ffn_sh_w2, f32)
    sh_b2 = np.asarray(L0_ffn_sh_b2, f32)
    ew1 = np.ascontiguousarray(np.asarray(L0_ffn_ew1, f32))
    eb1 = np.asarray(L0_ffn_eb1, f32)
    ew2 = np.ascontiguousarray(np.asarray(L0_ffn_ew2, f32))
    eb2 = np.asarray(L0_ffn_eb2, f32)

    wqt = np.ascontiguousarray(
        wq.T.reshape(8, 128, D).transpose(1, 0, 2).reshape(128, 8 * D))
    wkv0 = np.concatenate([wk, wv], axis=1)
    wkv = np.ascontiguousarray(
        wkv0.reshape(8, 128, 128).transpose(1, 0, 2).reshape(128, 8 * 128))
    wot = np.ascontiguousarray(
        wo.T.reshape(8, 128, D).transpose(1, 0, 2).reshape(128, 8 * D))
    w1_all = np.concatenate([sh_w1[None], ew1], axis=0)
    w1l = np.ascontiguousarray(
        w1_all.reshape(9, 8, 128, 8, 512).transpose(0, 3, 2, 1, 4).reshape(9, 8, 128, 4096))
    w2_all = np.concatenate([sh_w2[None], ew2], axis=0)
    w2l = np.ascontiguousarray(
        w2_all.reshape(9, 16, 2, 128, 2, 512).transpose(0, 4, 1, 3, 2, 5).reshape(9, 2, 16, 128, 1024))
    b1l = np.ascontiguousarray(sh_b1.reshape(32, 128).T)
    eb1l = np.ascontiguousarray(
        eb1.reshape(NE, 32, 128).transpose(2, 0, 1).reshape(128, NE * 32))
    eb2p = np.zeros((64, D), f32)
    eb2p[0:NE] = eb2
    eb2p[NE] = sh_b2

    if "nc" not in _NC_CACHE:
        _NC_CACHE["nc"] = build_program(debug=False)
    nc = _NC_CACHE["nc"]

    in_maps = []
    metas = []
    for core in range(8):
        beta = core // 4
        j = core % 4
        pi, own = _pi_perm(j)
        consts = _host_constants(pi, own)
        xb = x[beta]
        x_fm = np.ascontiguousarray(
            xb.T.reshape(D, 16, QUAD)[:, pi, :].reshape(D, S))
        x_tm = np.ascontiguousarray(
            np.concatenate([xb[q * QUAD:(q + 1) * QUAD] for q in own], axis=0))
        im = dict(consts)
        im.update(x_fm=x_fm, x_tm=x_tm, wqt=wqt, wkv=wkv, wot=wot,
                  gate_w=gate_w, gate_b=gate_b, w1l=w1l, w2l=w2l,
                  b1l=b1l, eb1l=eb1l, eb2p=eb2p)
        in_maps.append(im)
        metas.append((beta, own))

    res = run_bass_kernel_spmd(nc, in_maps, core_ids=list(range(8)))
    out = np.empty((B, S, D), f32)
    for core in range(8):
        beta, own = metas[core]
        yc = res.results[core]["y"]
        for i, q in enumerate(own):
            out[beta, q * QUAD:(q + 1) * QUAD] = yc[i * QUAD:(i + 1) * QUAD]
    return out
